# version 24 (final)
# baseline (speedup 1.0000x reference)
import numpy as np
import ml_dtypes
from contextlib import ExitStack

import concourse.bass as bass
import concourse.mybir as mybir
from concourse.bass_utils import run_bass_kernel_spmd

F32 = mybir.dt.float32
BF16 = mybir.dt.bfloat16
AF = mybir.ActivationFunctionType
ALU = mybir.AluOpType
AX = mybir.AxisListType
bfdt = ml_dtypes.bfloat16

NCORES = 8
NB = 2
T = 4096
D = 1024
NT = T // 128
NDC = D // 128
HID = 2730
H2 = 5460
HEADS = 8
DH = 64
INNER = 512
WIN = 512
NEG = -1.0e30
NPAIR = 22
PAIR_SZ = [128] * 21 + [42]
CH_OFF = np.cumsum([0] + [2 * s for s in PAIR_SZ]).tolist()
STR_W = T + 1

ENGS = ("tensor", "vector", "scalar", "gpsimd", "sync")


def _rotary_tables():
    half = np.arange(0, DH, 2, dtype=np.float64)
    inv_freq = 1.0 / 10000.0 ** (half / DH)
    t = np.arange(T, dtype=np.float64)
    freqs = t[:, None] * inv_freq[None, :]
    freqs = np.concatenate([freqs, freqs], axis=-1)
    base = (half + 0.4 * DH) / (1.4 * DH)
    power = (t - T // 2) / float(WIN // 2)
    scale = base[None, :] ** power[:, None]
    scale = np.concatenate([scale, scale], axis=-1)
    cos, sin = np.cos(freqs), np.sin(freqs)
    sgn = np.concatenate([-np.ones(DH // 2), np.ones(DH // 2)])
    csq = (8.0 * cos * scale).astype(np.float32)
    ssq = (8.0 * sin * scale * sgn).astype(np.float32)
    csk = (cos / scale).astype(np.float32)
    ssk = (sin / scale * sgn).astype(np.float32)
    return csq, ssq, csk, ssk


def host_prep(inputs):
    c = np.ascontiguousarray
    d = {}
    d["WqT"] = c(np.asarray(inputs["L0_attn_Wq"]).T)
    d["WkT"] = c(np.asarray(inputs["L0_attn_Wk"]).T)
    d["Wv"] = c(np.asarray(inputs["L0_attn_Wv"]))
    d["WoT"] = c(np.asarray(inputs["L0_attn_Wo"]).T)
    d["wqkv"] = c(np.asarray(inputs["L0_ffn_wqkv"]))
    d["wo"] = c(np.asarray(inputs["L0_ffn_wo"]))
    w1 = np.asarray(inputs["L0_ffn_w1"])
    b1 = np.asarray(inputs["L0_ffn_b1"])
    perm = []
    for ci in range(NPAIR):
        sz = PAIR_SZ[ci]
        perm.extend(range(128 * ci, 128 * ci + sz))
        perm.extend(range(HID + 128 * ci, HID + 128 * ci + sz))
    perm = np.asarray(perm)
    d["w1p"] = c(w1[:, :, perm])
    b1p = b1[:, perm]
    b1_pk = np.zeros((2, 128, 2 * NPAIR), np.float32)
    for ci in range(NPAIR):
        sz = PAIR_SZ[ci]
        off = CH_OFF[ci]
        b1_pk[:, :sz, 2 * ci] = b1p[:, off : off + sz]
        b1_pk[:, :sz, 2 * ci + 1] = b1p[:, off + sz : off + 2 * sz]
    d["b1_pk"] = b1_pk
    d["w2"] = c(np.asarray(inputs["L0_ffn_w2"]))
    d["lng_pk"] = c(np.asarray(inputs["L0_ffn_ln_g"]).reshape(2, 8, 128).transpose(0, 2, 1))
    d["lnb_pk"] = c(np.asarray(inputs["L0_ffn_ln_b"]).reshape(2, 8, 128).transpose(0, 2, 1))
    d["b2_pk"] = c(np.asarray(inputs["L0_ffn_b2"]).reshape(2, 8, 128).transpose(0, 2, 1))
    qs_t = np.tile(np.asarray(inputs["L0_ffn_q_scale"], np.float32), (1, HEADS))
    ks_t = np.tile(np.asarray(inputs["L0_ffn_k_scale"], np.float32), (1, HEADS))
    d["qscale_bc"] = c(np.broadcast_to(qs_t[:, None, :], (2, 128, INNER)))
    d["kscale_bc"] = c(np.broadcast_to(ks_t[:, None, :], (2, 128, INNER)))
    csq, ssq, csk, ssk = _rotary_tables()
    d["csq"], d["ssq"], d["csk"], d["ssk"] = csq, ssq, csk, ssk
    d["ident"] = np.eye(128, dtype=np.float32)
    d["ones_row"] = np.ones((1, 128), np.float32)
    d["ones_col"] = np.ones((128, 1), np.float32)
    ii, jj = np.indices((128, 128))
    d["b_diag"] = np.where(jj <= ii, 0.0, NEG).astype(np.float32)
    d["b_old"] = np.where(jj >= ii, 0.0, NEG).astype(np.float32)
    d["epses"] = np.tile(np.asarray([[1e-5, 1e-24]], np.float32), (128, 1))
    d["ones_nd"] = np.full((128, 128), -1.0 / D, np.float32)
    d["ones_pd"] = np.full((128, 128), 1.0 / D, np.float32)
    return d


class KB:

    def __init__(self, dev=False):
        self.dev = dev
        self.nc = bass.Bass("TRN2", target_bir_lowering=False, debug=False,
                            detect_race_conditions=False)
        self.stack = ExitStack()
        self.sems = {}
        self.all_sems = {}
        self.bar_n = 0
        self.sbuf_cols = 0

    def sem(self, name):
        if name not in self.sems:
            self.sems[name] = self.stack.enter_context(self.nc.semaphore(name))
            self.all_sems[name] = self.sems[name]
        return self.sems[name]

    def barrier(self, block, flush=None):
        h = self.sem("bar")
        self.bar_n += 5
        tgt = self.bar_n
        flush = flush or {}
        for name in ENGS:
            def body(e, name=name):
                for sm, val in flush.get(name, ()):
                    e.wait_ge(sm, val)
                e.nop().then_inc(h, 1)
                e.wait_ge(h, tgt)
            getattr(block, name)(body)

    def recycle(self, block, prefixes, force=False):
        if not force and len(self.sems) < 72:
            return
        keep = ("bar", "cld")
        names = [n for n in self.sems
                 if n not in keep and not n.startswith("tap")]
        if not names:
            return
        handles = [self.sems.pop(n) for n in names]

        def g(g):
            self.nc.clear_and_free_semaphores(handles)
        block.gpsimd(g)
        self.barrier(block)

    def sb(self, st, name, shape, dtype):
        return st.enter_context(self.nc.sbuf_tensor(name, shape, dtype))

    def ps(self, st, name, shape, dtype=F32):
        return st.enter_context(self.nc.psum_tensor(name, shape, dtype))



def phase_W(kb, block, din, dr):
    nc = kb.nc
    with ExitStack() as st:
        wv_sb = kb.sb(st, "w_wv", [128, NDC * D], BF16)
        wot_sb = kb.sb(st, "w_wot", [128, NDC * D], BF16)
        mvo_st = kb.sb(st, "w_mvo", [128, 2 * D], BF16)
        psA = kb.ps(st, "w_psA", [128, 512], F32)
        psB = kb.ps(st, "w_psB", [128, 512], F32)
        s_ld = kb.sem("w_ld")
        s_mm = kb.sem("w_mm")
        s_ev = kb.sem("w_ev")
        s_st = kb.sem("w_st")

        def g(g):
            for l in range(2):
                g.dma_start(out=dr["w1p_bf"][l], in_=din["w1p"][l]).then_inc(kb.sem("w_cast"), 16)
                g.dma_start(out=dr["w2_bf"][l], in_=din["w2"][l]).then_inc(kb.sem("w_cast"), 16)
            g.dma_start(
                out=wv_sb[:].rearrange("p (k m) -> p k m", k=NDC),
                in_=din["Wv"].rearrange("(k p) m -> p k m", p=128),
            ).then_inc(s_ld, 16)
            g.dma_start(
                out=wot_sb[:].rearrange("p (k m) -> p k m", k=NDC),
                in_=din["WoT"].rearrange("(k p) m -> p k m", p=128),
            ).then_inc(s_ld, 16)
        block.gpsimd(g)

        def t(tn):
            tn.wait_ge(s_ld, 32)
            n = 0
            for mc in range(NDC):
                for nr in range(2):
                    n += 1
                    if n > 2:
                        tn.wait_ge(s_ev, n - 2)
                    pt = psA if nr == 0 else psB
                    for kc in range(NDC):
                        mm = tn.matmul(
                            pt[:],
                            wv_sb[:, kc * D + mc * 128 : kc * D + (mc + 1) * 128],
                            wot_sb[:, kc * D + nr * 512 : kc * D + nr * 512 + 512],
                            start=(kc == 0), stop=(kc == NDC - 1),
                        )
                    mm.then_inc(s_mm, 1)
        block.tensor(t)

        def v(v):
            n = 0
            for mc in range(NDC):
                if mc >= 2:
                    v.wait_ge(s_st, 16 * (mc - 1))
                for nr in range(2):
                    n += 1
                    v.wait_ge(s_mm, n)
                    v.tensor_copy(
                        mvo_st[:, (mc % 2) * D + nr * 512 : (mc % 2) * D + nr * 512 + 512],
                        (psA if nr == 0 else psB)[:],
                    ).then_inc(s_ev, 1)
        block.vector(v)

        def sy(sy):
            for mc in range(NDC):
                sy.wait_ge(s_ev, 2 * (mc + 1))
                sy.dma_start(
                    out=dr["mvod"][mc * 128 : (mc + 1) * 128, :],
                    in_=mvo_st[:, (mc % 2) * D : (mc % 2) * D + D],
                ).then_inc(s_st, 16)
            sy.wait_ge(s_st, 16 * NDC)
        block.sync(sy)
        return {
            "gpsimd": [(kb.sem("w_cast"), 64), (s_ld, 32)],
            "tensor": [(s_mm, 16)],
            "vector": [(s_ev, 16)],
            "sync": [(s_st, 16 * NDC)],
        }


def emit_ln_tiles(kb, block, st, tag, b, din, dst_writer, want_bf=True):
    nc = kb.nc
    xt = kb.sb(st, tag + "_xt", [128, 2 * D], F32)
    sq = kb.sb(st, tag + "_sq", [128, D], F32)
    xn = kb.sb(st, tag + "_xn", [128, 2 * D], F32)
    rows = kb.sb(st, tag + "_rows", [128, 8], F32)
    idf = kb.sb(st, tag + "_idf", [128, 128], F32)
    psT = [kb.ps(st, tag + f"_psT{i}", [128, 512], F32) for i in range(4)]

    s_x = kb.sem(tag + "_x")
    s_sq = kb.sem(tag + "_sq")
    s_v = kb.sem(tag + "_v")
    s_r = kb.sem(tag + "_r")
    s_xn = kb.sem(tag + "_xn")
    s_tr = kb.sem(tag + "_tr")
    s_ev = kb.sem(tag + "_ev")
    s_idf = kb.sem(tag + "_id")

    def sy(sy):
        sy.dma_start(out=idf[:], in_=din["ident"]).then_inc(s_idf, 16)
        for tc in range(NT):
            if tc >= 2:
                sy.wait_ge(s_xn, tc - 1)
            sy.dma_start(
                out=xt[:, (tc % 2) * D : (tc % 2) * D + D],
                in_=din["xsh"][b, tc * 128 : (tc + 1) * 128, :],
            ).then_inc(s_x, 16)
    block.sync(sy)

    def s(s):
        for tc in range(NT + 1):
            if tc < NT:
                s.wait_ge(s_x, 16 * (tc + 1))
                if tc >= 1:
                    s.wait_ge(s_v, 5 * tc)
                    s.wait_ge(s_xn, tc)
                s.activation(sq[:], xt[:, (tc % 2) * D : (tc % 2) * D + D],
                             AF.Square).then_inc(s_sq, 1)
                s.wait_ge(s_v, 5 * tc + 5)
                s.activation(rows[:, 5:6], rows[:, 4:5], AF.Ln, bias=kb.epses[:, 0:1]
                             ).then_inc(s_r, 1)
                s.wait_ge(s_r, 2 * tc + 1)
                s.activation(rows[:, 6:7], rows[:, 5:6], AF.Exp, scale=-0.5
                             ).then_inc(s_r, 1)
            if tc >= 1:
                for half in range(2):
                    s.wait_ge(s_tr, 2 * (tc - 1) + half + 1)
                    ins = dst_writer(s, tc - 1, half, psT[(2 * (tc - 1) + half) % 4])
                    ins.then_inc(s_ev, 1)
    block.scalar(s)

    def v(v):
        for tc in range(NT):
            xtv = xt[:, (tc % 2) * D : (tc % 2) * D + D]
            v.wait_ge(s_x, 16 * (tc + 1))
            if tc >= 1:
                v.wait_ge(s_xn, tc)
            v.reduce_sum(rows[:, 2:3], xtv, axis=AX.X, negate=True).then_inc(s_v, 1)
            v.wait_ge(s_v, 5 * tc + 1)
            v.tensor_scalar(rows[:, 2:3], rows[:, 2:3], 1.0 / D, None, ALU.mult
                            ).then_inc(s_v, 1)
            v.wait_ge(s_sq, tc + 1)
            v.reduce_sum(rows[:, 1:2], sq[:], axis=AX.X).then_inc(s_v, 1)
            v.wait_ge(s_v, 5 * tc + 3)
            v.tensor_tensor(rows[:, 3:4], rows[:, 2:3], rows[:, 2:3], ALU.mult
                            ).then_inc(s_v, 1)
            v.wait_ge(s_v, 5 * tc + 4)
            if tc >= 1:
                v.wait_ge(s_r, 2 * tc - 1)
            v.scalar_tensor_tensor(
                rows[:, 4:5], rows[:, 1:2], 1.0 / D, rows[:, 3:4],
                ALU.mult, ALU.subtract,
            ).then_inc(s_v, 1)
            v.wait_ge(s_r, 2 * (tc + 1))
            if tc >= 2:
                v.wait_ge(s_tr, 2 * (tc - 1))
            v.tensor_scalar(
                xn[:, (tc % 2) * D : (tc % 2) * D + D], xtv,
                rows[:, 2:3], rows[:, 6:7], ALU.add, ALU.mult,
            ).then_inc(s_xn, 1)
    block.vector(v)

    def t(tn):
        tn.wait_ge(s_idf, 16)
        for tc in range(NT):
            tn.wait_ge(s_xn, tc + 1)
            for half in range(2):
                if 2 * tc + half >= 4:
                    tn.wait_ge(s_ev, 2 * tc + half - 3)
                pt = psT[(2 * tc + half) % 4]
                for j in range(4):
                    dc = 4 * half + j
                    tr = tn.transpose(
                        pt[:, j * 128 : (j + 1) * 128],
                        xn[:, (tc % 2) * D + dc * 128 : (tc % 2) * D + (dc + 1) * 128],
                        idf[:],
                    )
                tr.then_inc(s_tr, 1)
    block.tensor(t)

    return {
        "sync": [(s_x, 16 * NT), (s_idf, 16)],
        "scalar": [(s_sq, NT), (s_r, 2 * NT), (s_ev, 2 * NT)],
        "vector": [(s_v, 5 * NT), (s_xn, NT)],
        "tensor": [(s_tr, 2 * NT)],
    }


_IDF = {}


def din_identf_ap(kb):
    return _IDF[id(kb)]


def phase_P0(kb, block, din, dr, xnT_bf, b):
    with ExitStack() as st:
        def writer(s, tc, half, pt):
            dst = xnT_bf[:].rearrange("p (c t) -> p c t", c=NDC)[
                :, 4 * half : 4 * half + 4, tc * 128 : (tc + 1) * 128
            ]
            return s.copy(dst, pt[:].rearrange("p (c t) -> p c t", c=4))
        flush = emit_ln_tiles(kb, block, st, f"p0{b}", b, din, writer)
    return flush


def phase_A1(kb, block, din, dr, xnT_bf, b):
    nc = kb.nc
    with ExitStack() as st:
        w_sb = kb.sb(st, f"a1w{b}", [128, 2 * NDC * D], BF16)
        mvo_sb = kb.sb(st, f"a1m{b}", [128, NDC * D], BF16)
        stg = kb.sb(st, f"a1s{b}", [128, 4 * 512], BF16)
        psA = kb.ps(st, f"a1pA{b}", [128, 512], F32)
        psB = kb.ps(st, f"a1pB{b}", [128, 512], F32)
        s_w = kb.sem(f"a1w{b}")
        s_mm = kb.sem(f"a1mm{b}")
        s_ev = kb.sem(f"a1ev{b}")
        s_st = kb.sem(f"a1st{b}")

        def g(g):
            for i, nm in enumerate(("WkT", "WqT")):
                g.dma_start(
                    out=w_sb[:, i * NDC * D : (i + 1) * NDC * D].rearrange(
                        "p (k m) -> p k m", k=NDC),
                    in_=din[nm].rearrange("(k p) m -> p k m", p=128),
                ).then_inc(s_w, 16)
            g.dma_start(
                out=mvo_sb[:].rearrange("p (k m) -> p k m", k=NDC),
                in_=dr["mvod"].rearrange("(k p) m -> p k m", p=128),
            ).then_inc(s_w, 16)
        block.gpsimd(g)

        jobs = []
        for i in range(2):
            for mc in range(NDC):
                for tr in range(8):
                    jobs.append(("kq", i, mc, tr))
        for tc in range(NT):
            for nh in range(2):
                jobs.append(("v", 0, tc, nh))

        def t(tn):
            tn.wait_ge(s_w, 48)
            for n, (kind, i, a_, b_) in enumerate(jobs, 1):
                if n > 2:
                    tn.wait_ge(s_ev, n - 2)
                pt = psA if n % 2 else psB
                if kind == "kq":
                    mc, tr = a_, b_
                    for kc in range(NDC):
                        mm = tn.matmul(
                            pt[:],
                            w_sb[:, (i * NDC + kc) * D + mc * 128 : (i * NDC + kc) * D + (mc + 1) * 128],
                            xnT_bf[:, kc * T + tr * 512 : kc * T + tr * 512 + 512],
                            start=(kc == 0), stop=(kc == NDC - 1),
                        )
                else:
                    tc, nh = a_, b_
                    for kc in range(NDC):
                        mm = tn.matmul(
                            pt[:],
                            xnT_bf[:, kc * T + tc * 128 : kc * T + (tc + 1) * 128],
                            mvo_sb[:, kc * D + nh * 512 : kc * D + nh * 512 + 512],
                            start=(kc == 0), stop=(kc == NDC - 1),
                        )
                mm.then_inc(s_mm, 1)
        block.tensor(t)

        def s(s):
            for n, (kind, i, a_, b_) in enumerate(jobs, 1):
                s.wait_ge(s_mm, n)
                if n > 4:
                    s.wait_ge(s_st, 16 * (n - 4))
                pt = psA if n % 2 else psB
                dst = stg[:, ((n - 1) % 4) * 512 : ((n - 1) % 4) * 512 + 512]
                if kind == "kq" and i == 1:
                    ins = s.activation(dst, pt[:], AF.Copy, scale=1.0 / 32.0)
                else:
                    ins = s.copy(dst, pt[:])
                ins.then_inc(s_ev, 1)
        block.scalar(s)

        def sy(sy):
            for n, (kind, i, a_, b_) in enumerate(jobs, 1):
                sy.wait_ge(s_ev, n)
                src = stg[:, ((n - 1) % 4) * 512 : ((n - 1) % 4) * 512 + 512]
                if kind == "kq":
                    mc, tr = a_, b_
                    dstd = (dr["KTd"] if i == 0 else dr["QTd"])[b, mc, :, tr * 512 : tr * 512 + 512]
                else:
                    tc, nh = a_, b_
                    dstd = dr["Vpd"][b, tc * 128 : (tc + 1) * 128, nh * 512 : nh * 512 + 512]
                sy.dma_start(out=dstd, in_=src).then_inc(s_st, 16)
            sy.wait_ge(s_st, 16 * len(jobs))
        block.sync(sy)
        return {
            "gpsimd": [(s_w, 48)],
            "tensor": [(s_mm, len(jobs))],
            "scalar": [(s_ev, len(jobs))],
            "sync": [(s_st, 16 * len(jobs))],
        }



def build(dev=False, phases=("W", "P0", "A1", "A2", "FI", "F0", "F1", "FIN")):
    kb = KB(dev=dev)
    nc = kb.nc

    din = {}
    for nm, shp in [
        ("xsh", [NB, T, D]), ("WqT", [D, D]), ("WkT", [D, D]), ("Wv", [D, D]),
        ("WoT", [D, D]), ("wqkv", [2, D, 3 * INNER]), ("wo", [2, INNER, D]),
        ("w1p", [2, D, H2]), ("b1_pk", [2, 128, 2 * NPAIR]), ("w2", [2, HID, D]),
        ("lng_pk", [2, 128, 8]), ("lnb_pk", [2, 128, 8]), ("b2_pk", [2, 128, 8]),
        ("qscale_bc", [2, 128, INNER]), ("kscale_bc", [2, 128, INNER]),
        ("csq", [T, DH]), ("ssq", [T, DH]), ("csk", [T, DH]), ("ssk", [T, DH]),
        ("ident", [128, 128]), ("ones_row", [1, 128]), ("ones_col", [128, 1]),
        ("b_diag", [128, 128]), ("b_old", [128, 128]), ("epses", [128, 2]),
        ("ones_nd", [128, 128]), ("ones_pd", [128, 128]),
    ]:
        din[nm] = nc.dram_tensor(nm, shp, F32, kind="ExternalInput").ap()

    out = nc.dram_tensor("out", [NB, T, D], F32, kind="ExternalOutput").ap()

    def scratch(name, shape, dtype):
        kind = "ExternalOutput" if dev else "Internal"
        return nc.dram_tensor(name, shape, dtype, kind=kind).ap()

    dr = {
        "w1p_bf": scratch("w1p_bf", [2, D, H2], BF16),
        "w2_bf": scratch("w2_bf", [2, HID, D], BF16),
        "mvod": scratch("mvod", [D, D], BF16),
        "QTd": scratch("QTd", [NB, NDC, 128, T], BF16),
        "KTd": scratch("KTd", [NB, NDC, 128, T], BF16),
        "Vpd": scratch("Vpd", [NB, T, D], BF16),
        "pbuf": scratch("pbuf", [NB, T, D], F32),
        "tapS": scratch("tapS", [128, T], F32),
        "tapP": scratch("tapP", [128, T], BF16),
        "tapC": scratch("tapC", [128, 4], F32),
        "out": out,
    }

    with kb.stack:
        block = kb.stack.enter_context(nc.Block())
        epses = kb.sb(kb.stack, "epses_s", [128, 2], F32)
        kb.epses = epses
        s_c = kb.sem("cld")

        def g0(g):
            g.dma_start(out=epses[:], in_=din["epses"]).then_inc(s_c, 16)
            g.wait_ge(s_c, 16)
        block.gpsimd(g0)
        kb.barrier(block)

        if "W" in phases:
            fl = phase_W(kb, block, din, dr)
            kb.barrier(block, fl)
            kb.recycle(block, ["w_"])

        for b in range(NB):
            with ExitStack() as stb:
                need_xnT = ("P0" in phases) or ("A1" in phases)
                if need_xnT:
                    xnT_bf = kb.sb(stb, f"xnT{b}", [128, NDC * T], BF16)
                if "P0" in phases:
                    fl = phase_P0(kb, block, din, dr, xnT_bf, b)
                    kb.barrier(block, fl)
                    kb.recycle(block, [f"p0{b}"])
                if "A1" in phases:
                    fl = phase_A1(kb, block, din, dr, xnT_bf, b)
                    kb.barrier(block, fl)
                    kb.recycle(block, ["a1"])
            if "A2" in phases:
                fl = phase_A2(kb, block, din, dr, b)
                kb.barrier(block, fl)
                kb.recycle(block, ["a2"])
            if any(p in phases for p in ("FI", "F0", "F1", "FIN", "F0a", "F1a")):
                with ExitStack() as stf:
                    stream = kb.sb(stf, f"stream{b}", [128, NDC * STR_W], F32)

                    def tap(name, b=b, stream=stream):
                        if not dev or b != 0:
                            return
                        tsr = nc.dram_tensor(f"tap_{name}", [128, NDC * STR_W], F32,
                                             kind="ExternalOutput").ap()
                        s_t = kb.sem(f"tap{name}")

                        def gt(g):
                            g.dma_start(out=tsr, in_=stream[:]).then_inc(s_t, 16)
                        block.gpsimd(gt)
                        kb.barrier(block, {"gpsimd": [(s_t, 16)]})

                    if "FI" in phases:
                        fl = phase_FI(kb, block, din, dr, stream, b)
                        kb.barrier(block, fl)
                        kb.recycle(block, [f"fi{b}"])
                        tap("fi")
                    for l in range(2):
                        if f"F{l}" in phases or f"F{l}a" in phases:
                            fl = phase_Fa(kb, block, din, dr, stream, b, l)
                            kb.barrier(block, fl)
                            kb.recycle(block, [f"fa{b}{l}"])
                            tap(f"f{l}a")
                        if f"F{l}" in phases:
                            fl = phase_Fb(kb, block, din, dr, stream, b, l)
                            kb.barrier(block, fl)
                            kb.recycle(block, [f"fb{b}{l}"])
                            tap(f"f{l}b")
                    if "FIN" in phases:
                        fl = phase_FIN(kb, block, din, dr, stream, b)
                        kb.barrier(block, fl)
                        kb.recycle(block, [f"fin{b}", "tap"])


    nc.finalize()
    return kb, dr



def phase_A2(kb, block, din, dr, b):
    nc = kb.nc
    with ExitStack() as st:
        KT_sb = kb.sb(st, f"a2kt{b}", [128, NDC * T], BF16)
        Vp_sb = kb.sb(st, f"a2vp{b}", [128, NT * D], BF16)
        qt = kb.sb(st, f"a2qt{b}", [128, 2 * NDC * 128], BF16)
        P_bf = kb.sb(st, f"a2p{b}", [128, 2 * T], BF16)
        PT_sb = kb.sb(st, f"a2pt{b}", [128, 2 * T], BF16)
        x0t = kb.sb(st, f"a2x0{b}", [128, 2 * D], F32)
        p_sb = kb.sb(st, f"a2pb{b}", [128, 2 * D], F32)
        bdg = kb.sb(st, f"a2bd{b}", [128, 128], F32)
        idb = kb.sb(st, f"a2ib{b}", [128, 128], BF16)
        cols = kb.sb(st, f"a2c{b}", [128, 8], F32)
        cols_rs = kb.sb(st, f"a2cr{b}", [128, 16], F32)
        ps_s = [kb.ps(st, f"a2ps{b}{i}", [128, 512], F32) for i in range(2)]
        ps_t = [kb.ps(st, f"a2pt{b}{i}", [128, 512], BF16) for i in range(2)]
        ps_o = [kb.ps(st, f"a2po{b}{i}", [128, 512], F32) for i in range(4)]

        s_kv = kb.sem(f"a2kv{b}")
        s_qt = kb.sem(f"a2sqt{b}")
        s_x0 = kb.sem(f"a2sx0{b}")
        s_smm = kb.sem(f"a2smm{b}")
        s_dg = kb.sem(f"a2dg{b}")
        s_rs = kb.sem(f"a2rs{b}")
        s_pe = kb.sem(f"a2pe{b}")
        s_rc = kb.sem(f"a2rc{b}")
        s_ptr = kb.sem(f"a2ptr{b}")
        s_ptev = kb.sem(f"a2pte{b}")
        s_pv = kb.sem(f"a2pv{b}")
        s_pev = kb.sem(f"a2pev{b}")
        s_pst = kb.sem(f"a2pst{b}")

        def R(qc):
            return qc // 4 + 1

        def G(qc):
            return (qc + 4) // 4

        range_meta = []
        smm_at = {-1: 0}
        sev_at = {-1: 0}
        grp_at = {-1: 0, -2: 0}
        tot = 0
        gtot = 0
        for qc in range(NT):
            for r in range(R(qc)):
                range_meta.append((qc, r == R(qc) - 1))
            tot += R(qc)
            smm_at[qc] = tot
            sev_at[qc] = tot
            gtot += G(qc)
            grp_at[qc] = gtot

        def g(g):
            g.dma_start(
                out=KT_sb[:].rearrange("p (c t) -> p c t", c=NDC),
                in_=dr["KTd"][b].transpose([1, 0, 2]),
            ).then_inc(s_kv, 16)
            for hh in range(2):
                g.dma_start(
                    out=Vp_sb[:].rearrange("p (c d) -> p c d", c=NT)[:, hh * 16 : (hh + 1) * 16],
                    in_=dr["Vpd"][b, hh * 2048 : (hh + 1) * 2048, :].rearrange(
                        "(c p) d -> p c d", p=128),
                ).then_inc(s_kv, 16)
            g.dma_start(out=bdg[:], in_=din["b_diag"]).then_inc(s_kv, 16)
            g.dma_start(out=idb[:], in_=din["ident"]).then_inc(s_kv, 16)
        block.gpsimd(g)

        def sy(sy):
            for qc in range(NT):
                if qc >= 2:
                    sy.wait_ge(s_pev, qc - 1)
                sy.dma_start(
                    out=x0t[:, (qc % 2) * D : (qc % 2) * D + D],
                    in_=din["xsh"][b, qc * 128 : (qc + 1) * 128, :],
                ).then_inc(s_x0, 16)
                if qc >= 1:
                    sy.wait_ge(s_pev, qc)
                    sy.dma_start(
                        out=dr["pbuf"][b, (qc - 1) * 128 : qc * 128, :],
                        in_=p_sb[:, ((qc - 1) % 2) * D : ((qc - 1) % 2) * D + D],
                    ).then_inc(s_pst, 16)
            sy.wait_ge(s_pev, NT)
            sy.dma_start(
                out=dr["pbuf"][b, (NT - 1) * 128 : NT * 128, :],
                in_=p_sb[:, ((NT - 1) % 2) * D : ((NT - 1) % 2) * D + D],
            ).then_inc(s_pst, 16)
            sy.wait_ge(s_pst, 16 * NT)
        block.sync(sy)

        def trpv(tn, qc):
            nk = qc + 1
            ngr = G(qc)
            tn.wait_ge(s_pe, smm_at[qc])
            for gi in range(ngr):
                gidx = grp_at[qc - 1] + gi + 1
                if gidx > 2:
                    tn.wait_ge(s_ptev, gidx - 2)
                for j in range(min(4, nk - 4 * gi)):
                    kc = 4 * gi + j
                    tr = tn.transpose(
                        ps_t[(gidx - 1) % 2][:, j * 128 : (j + 1) * 128],
                        P_bf[:, (qc % 2) * T + kc * 128 : (qc % 2) * T + (kc + 1) * 128],
                        idb[:],
                    )
                tr.then_inc(s_ptr, 1)
            if qc >= 2:
                tn.wait_ge(s_pev, qc - 1)
            for gi in range(ngr):
                gidx = grp_at[qc - 1] + gi + 1
                tn.wait_ge(s_ptev, gidx)
                for j in range(min(4, nk - 4 * gi)):
                    kc = 4 * gi + j
                    for nh in range(2):
                        mm = tn.matmul(
                            ps_o[(qc % 2) * 2 + nh][:],
                            PT_sb[:, (qc % 2) * T + kc * 128 : (qc % 2) * T + (kc + 1) * 128],
                            Vp_sb[:, kc * D + nh * 512 : kc * D + nh * 512 + 512],
                            start=(kc == 0), stop=(kc == nk - 1),
                        )
            mm.then_inc(s_pv, 1)

        def t(tn):
            tn.wait_ge(s_kv, 80)
            rtot = 0
            for qc in range(NT):
                nk = qc + 1
                tn.wait_ge(s_qt, 16 * (qc + 1))
                if qc >= 2:
                    tn.wait_ge(s_ptr, grp_at[qc - 2])
                for r in range(R(qc)):
                    rtot += 1
                    if rtot > 2:
                        tn.wait_ge(s_pe, rtot - 2)
                    w = min(512, nk * 128 - r * 512)
                    pt = ps_s[(rtot - 1) % 2]
                    for kc in range(NDC):
                        mm = tn.matmul(
                            pt[:, 0:w],
                            qt[:, (qc % 2) * 1024 + kc * 128 : (qc % 2) * 1024 + (kc + 1) * 128],
                            KT_sb[:, kc * T + r * 512 : kc * T + r * 512 + w],
                            start=(kc == 0), stop=(kc == NDC - 1),
                        )
                    mm.then_inc(s_smm, 1)
                if qc >= 1:
                    trpv(tn, qc - 1)
            trpv(tn, NT - 1)
        block.tensor(t)

        def s(s):
            s.dma_start(
                out=qt[:, (0 % 2) * 1024 : (0 % 2) * 1024 + 1024].rearrange(
                    "p (c t) -> p c t", c=NDC),
                in_=dr["QTd"][b][:, :, 0 * 128 : (0 + 1) * 128].transpose([1, 0, 2]),
            ).then_inc(s_qt, 16)
            s.dma_start(
                out=qt[:, (1 % 2) * 1024 : (1 % 2) * 1024 + 1024].rearrange(
                    "p (c t) -> p c t", c=NDC),
                in_=dr["QTd"][b][:, :, 1 * 128 : (1 + 1) * 128].transpose([1, 0, 2]),
            ).then_inc(s_qt, 16)
            rtot = 0
            for qc in range(NT):
                nk = qc + 1
                if qc >= 2:
                    s.wait_ge(s_ptr, grp_at[qc - 2])
                    s.wait_ge(s_rc, qc - 1)
                for r in range(R(qc)):
                    rtot += 1
                    s.wait_ge(s_smm, rtot)
                    if r == R(qc) - 1:
                        s.wait_ge(s_dg, qc + 1)
                    w = min(512, nk * 128 - r * 512)
                    pt = ps_s[(rtot - 1) % 2]
                    ins = s.activation(
                        P_bf[:, (qc % 2) * T + r * 512 : (qc % 2) * T + r * 512 + w],
                        pt[:, 0:w], AF.Exp,
                        accum_out=cols_rs[:, (qc % 2) * 8 + r : (qc % 2) * 8 + r + 1],
                    )
                    ins.then_inc(s_pe, 1)
                if qc + 2 < NT:
                    s.wait_ge(s_smm, smm_at[qc])
                    s.dma_start(
                        out=qt[:, ((qc + 2) % 2) * 1024 : ((qc + 2) % 2) * 1024 + 1024].rearrange(
                            "p (c t) -> p c t", c=NDC),
                        in_=dr["QTd"][b][:, :, (qc + 2) * 128 : ((qc + 2) + 1) * 128].transpose([1, 0, 2]),
                    ).then_inc(s_qt, 16)
        block.scalar(s)

        def v(v):
            for qc in range(NT):
                nk = qc + 1
                rtot = smm_at[qc]
                v.wait_ge(s_smm, rtot)
                lr = R(qc) - 1
                w = nk * 128 - lr * 512
                pt = ps_s[(rtot - 1) % 2]
                v.tensor_tensor(
                    pt[:, w - 128 : w],
                    pt[:, w - 128 : w], bdg[:], ALU.add,
                ).then_inc(s_dg, 1)
                if qc >= 1:
                    pq = qc - 1
                    for gi in range(G(pq)):
                        gidx = grp_at[pq - 1] + gi + 1
                        v.wait_ge(s_ptr, gidx)
                        if gidx > 2:
                            v.wait_ge(s_pv, pq)
                        wg = min(512, pq * 128 + 128 - gi * 512)
                        ins = v.tensor_copy(
                            PT_sb[:, (pq % 2) * T + gi * 512 : (pq % 2) * T + gi * 512 + wg],
                            ps_t[(gidx - 1) % 2][:, 0:wg],
                        )
                        ins.then_inc(s_ptev, 1)
                    v.wait_ge(s_pv, pq + 1)
                    v.wait_ge(s_x0, 16 * (pq + 1))
                    v.wait_ge(s_rc, pq + 1)
                    if pq >= 2:
                        v.wait_ge(s_pst, 16 * (pq - 1))
                    for nh in range(2):
                        ins = v.scalar_tensor_tensor(
                            p_sb[:, (pq % 2) * D + nh * 512 : (pq % 2) * D + nh * 512 + 512],
                            ps_o[(pq % 2) * 2 + nh][:], cols[:, 4 + pq % 2 : 5 + pq % 2],
                            x0t[:, (pq % 2) * D + nh * 512 : (pq % 2) * D + nh * 512 + 512],
                            ALU.mult, ALU.add,
                        )
                    ins.then_inc(s_pev, 1)
                v.wait_ge(s_pe, smm_at[qc])
                ins = v.reduce_sum(cols[:, 2 + qc % 2 : 3 + qc % 2],
                                   cols_rs[:, (qc % 2) * 8 : (qc % 2) * 8 + R(qc)],
                                   axis=AX.X)
                ins.then_inc(s_rs, 1)
                v.wait_ge(s_rs, qc + 1)
                v.reciprocal(cols[:, 4 + qc % 2 : 5 + qc % 2],
                             cols[:, 2 + qc % 2 : 3 + qc % 2]).then_inc(s_rc, 1)
            pq = NT - 1
            for gi in range(G(pq)):
                gidx = grp_at[pq - 1] + gi + 1
                v.wait_ge(s_ptr, gidx)
                if gidx > 2:
                    v.wait_ge(s_pv, pq)
                wg = min(512, pq * 128 + 128 - gi * 512)
                ins = v.tensor_copy(
                    PT_sb[:, (pq % 2) * T + gi * 512 : (pq % 2) * T + gi * 512 + wg],
                    ps_t[(gidx - 1) % 2][:, 0:wg],
                )
                ins.then_inc(s_ptev, 1)
            v.wait_ge(s_pv, pq + 1)
            v.wait_ge(s_x0, 16 * (pq + 1))
            v.wait_ge(s_rc, pq + 1)
            v.wait_ge(s_pst, 16 * (pq - 1))
            for nh in range(2):
                ins = v.scalar_tensor_tensor(
                    p_sb[:, (pq % 2) * D + nh * 512 : (pq % 2) * D + nh * 512 + 512],
                    ps_o[(pq % 2) * 2 + nh][:], cols[:, 4 + pq % 2 : 5 + pq % 2],
                    x0t[:, (pq % 2) * D + nh * 512 : (pq % 2) * D + nh * 512 + 512],
                    ALU.mult, ALU.add,
                )
            ins.then_inc(s_pev, 1)
        block.vector(v)

        return {
            "gpsimd": [(s_kv, 80)],
            "sync": [(s_pst, 16 * NT)],
            "tensor": [(s_smm, sum(R(q) for q in range(NT))), (s_pv, NT),
                       (s_ptr, grp_at[NT - 1])],
            "scalar": [(s_pe, sum(R(q) for q in range(NT))), (s_qt, 16 * NT)],
            "vector": [(s_pev, NT), (s_ptev, grp_at[NT - 1]), (s_rc, NT),
                       (s_rs, NT), (s_dg, NT)],
        }


def phase_FI(kb, block, din, dr, stream, b):
    with ExitStack() as st:
        s_z = kb.sem(f"fi{b}z")

        def vz(v):
            v.memset(
                stream[:].rearrange("p (c w) -> p c w", c=NDC)[:, :, 0:1], 0.0
            ).then_inc(s_z, 1)
        block.vector(vz)

        def writer(s, tc, half, pt):
            dst = stream[:].rearrange("p (c w) -> p c w", c=NDC)[
                :, 4 * half : 4 * half + 4, 1 + tc * 128 : 1 + (tc + 1) * 128
            ]
            return s.copy(dst, pt[:].rearrange("p (c t) -> p c t", c=4))
        flush = emit_ln_tiles(kb, block, st, f"fi{b}", b, din, writer)
        flush.setdefault("vector", []).append((s_z, 1))
    return flush


FA_NCH = [NT]
FA_DBG = set()


def phase_Fa(kb, block, din, dr, stream, b, l):
    tag = f"fa{b}{l}"
    NCH = FA_NCH[0]
    with ExitStack() as st:
        wqkv_sb = kb.sb(st, tag + "wq", [128, NDC * 3 * INNER], BF16)
        wo_sb = kb.sb(st, tag + "wo", [128, 4 * D], BF16)
        qsc_bc = kb.sb(st, tag + "qs", [128, INNER], BF16)
        ksc_bc = kb.sb(st, tag + "ks", [128, INNER], BF16)
        bmask = kb.sb(st, tag + "bm", [128, 256], F32)
        idb = kb.sb(st, tag + "ib", [128, 128], BF16)
        tbl = kb.sb(st, tag + "tb", [128, 2 * 4 * 64], F32)
        xsh = kb.sb(st, tag + "xs", [128, D], BF16)
        qkf = kb.sb(st, tag + "qk", [128, 2 * INNER], F32)
        nrm = kb.sb(st, tag + "nr", [128, 16], F32)
        rst = kb.sb(st, tag + "rs", [128, 16], F32)
        tmp1 = kb.sb(st, tag + "t1", [128, INNER], F32)
        sqsc = kb.sb(st, tag + "sq", [128, DH], F32)
        qrot = kb.sb(st, tag + "qr", [128, INNER], BF16)
        krot = kb.sb(st, tag + "kr", [128, INNER], BF16)
        qT = kb.sb(st, tag + "qt", [64, HEADS * 128], BF16)
        kring = kb.sb(st, tag + "kg", [64, HEADS * 5 * 128], BF16)
        vring = kb.sb(st, tag + "vg", [128, 5 * 512], BF16)
        Pbn = kb.sb(st, tag + "pb", [128, 3 * 640], BF16)
        PT_sb = kb.sb(st, tag + "pt", [128, 2 * 640], BF16)
        oT_buf = kb.sb(st, tag + "ob", [128, 4 * 512], BF16)
        ncol = kb.sb(st, tag + "nc", [128, 24], F32)
        scr5 = kb.sb(st, tag + "s5", [128, 256], F32)
        stash = kb.sb(st, tag + "sh", [128, 4], F32)

        ps_qkv = kb.ps(st, tag + "pq", [128, 512], F32)
        ps_sa = [kb.ps(st, tag + f"psa{i}", [128, 512], F32) for i in range(2)]
        ps_sbp = kb.ps(st, tag + "psb", [128, 256], F32)
        ps_tr = [kb.ps(st, tag + f"ptr{i}", [128, 1024], BF16) for i in range(2)]
        ps_oT = kb.ps(st, tag + "pot", [128, 512], F32)
        ps_wo = kb.ps(st, tag + "pwo", [128, 512], F32)

        s_ini = kb.sem(tag + "ini")
        s_in2 = kb.sem(tag + "in2")
        s_in3 = kb.sem(tag + "in3")
        s_tb = kb.sem(tag + "tbl")
        s_xc = kb.sem(tag + "xc")
        s_qkvm = kb.sem(tag + "qm")
        s_qev = kb.sem(tag + "qe")
        s_sqq = kb.sem(tag + "q2")
        s_rst = kb.sem(tag + "rt")
        s_rot = kb.sem(tag + "ro")
        s_qktr = kb.sem(tag + "qk2")
        s_trev = kb.sem(tag + "tv")
        s_smm = kb.sem(tag + "sm")
        s_msk = kb.sem(tag + "mk")
        s_peh = kb.sem(tag + "pe")
        s_rci = kb.sem(tag + "rc")
        s_pnb = kb.sem(tag + "pn")
        s_ptr = kb.sem(tag + "pr")
        s_ptev = kb.sem(tag + "pv")
        s_pvm = kb.sem(tag + "pm")
        s_oev = kb.sem(tag + "oe")
        s_wom = kb.sem(tag + "wm")
        s_wov = kb.sem(tag + "wv")
        s_sta = kb.sem(tag + "sa")
        s_spc = kb.sem(tag + "sp")
        s_vv = kb.sem(tag + "vv")

        def nsl(tc):
            return min(tc + 1, 5)

        def spans(tc, h):
            n = nsl(tc)
            segs = [(h * 640 + (tc % 5) * 128, 128, 0)]
            lo = tc - n + 1
            si = 1
            while si < n:
                s0 = (lo + si - 1) % 5
                run = min(n - si, 5 - s0)
                segs.append((h * 640 + s0 * 128, run * 128, si))
                si += run
            return segs

        def g(g):
            g.dma_start(
                out=wqkv_sb[:].rearrange("p (k m) -> p k m", k=NDC),
                in_=din["wqkv"][l].rearrange("(k p) m -> p k m", p=128),
            ).then_inc(s_ini, 16)
            g.dma_start(
                out=wo_sb[:].rearrange("p (k m) -> p k m", k=4),
                in_=din["wo"][l].rearrange("(k p) m -> p k m", p=128),
            ).then_inc(s_ini, 16)
            g.dma_start(out=qsc_bc[:], in_=din["qscale_bc"][l]).then_inc(s_in3, 16)
            g.dma_start(out=ksc_bc[:], in_=din["kscale_bc"][l]).then_inc(s_in3, 16)
            g.dma_start(out=idb[:], in_=din["ident"]).then_inc(s_in3, 16)
            for tc in range(NCH):
                if tc >= 1:
                    g.wait_ge(s_qkvm, 3 * tc)
                bound = tc % 4 == 0 and tc >= 4
                if bound:
                    g.wait_ge(s_sta, tc // 4)
                for dc in range(NDC):
                    off = dc * STR_W + tc * 128 + (1 if dc < 4 else 0)
                    if dc >= 4 and bound:
                        g.tensor_copy(xsh[:, dc * 128 : dc * 128 + 1],
                                      stash[:, dc - 4 : dc - 3])
                        ins = g.tensor_copy(
                            xsh[:, dc * 128 + 1 : (dc + 1) * 128],
                            stream[:, off + 1 : off + 128])
                    else:
                        ins = g.tensor_copy(
                            xsh[:, dc * 128 : (dc + 1) * 128],
                            stream[:, off : off + 128])
                ins.then_inc(s_xc, 1)
        block.gpsimd(g)

        def sy(sy):
            sy.dma_start(out=bmask[:, 0:128], in_=din["b_diag"]).then_inc(s_in2, 16)
            sy.dma_start(out=bmask[:, 128:256], in_=din["b_old"]).then_inc(s_in2, 16)
            for gi in range(NCH):
                if gi >= 2:
                    sy.wait_ge(s_rot, 2 * (gi - 1))
                for ti, nmt in enumerate(("csq", "ssq", "csk", "ssk")):
                    sy.dma_start(
                        out=tbl[:, (gi % 2) * 256 + ti * 64 : (gi % 2) * 256 + (ti + 1) * 64],
                        in_=din[nmt][gi * 128 : (gi + 1) * 128, :],
                    ).then_inc(s_tb, 16)
        block.sync(sy)

        def t(tn):
            tn.wait_ge(s_ini, 32)
            tn.wait_ge(s_in3, 48)
            pend_wo = []
            pend_pp = []

            def emit_wo(k):
                for _ in range(k):
                    if pend_wo:
                        pend_wo.pop(0)()

            def ptrpv(ptc, h):
                pn = nsl(ptc)
                if FA_DBG & {"nosbp", "nosbpexp"}:
                    pn = min(pn, 4)
                gh = ptc * HEADS + h
                p3 = (gh % 3) * 640
                p2 = (gh % 2) * 640
                trb = ps_tr[gh % 2]
                tn.wait_ge(s_pnb, gh + 1)
                if gh >= 2:
                    tn.wait_ge(s_ptev, gh - 1)
                if h == 0:
                    tn.wait_ge(s_qev, 3 * (ptc + 1))
                for si in range(pn):
                    tr = tn.transpose(
                        trb[:, si * 128 : (si + 1) * 128],
                        Pbn[:, p3 + si * 128 : p3 + (si + 1) * 128], idb[:])
                tr.then_inc(s_ptr, 1)
                tn.wait_ge(s_ptev, gh + 1)
                if h == 0 and ptc >= 1:
                    tn.wait_ge(s_oev, ptc)
                for si in range(pn):
                    s_ = ptc if si == 0 else ptc - pn + si
                    tp = (0, 64) if h % 2 else None
                    mm = tn.matmul(
                        ps_oT[64 * (h % 2) : 64 * (h % 2) + 64,
                              (h // 2) * 128 : (h // 2) * 128 + 128],
                        vring[:, (s_ % 5) * 512 + h * 64 : (s_ % 5) * 512 + (h + 1) * 64],
                        PT_sb[:, p2 + si * 128 : p2 + (si + 1) * 128],
                        start=(si == 0), stop=(si == pn - 1),
                        tile_position=tp)
                mm.then_inc(s_pvm, 1)

            for tc in range(NCH):
                gh0 = tc * HEADS
                n = nsl(tc)
                tn.wait_ge(s_xc, tc + 1)
                if tc >= 1:
                    tn.wait_ge(s_qev, 3 * tc)
                for seg in range(3):
                    if seg >= 1:
                        tn.wait_ge(s_qev, 3 * tc + seg)
                    for dc in range(NDC):
                        mm = tn.matmul(
                            ps_qkv[:], xsh[:, dc * 128 : (dc + 1) * 128],
                            wqkv_sb[:, dc * 1536 + seg * 512 : dc * 1536 + (seg + 1) * 512],
                            start=(dc == 0), stop=(dc == NDC - 1))
                    mm.then_inc(s_qkvm, 1)
                    if pend_pp:
                        ptrpv(*pend_pp.pop(0))
                while pend_pp:
                    ptrpv(*pend_pp.pop(0))
                emit_wo(8)
                tn.wait_ge(s_rot, 2 * (tc + 1))
                if tc >= 1:
                    tn.wait_ge(s_trev, 4 * tc)
                    tn.wait_ge(s_ptev, 8 * tc)
                for ki, buf in enumerate((qrot, krot)):
                    for h in range(HEADS):
                        tp = (0, 64) if h % 2 else None
                        tr = tn.transpose(
                            ps_tr[ki][64 * (h % 2) : 64 * (h % 2) + 64,
                                      (h // 2) * 128 : (h // 2) * 128 + 128],
                            buf[:, h * 64 : (h + 1) * 64], idb[:],
                            tile_position=tp)
                tr.then_inc(s_qktr, 1)
                for h in range(HEADS):
                    gh = gh0 + h
                    if gh >= 2:
                        tn.wait_ge(s_peh, gh - 1)
                    if h == 0:
                        tn.wait_ge(s_trev, 4 * (tc + 1))
                    sb0 = (gh % 2) * 128
                    for col0, ncols, si0 in spans(tc, h):
                        if "nosbp" in FA_DBG and si0 + ncols // 128 > 4:
                            ncols -= 128
                            if ncols == 0:
                                continue
                        c = col0
                        rem = ncols
                        si = si0
                        while rem > 0:
                            if si < 4:
                                take = min(rem, (4 - si) * 128)
                                dst = ps_sa[gh % 2][:, si * 128 : si * 128 + take]
                            else:
                                take = rem
                                dst = ps_sbp[:, sb0 : sb0 + take]
                            mm = tn.matmul(
                                dst, qT[:, h * 128 : (h + 1) * 128],
                                kring[:, c : c + take],
                                start=True, stop=True)
                            c += take
                            si += take // 128
                            rem -= take
                    mm.then_inc(s_smm, 1)
                    pend_pp.append((tc, h))
                    while len(pend_pp) > 2:
                        ptrpv(*pend_pp.pop(0))
                if tc == NCH - 1:
                    while pend_pp:
                        ptrpv(*pend_pp.pop(0))
                if tc % 4 == 3:
                    gi = tc // 4

                    def mk(dc, gi=gi):
                        def go():
                            tn.wait_ge(s_oev, 4 * gi + 4)
                            if dc == 0:
                                if gi >= 1:
                                    tn.wait_ge(s_wov, 8 * gi)
                            else:
                                tn.wait_ge(s_wov, 8 * gi + dc)
                            for kc in range(4):
                                mm = tn.matmul(
                                    ps_wo[:],
                                    wo_sb[:, kc * D + dc * 128 : kc * D + (dc + 1) * 128],
                                    oT_buf[:, kc * 512 : (kc + 1) * 512],
                                    start=(kc == 0), stop=(kc == 3))
                            mm.then_inc(s_wom, 1)
                        return go
                    for dc in range(NDC):
                        pend_wo.append(mk(dc))
            emit_wo(len(pend_wo))
        block.tensor(t)

        spc_n = [0]

        def s(s):
            for tc in range(NCH):
                gh0 = tc * HEADS
                n = nsl(tc)
                s.wait_ge(s_qkvm, 3 * tc + 1)
                if tc >= 1:
                    s.wait_ge(s_rot, 2 * tc)
                ins = s.copy(qkf[:, 0:512], ps_qkv[:])
                ins.then_inc(s_qev, 1)
                s.wait_ge(s_qkvm, 3 * tc + 2)
                ins = s.copy(qkf[:, 512:1024], ps_qkv[:])
                ins.then_inc(s_qev, 1)
                s.wait_ge(s_qkvm, 3 * tc + 3)
                if tc >= 1:
                    s.wait_ge(s_pvm, 8 * tc)
                ins = s.copy(vring[:, (tc % 5) * 512 : (tc % 5) * 512 + 512], ps_qkv[:])
                ins.then_inc(s_qev, 1)
                s.wait_ge(s_qev, 3 * tc + 2)
                for hh in range(16):
                    ins = s.activation(sqsc[:], qkf[:, hh * 64 : (hh + 1) * 64],
                                       AF.Square, accum_out=nrm[:, hh : hh + 1])
                ins.then_inc(s_sqq, 1)
                s.wait_ge(s_sqq, tc + 1)
                s.activation(rst[:], nrm[:], AF.Ln, bias=kb.epses[:, 1:2]).then_inc(s_rst, 1)
                s.wait_ge(s_rst, 2 * tc + 1)
                s.activation(rst[:], rst[:], AF.Exp, scale=-0.5).then_inc(s_rst, 1)
                for h in range(HEADS):
                    gh = gh0 + h
                    p3 = (gh % 3) * 640
                    nb = (gh % 3) * 8
                    sb0 = (gh % 2) * 128
                    sbp_on = n == 5 and not (FA_DBG & {"nosbp", "nosbpexp"})
                    if sbp_on:
                        s.wait_ge(s_smm, gh + 1)
                        ins = s.copy(scr5[:, sb0 : sb0 + 128],
                                     ps_sbp[:, sb0 : sb0 + 128])
                        ins.then_inc(s_spc, 1)
                        spc_n[0] += 1
                    s.wait_ge(s_msk, gh + 1)
                    if gh >= 3:
                        s.wait_ge(s_ptr, gh - 2)
                    na = min(n, 4) * 128
                    ins = s.activation(
                        Pbn[:, p3 : p3 + na], ps_sa[gh % 2][:, 0:na], AF.Exp,
                        accum_out=ncol[:, nb : nb + 1])
                    if sbp_on:
                        s.wait_ge(s_spc, spc_n[0])
                        ins = s.activation(
                            Pbn[:, p3 + 512 : p3 + 640],
                            scr5[:, sb0 : sb0 + 128], AF.Exp,
                            accum_out=ncol[:, nb + 1 : nb + 2])
                    ins.then_inc(s_peh, 1)
        block.scalar(s)

        vv = [0]

        def v(v):
            pend_pt = []

            def flush_pt(k):
                for _ in range(k):
                    if pend_pt:
                        ptc, pgh = pend_pt.pop(0)
                        pn = nsl(ptc)
                        p2v = (pgh % 2) * 640
                        v.wait_ge(s_ptr, pgh + 1)
                        if pgh >= 2:
                            v.wait_ge(s_pvm, pgh - 1)
                        ins = v.tensor_copy(PT_sb[:, p2v : p2v + pn * 128],
                                            ps_tr[pgh % 2][:, 0 : pn * 128])
                        ins.then_inc(s_ptev, 1)

            def VI(ins):
                vv[0] += 1
                ins.then_inc(s_vv, 1)
            v.wait_ge(s_in2, 32)
            v.wait_ge(s_in3, 32)
            rot_end = {}

            pend_add = []

            def drain_add(k):
                for _ in range(k):
                    if pend_add:
                        pend_add.pop(0)()

            def tail(ptc):
                v.wait_ge(s_pvm, 8 * (ptc + 1))
                ins = v.tensor_copy(
                    oT_buf[:].rearrange("p (j g x) -> p j g x", j=4, g=4)[
                        :, :, ptc % 4, :],
                    ps_oT[:].rearrange("p (j x) -> p j x", j=4))
                ins.then_inc(s_oev, 1)
                if ptc % 4 == 3:
                    gi = ptc // 4

                    def mka(dc, gi=gi):
                        def go():
                            v.wait_ge(s_wom, 8 * gi + dc + 1)
                            sl = stream[:, dc * STR_W + 1 + gi * 512 :
                                        dc * STR_W + 1 + (gi + 1) * 512]
                            ins = v.tensor_tensor(sl, sl, ps_wo[:], ALU.add)
                            ins.then_inc(s_wov, 1)
                        return go
                    for dc in range(NDC):
                        pend_add.append(mka(dc))

            for tc in range(NCH):
                gh0 = tc * HEADS
                n = nsl(tc)
                flush_pt(len(pend_pt))
                if tc >= 1:
                    tail(tc - 1)
                if tc >= 1:
                    v.wait_ge(s_vv, rot_end[tc - 1])
                    v.wait_ge(s_rot, 2 * tc)
                v.wait_ge(s_rst, 2 * (tc + 1))
                rq = rst[:, 0:8].unsqueeze(2).broadcast_to([128, 8, 64]
                     ).rearrange("p h (two half) -> p h two half", two=2)
                rk = rst[:, 8:16].unsqueeze(2).broadcast_to([128, 8, 64]
                     ).rearrange("p h (two half) -> p h two half", two=2)

                def q4(x):
                    return x.rearrange("p (h two half) -> p h two half", h=8, two=2)

                def sw4(x):
                    return q4(x)[:, :, ::-1, :]

                def tb4(ts):
                    return ts.rearrange("p (two half) -> p two half", two=2
                           ).unsqueeze(1).broadcast_to([128, 8, 2, 32])
                VI(v.tensor_tensor(q4(qkf[:, 0:512]), q4(qkf[:, 0:512]), rq, ALU.mult))
                drain_add(1)
                VI(v.tensor_tensor(q4(qkf[:, 512:1024]), q4(qkf[:, 512:1024]), rk, ALU.mult))
                drain_add(1)
                v.wait_ge(s_vv, vv[0])
                VI(v.tensor_tensor(q4(qkf[:, 0:512]), q4(qkf[:, 0:512]), q4(qsc_bc[:]), ALU.mult))
                drain_add(1)
                VI(v.tensor_tensor(q4(qkf[:, 512:1024]), q4(qkf[:, 512:1024]), q4(ksc_bc[:]), ALU.mult))
                drain_add(1)
                v.wait_ge(s_tb, 64 * (tc + 1))
                tb0 = (tc % 2) * 256
                cq = tb4(tbl[:, tb0 + 0 : tb0 + 64])
                sq_ = tb4(tbl[:, tb0 + 64 : tb0 + 128])
                ck = tb4(tbl[:, tb0 + 128 : tb0 + 192])
                sk = tb4(tbl[:, tb0 + 192 : tb0 + 256])
                v.wait_ge(s_vv, vv[0])
                VI(v.tensor_tensor(q4(tmp1[:]), sw4(qkf[:, 0:512]), sq_, ALU.mult))
                drain_add(1)
                v.wait_ge(s_vv, vv[0])
                VI(v.tensor_tensor(q4(qkf[:, 0:512]), q4(qkf[:, 0:512]), cq, ALU.mult))
                drain_add(1)
                v.wait_ge(s_vv, vv[0])
                ins = v.tensor_add(qrot[:], tmp1[:], qkf[:, 0:512])
                ins.then_inc(s_rot, 1)
                drain_add(1)
                VI(v.tensor_tensor(q4(tmp1[:]), sw4(qkf[:, 512:1024]), sk, ALU.mult))
                v.wait_ge(s_vv, vv[0])
                VI(v.tensor_tensor(q4(qkf[:, 512:1024]), q4(qkf[:, 512:1024]), ck, ALU.mult))
                drain_add(1)
                v.wait_ge(s_vv, vv[0])
                ins = v.tensor_add(krot[:], tmp1[:], qkf[:, 512:1024])
                ins.then_inc(s_rot, 1)
                rot_end[tc] = vv[0]
                v.wait_ge(s_qktr, tc + 1)
                v.wait_ge(s_smm, 8 * tc)
                qTv = qT[:].rearrange("p (h x) -> p h x", h=HEADS)
                krv = kring[:].rearrange("p (h s x) -> p h s x", h=HEADS, s=5)[
                    :, :, tc % 5, :]
                ins = v.tensor_copy(qTv[:, 0::2, :],
                                    ps_tr[0][0:64, 0:512].rearrange("p (j x) -> p j x", j=4))
                ins.then_inc(s_trev, 1)
                ins = v.tensor_copy(qTv[:, 1::2, :],
                                    ps_tr[0][64:128, 0:512].rearrange("p (j x) -> p j x", j=4))
                ins.then_inc(s_trev, 1)
                ins = v.tensor_copy(krv[:, 0::2, :],
                                    ps_tr[1][0:64, 0:512].rearrange("p (j x) -> p j x", j=4))
                ins.then_inc(s_trev, 1)
                ins = v.tensor_copy(krv[:, 1::2, :],
                                    ps_tr[1][64:128, 0:512].rearrange("p (j x) -> p j x", j=4))
                ins.then_inc(s_trev, 1)

                def vpartB(pgh):
                    nb = (pgh % 3) * 8
                    p3 = (pgh % 3) * 640
                    pn = nsl(pgh // HEADS)
                    if FA_DBG & {"nosbp", "nosbpexp"}:
                        pn = min(pn, 4)
                    v.wait_ge(s_peh, pgh + 1)
                    if pn == 5:
                        VI(v.tensor_add(ncol[:, nb : nb + 1], ncol[:, nb : nb + 1],
                                        ncol[:, nb + 1 : nb + 2]))
                        v.wait_ge(s_vv, vv[0])
                    ins = v.reciprocal(ncol[:, nb + 2 : nb + 3],
                                       ncol[:, nb : nb + 1])
                    ins.then_inc(s_rci, 1)
                    v.wait_ge(s_rci, pgh + 1)
                    ins = v.tensor_scalar(
                        Pbn[:, p3 : p3 + pn * 128],
                        Pbn[:, p3 : p3 + pn * 128],
                        ncol[:, nb + 2 : nb + 3], None, ALU.mult)
                    ins.then_inc(s_pnb, 1)

                mw = 256 if (tc >= 4 and "nobold" not in FA_DBG) else 128
                for h in range(HEADS):
                    gh = gh0 + h
                    v.wait_ge(s_smm, gh + 1)
                    ins = v.tensor_tensor(ps_sa[gh % 2][:, 0:mw],
                                          ps_sa[gh % 2][:, 0:mw],
                                          bmask[:, 0:mw], ALU.add)
                    ins.then_inc(s_msk, 1)
                    if h >= 1:
                        vpartB(gh - 1)
                    pend_pt.append((tc, gh))
                    flush_pt(len(pend_pt) - 2)
                vpartB(gh0 + HEADS - 1)
                if tc % 4 == 3 and tc < NCH - 1:
                    gi = tc // 4
                    for dc in range(4, 8):
                        ins = v.tensor_copy(
                            stash[:, dc - 4 : dc - 3],
                            stream[:, dc * STR_W + (gi + 1) * 512 :
                                   dc * STR_W + (gi + 1) * 512 + 1])
                    ins.then_inc(s_sta, 1)
                if tc == NCH - 1:
                    flush_pt(len(pend_pt))
                    tail(tc)
                    drain_add(len(pend_add))
        block.vector(v)

        ngr = NCH // 4
        return {
            "gpsimd": [(s_xc, NCH), (s_ini, 32), (s_in3, 48)],
            "sync": [(s_tb, 64 * NCH), (s_in2, 32)],
            "tensor": [(s_qkvm, 3 * NCH), (s_qktr, NCH), (s_smm, 8 * NCH),
                       (s_wom, 8 * ngr), (s_pvm, 8 * NCH), (s_ptr, 8 * NCH)],
            "scalar": [(s_qev, 3 * NCH), (s_sqq, NCH), (s_rst, 2 * NCH),
                       (s_peh, 8 * NCH), (s_ptev, 8 * NCH), (s_spc, spc_n[0])],
            "vector": [(s_rot, 2 * NCH), (s_trev, 4 * NCH),
                       (s_msk, 8 * NCH), (s_sta, ngr - 1), (s_vv, vv[0]),
                       (s_wov, 8 * ngr), (s_oev, NCH), (s_pnb, 8 * NCH),
                       (s_rci, 8 * NCH)],
        }


def phase_Fb(kb, block, din, dr, stream, b, l):
    tag = f"fb{b}{l}"
    NTR = 8
    with ExitStack() as st:
        h2bf = kb.sb(st, tag + "h2", [128, NDC * 512], BF16)
        z_bf = kb.sb(st, tag + "zb", [128, NDC * 512], BF16)
        sq2 = kb.sb(st, tag + "sq", [128, NDC * 512], BF16)
        R1sb = kb.sb(st, tag + "r1", [128, 512], F32)
        R2sb = kb.sb(st, tag + "r2", [128, 512], F32)
        musb = kb.sb(st, tag + "mu", [128, 512], F32)
        ztmp = kb.sb(st, tag + "zt", [128, 512], F32)
        btmp = musb
        h4_buf = kb.sb(st, tag + "h4", [128, NPAIR * 512], BF16)
        w1t = kb.sb(st, tag + "w1", [128, 3 * NDC * 256], BF16)
        w2t = kb.sb(st, tag + "w2", [128, 2 * NPAIR * 128], BF16)
        gel = kb.sb(st, tag + "ge", [128, 2 * 512], BF16)
        b1c = kb.sb(st, tag + "b1", [128, 2 * NPAIR], F32)
        stash = kb.sb(st, tag + "sh", [128, 4], F32)
        lngc = kb.sb(st, tag + "lg", [128, 8], F32)
        lnbc = kb.sb(st, tag + "lb", [128, 8], F32)
        b2c = kb.sb(st, tag + "b2", [128, 8], F32)
        onb_nd = kb.sb(st, tag + "on", [128, 128], BF16)

        ps_mu = kb.ps(st, tag + "pmu", [128, 512], F32)
        ps_ms = kb.ps(st, tag + "pms", [128, 512], F32)
        ps_bc1 = kb.ps(st, tag + "pb1", [128, 512], F32)
        ps_bc2 = kb.ps(st, tag + "pb2", [128, 512], F32)
        ps_a = kb.ps(st, tag + "pa0", [128, 512], F32)
        ps_g = kb.ps(st, tag + "pg0", [128, 512], F32)
        ps_a2 = kb.ps(st, tag + "pa1", [128, 512], F32)
        ps_g2 = kb.ps(st, tag + "pg1", [128, 512], F32)

        s_ini = kb.sem(tag + "ini")
        s_in2 = kb.sem(tag + "in2")
        s_h2c = kb.sem(tag + "hc")
        s_sq2 = kb.sem(tag + "sq")
        s_stm = kb.sem(tag + "st")
        s_msv = kb.sem(tag + "mv")
        s_rld = kb.sem(tag + "rl")
        s_r2 = kb.sem(tag + "m2")
        s_z = kb.sem(tag + "zz")
        s_w1 = kb.sem(tag + "w1")
        s_h3m = kb.sem(tag + "h3")
        s_gel = kb.sem(tag + "gl")
        s_h4 = kb.sem(tag + "h4")
        s_w2 = kb.sem(tag + "w2")
        s_xm = kb.sem(tag + "xm")
        s_xadd = kb.sem(tag + "xa")
        s_stb = kb.sem(tag + "sb")

        def sy(sy):
            sy.dma_start(out=b1c[:], in_=din["b1_pk"][l]).then_inc(s_ini, 16)
            sy.dma_start(out=lngc[:], in_=din["lng_pk"][l]).then_inc(s_ini, 16)
            sy.dma_start(out=lnbc[:], in_=din["lnb_pk"][l]).then_inc(s_ini, 16)
            sy.dma_start(out=b2c[:], in_=din["b2_pk"][l]).then_inc(s_ini, 16)
        block.sync(sy)

        def g(g):
            g.dma_start(out=onb_nd[:], in_=din["ones_nd"]).then_inc(s_in2, 16)
            def cast_op(tr):
                if tr >= 1:
                    g.wait_ge(s_stm, 2 * tr)
                    g.wait_ge(s_sq2, tr)
                    g.wait_ge(s_stb, tr)
                h2v = h2bf[:].rearrange("p (d x) -> p d x", d=NDC)
                stv = stream[:].rearrange("p (d w) -> p d w", d=NDC)
                g.tensor_copy(h2v[:, 0:4, :],
                              stv[:, 0:4, tr * 512 + 1 : tr * 512 + 513])
                if tr >= 1:
                    g.tensor_copy(
                        h2v[:, 4:8, 0:1],
                        stash[:].rearrange("p (d x) -> p d x", d=4))
                    ins = g.tensor_copy(
                        h2v[:, 4:8, 1:512],
                        stv[:, 4:8, tr * 512 + 1 : tr * 512 + 512])
                else:
                    ins = g.tensor_copy(h2v[:, 4:8, :],
                                        stv[:, 4:8, 0:512])
                ins.then_inc(s_h2c, 1)

            def w1_loads(tr, lo=0, hi=NPAIR):
                for kc in range(lo, hi):
                    sz = PAIR_SZ[kc]
                    gk = tr * NPAIR + kc
                    if gk >= 3:
                        g.wait_ge(s_h3m, gk - 2)
                    g.dma_start(
                        out=w1t[:, ((gk % 3) * NDC) * 256 : ((gk % 3) * NDC) * 256 + NDC * 2 * sz
                                ].rearrange("p (c m) -> p c m", c=NDC),
                        in_=dr["w1p_bf"][l].rearrange("(c p) m -> p c m", p=128)[
                            :, :, CH_OFF[kc] : CH_OFF[kc] + 2 * sz],
                    ).then_inc(s_w1, 16)

            cast_op(0)
            w1_loads(0, 0, 14)
            for tr in range(NTR):
                if tr < NTR - 1:
                    cast_op(tr + 1)
                w1_loads(tr, 14, NPAIR)
                if tr < NTR - 1:
                    w1_loads(tr + 1, 0, 3)
                for dc in range(NDC):
                    if tr * NDC + dc >= 2:
                        g.wait_ge(s_xm, tr * NDC + dc - 1)
                    g.dma_start(
                        out=w2t[:, (dc % 2) * NPAIR * 128 : (dc % 2) * NPAIR * 128 + 21 * 128
                                ].rearrange("p (c m) -> p c m", c=21),
                        in_=dr["w2_bf"][l][0:2688, dc * 128 : (dc + 1) * 128].rearrange(
                            "(c p) m -> p c m", p=128),
                    ).then_inc(s_w2, 16)
                    g.dma_start(
                        out=w2t[0:42, (dc % 2) * NPAIR * 128 + 21 * 128 : (dc % 2) * NPAIR * 128 + 22 * 128],
                        in_=dr["w2_bf"][l][2688:2730, dc * 128 : (dc + 1) * 128],
                    ).then_inc(s_w2, 16)
                if tr < NTR - 1:
                    w1_loads(tr + 1, 3, 14)
        block.gpsimd(g)

        def t(tn):
            tn.wait_ge(s_ini, 64)
            tn.wait_ge(s_in2, 16)

            def stats_mm(T):
                tn.wait_ge(s_h2c, T + 1)
                if T >= 1:
                    tn.wait_ge(s_msv, 3 * T - 2)
                for dc in range(NDC):
                    mm = tn.matmul(ps_mu[:], onb_nd[:],
                                   h2bf[:, dc * 512 : (dc + 1) * 512],
                                   start=(dc == 0), stop=(dc == NDC - 1))
                mm.then_inc(s_stm, 1)
                tn.wait_ge(s_sq2, T + 1)
                if T >= 1:
                    tn.wait_ge(s_rld, 2 * T - 1)
                for dc in range(NDC):
                    mm = tn.matmul(ps_ms[:], onb_nd[:],
                                   sq2[:, dc * 512 : (dc + 1) * 512],
                                   start=(dc == 0), stop=(dc == NDC - 1))
                mm.then_inc(s_stm, 1)

            stats_mm(0)
            for tr in range(NTR):
                for kc in range(NPAIR):
                    sz = PAIR_SZ[kc]
                    gk = tr * NPAIR + kc
                    tn.wait_ge(s_w1, 16 * (gk + 1))
                    tn.wait_ge(s_z, 8 * (tr + 1))
                    if gk >= 2:
                        tn.wait_ge(s_h4, gk - 1)
                    pa = ps_a if kc % 2 == 0 else ps_a2
                    pg = ps_g if kc % 2 == 0 else ps_g2
                    for dc in range(NDC):
                        base = (gk % 3) * NDC * 256 + dc * 2 * sz
                        tn.matmul(pa[0:sz, :], w1t[:, base : base + sz],
                                  z_bf[:, dc * 512 : (dc + 1) * 512],
                                  start=(dc == 0), stop=(dc == NDC - 1))
                        mm = tn.matmul(pg[0:sz, :], w1t[:, base + sz : base + 2 * sz],
                                       z_bf[:, dc * 512 : (dc + 1) * 512],
                                       start=(dc == 0), stop=(dc == NDC - 1))
                    mm.then_inc(s_h3m, 1)
                if tr < NTR - 1:
                    stats_mm(tr + 1)
                for dc in range(NDC):
                    gd = tr * NDC + dc
                    tn.wait_ge(s_w2, 32 * (gd + 1))
                    px = ps_bc1 if dc % 2 == 0 else ps_bc2
                    if dc >= 2:
                        tn.wait_ge(s_xadd, 8 * tr + dc - 1)
                    for kc in range(NPAIR):
                        sz = PAIR_SZ[kc]
                        tn.wait_ge(s_h4, tr * NPAIR + kc + 1)
                        mm = tn.matmul(
                            px[:],
                            w2t[0:sz, (dc % 2) * NPAIR * 128 + kc * 128 : (dc % 2) * NPAIR * 128 + (kc + 1) * 128],
                            h4_buf[0:sz, kc * 512 : (kc + 1) * 512],
                            start=(kc == 0), stop=(kc == NPAIR - 1),
                        )
                    mm.then_inc(s_xm, 1)
        block.tensor(t)

        def s(s):
            def sq_op(T):
                s.wait_ge(s_h2c, T + 1)
                if T >= 1:
                    s.wait_ge(s_stm, 2 * T)
                s.activation(sq2[:], h2bf[:], AF.Square).then_inc(s_sq2, 1)

            def rstd_op(T):
                s.wait_ge(s_msv, 3 * (T + 1))
                if T >= 1:
                    s.wait_ge(s_z, 8 * T)
                s.activation(R1sb[:], ps_ms[:], AF.Ln, scale=-1.0,
                             bias=kb.epses[:, 0:1]).then_inc(s_rld, 1)
                s.wait_ge(s_rld, 2 * T + 1)
                s.activation(R1sb[:], R1sb[:], AF.Exp,
                             scale=-0.5).then_inc(s_rld, 1)

            sq_op(0)
            rstd_op(0)
            for tr in range(NTR):
                for kc in range(NPAIR):
                    sz = PAIR_SZ[kc]
                    gk = tr * NPAIR + kc
                    if kc == 10 and tr < NTR - 1:
                        sq_op(tr + 1)
                    s.wait_ge(s_h3m, gk + 1)
                    if gk >= 2:
                        s.wait_ge(s_h4, gk - 1)
                    pg = ps_g if kc % 2 == 0 else ps_g2
                    s.activation(gel[0:sz, (kc % 2) * 512 : (kc % 2) * 512 + 512],
                                 pg[0:sz, :], AF.Gelu,
                                 bias=b1c[0:sz, 2 * kc + 1 : 2 * kc + 2]).then_inc(s_gel, 1)
                if tr < NTR - 1:
                    rstd_op(tr + 1)
        block.scalar(s)

        def v(v):
            def stats_vec(T):
                v.wait_ge(s_stm, 2 * (T + 1))
                if T >= 1:
                    v.wait_ge(s_r2, T)
                    v.wait_ge(s_z, NDC * T)
                ins = v.tensor_copy(musb[:], ps_mu[:])
                ins.then_inc(s_msv, 1)
                v.wait_ge(s_msv, 3 * T + 1)
                ins = v.tensor_tensor(ztmp[:], musb[:], musb[:], ALU.mult)
                ins.then_inc(s_msv, 1)
                v.wait_ge(s_msv, 3 * T + 2)
                ins = v.tensor_tensor(ps_ms[:], ps_ms[:], ztmp[:], ALU.add)
                ins.then_inc(s_msv, 1)

            def r2_op(T):
                v.wait_ge(s_rld, 2 * (T + 1))
                ins = v.tensor_tensor(R2sb[:], musb[:], R1sb[:], ALU.mult)
                ins.then_inc(s_r2, 1)

            def z_op(T):
                v.wait_ge(s_rld, 2 * (T + 1))
                v.wait_ge(s_r2, T + 1)
                if T >= 1:
                    v.wait_ge(s_h3m, NPAIR * T)
                for dc in range(NDC):
                    gz = T * NDC + dc
                    off = dc * STR_W + T * 512 + (1 if dc < 4 else 0)
                    if dc >= 4 and T >= 1:
                        v.scalar_tensor_tensor(
                            ztmp[:, 0:1], stash[:, dc - 4 : dc - 3],
                            lngc[:, dc : dc + 1], R1sb[:, 0:1], ALU.mult, ALU.mult)
                        i1 = v.scalar_tensor_tensor(
                            ztmp[:, 1:512], stream[:, off + 1 : off + 512],
                            lngc[:, dc : dc + 1], R1sb[:, 1:512], ALU.mult, ALU.mult)
                    else:
                        i1 = v.scalar_tensor_tensor(
                            ztmp[:], stream[:, off : off + 512], lngc[:, dc : dc + 1],
                            R1sb[:], ALU.mult, ALU.mult)
                    i2 = v.tensor_scalar(btmp[:], R2sb[:], lngc[:, dc : dc + 1],
                                         lnbc[:, dc : dc + 1], ALU.mult, ALU.add)
                    i2.then_inc(s_zt, 1)
                    v.wait_ge(s_zt, gz + 1)
                    ins = v.tensor_tensor(z_bf[:, dc * 512 : (dc + 1) * 512],
                                          ztmp[:], btmp[:], ALU.add)
                    ins.then_inc(s_z, 1)

            s_zt = kb.sem(tag + "ztm")
            stats_vec(0)
            r2_op(0)
            z_op(0)
            for tr in range(NTR):
                if tr < NTR - 1:
                    v.wait_ge(s_h2c, tr + 1)
                    for dc in range(4, 8):
                        ins = v.tensor_copy(
                            stash[:, dc - 4 : dc - 3],
                            stream[:, dc * STR_W + (tr + 1) * 512 : dc * STR_W + (tr + 1) * 512 + 1])
                    ins.then_inc(s_stb, 1)
                for kc in range(NPAIR):
                    sz = PAIR_SZ[kc]
                    gk = tr * NPAIR + kc
                    v.wait_ge(s_gel, gk + 1)
                    if tr >= 1 and kc == 0:
                        v.wait_ge(s_xm, NDC * tr)
                    pa = ps_a if kc % 2 == 0 else ps_a2
                    ins = v.scalar_tensor_tensor(
                        h4_buf[0:sz, kc * 512 : (kc + 1) * 512],
                        pa[0:sz, :], b1c[0:sz, 2 * kc : 2 * kc + 1],
                        gel[0:sz, (kc % 2) * 512 : (kc % 2) * 512 + 512],
                        ALU.add, ALU.mult)
                    ins.then_inc(s_h4, 1)
                if tr < NTR - 1:
                    stats_vec(tr + 1)
                    r2_op(tr + 1)
                    z_op(tr + 1)
                for dc in range(NDC):
                    gd = tr * NDC + dc
                    v.wait_ge(s_xm, gd + 1)
                    px = ps_bc1 if dc % 2 == 0 else ps_bc2
                    sl = stream[:, dc * STR_W + 1 + tr * 512 : dc * STR_W + 1 + (tr + 1) * 512]
                    ins = v.scalar_tensor_tensor(sl, px[:], b2c[:, dc : dc + 1], sl,
                                                 ALU.add, ALU.add)
                    ins.then_inc(s_xadd, 1)
        block.vector(v)

        return {
            "sync": [(s_ini, 64)],
            "gpsimd": [(s_h2c, NTR), (s_w1, 16 * NPAIR * NTR), (s_w2, 32 * NDC * NTR),
                       (s_in2, 16)],
            "tensor": [(s_xm, NDC * NTR), (s_stm, 2 * NTR)],
            "scalar": [(s_gel, NPAIR * NTR), (s_sq2, NTR), (s_rld, 2 * NTR)],
            "vector": [(s_xadd, NDC * NTR), (s_h4, NPAIR * NTR), (s_z, NDC * NTR),
                       (s_msv, 3 * NTR), (s_r2, NTR), (s_stb, NTR - 1)],
        }


def phase_FIN(kb, block, din, dr, stream, b):
    with ExitStack() as st:
        ptile = kb.sb(st, f"fin{b}p", [128, 2 * D], F32)
        otile = kb.sb(st, f"fin{b}o", [128, 2 * D], F32)
        idf = kb.sb(st, f"fin{b}i", [128, 128], F32)
        psF = [kb.ps(st, f"fin{b}ps{i}", [128, 512], F32) for i in range(4)]
        s_p = kb.sem(f"fin{b}pl")
        s_tr = kb.sem(f"fin{b}tr")
        s_ad = kb.sem(f"fin{b}ad")
        s_st = kb.sem(f"fin{b}st")
        s_idf = kb.sem(f"fin{b}id")

        def sy(sy):
            sy.dma_start(out=idf[:], in_=din["ident"]).then_inc(s_idf, 16)
            for tc in range(NT):
                if tc >= 2:
                    sy.wait_ge(s_ad, 2 * (tc - 1))
                sy.dma_start(
                    out=ptile[:, (tc % 2) * D : (tc % 2) * D + D],
                    in_=dr["pbuf"][b, tc * 128 : (tc + 1) * 128, :],
                ).then_inc(s_p, 16)
                sy.wait_ge(s_ad, 2 * (tc + 1))
                sy.dma_start(
                    out=dr["out"][b, tc * 128 : (tc + 1) * 128, :],
                    in_=otile[:, (tc % 2) * D : (tc % 2) * D + D],
                ).then_inc(s_st, 16)
            sy.wait_ge(s_st, 16 * NT)
        block.sync(sy)

        def t(tn):
            tn.wait_ge(s_idf, 16)
            for tc in range(NT):
                for half in range(2):
                    gi = 2 * tc + half
                    if gi >= 4:
                        tn.wait_ge(s_ad, gi - 3)
                    pt = psF[gi % 4]
                    for j in range(4):
                        dc = 4 * half + j
                        tr = tn.transpose(
                            pt[:, j * 128 : (j + 1) * 128],
                            stream[:, dc * STR_W + 1 + tc * 128 : dc * STR_W + 1 + (tc + 1) * 128],
                            idf[:],
                        )
                    tr.then_inc(s_tr, 1)
        block.tensor(t)

        def v(v):
            for tc in range(NT):
                v.wait_ge(s_p, 16 * (tc + 1))
                if tc >= 2:
                    v.wait_ge(s_st, 16 * (tc - 1))
                for half in range(2):
                    gi = 2 * tc + half
                    v.wait_ge(s_tr, gi + 1)
                    v.tensor_tensor(
                        otile[:, (tc % 2) * D + half * 512 : (tc % 2) * D + half * 512 + 512],
                        psF[gi % 4][:],
                        ptile[:, (tc % 2) * D + half * 512 : (tc % 2) * D + half * 512 + 512],
                        ALU.add,
                    ).then_inc(s_ad, 1)
        block.vector(v)

        return {
            "sync": [(s_st, 16 * NT), (s_p, 16 * NT), (s_idf, 16)],
            "tensor": [(s_tr, 2 * NT)],
            "vector": [(s_ad, 2 * NT)],
        }



def kernel(**inputs):
    hp = host_prep(inputs)
    x = np.ascontiguousarray(np.asarray(inputs["L0_attn_in_features"], np.float32))
    kb, dr = build(dev=False)
    in_maps = []
    for c in range(NCORES):
        m = {"xsh": np.ascontiguousarray(x[NB * c : NB * (c + 1)])}
        m.update(hp)
        in_maps.append(m)
    res = run_bass_kernel_spmd(kb.nc, in_maps, core_ids=list(range(NCORES)))
    return np.concatenate([r["out"] for r in res.results], axis=0)


# version 25
# speedup vs baseline: 1.0027x; 1.0027x over previous
import numpy as np
import ml_dtypes
from contextlib import ExitStack

import concourse.bass as bass
import concourse.mybir as mybir
from concourse.bass_utils import run_bass_kernel_spmd

F32 = mybir.dt.float32
BF16 = mybir.dt.bfloat16
AF = mybir.ActivationFunctionType
ALU = mybir.AluOpType
AX = mybir.AxisListType
bfdt = ml_dtypes.bfloat16

NCORES = 8
NB = 2
T = 4096
D = 1024
NT = T // 128
NDC = D // 128
HID = 2730
H2 = 5460
HEADS = 8
DH = 64
INNER = 512
WIN = 512
NEG = -1.0e30
NPAIR = 22
PAIR_SZ = [128] * 21 + [42]
CH_OFF = np.cumsum([0] + [2 * s for s in PAIR_SZ]).tolist()
STR_W = T + 1

ENGS = ("tensor", "vector", "scalar", "gpsimd", "sync")


def _rotary_tables():
    half = np.arange(0, DH, 2, dtype=np.float64)
    inv_freq = 1.0 / 10000.0 ** (half / DH)
    t = np.arange(T, dtype=np.float64)
    freqs = t[:, None] * inv_freq[None, :]
    freqs = np.concatenate([freqs, freqs], axis=-1)
    base = (half + 0.4 * DH) / (1.4 * DH)
    power = (t - T // 2) / float(WIN // 2)
    scale = base[None, :] ** power[:, None]
    scale = np.concatenate([scale, scale], axis=-1)
    cos, sin = np.cos(freqs), np.sin(freqs)
    sgn = np.concatenate([-np.ones(DH // 2), np.ones(DH // 2)])
    csq = (8.0 * cos * scale).astype(np.float32)
    ssq = (8.0 * sin * scale * sgn).astype(np.float32)
    csk = (cos / scale).astype(np.float32)
    ssk = (sin / scale * sgn).astype(np.float32)
    return csq, ssq, csk, ssk


def host_prep(inputs):
    c = np.ascontiguousarray
    d = {}
    d["WqT"] = c(np.asarray(inputs["L0_attn_Wq"]).T)
    d["WkT"] = c(np.asarray(inputs["L0_attn_Wk"]).T)
    d["Wv"] = c(np.asarray(inputs["L0_attn_Wv"]))
    d["WoT"] = c(np.asarray(inputs["L0_attn_Wo"]).T)
    d["wqkv"] = c(np.asarray(inputs["L0_ffn_wqkv"]))
    d["wo"] = c(np.asarray(inputs["L0_ffn_wo"]))
    w1 = np.asarray(inputs["L0_ffn_w1"])
    b1 = np.asarray(inputs["L0_ffn_b1"])
    perm = []
    for ci in range(NPAIR):
        sz = PAIR_SZ[ci]
        perm.extend(range(128 * ci, 128 * ci + sz))
        perm.extend(range(HID + 128 * ci, HID + 128 * ci + sz))
    perm = np.asarray(perm)
    d["w1p"] = c(w1[:, :, perm])
    b1p = b1[:, perm]
    b1_pk = np.zeros((2, 128, 2 * NPAIR), np.float32)
    for ci in range(NPAIR):
        sz = PAIR_SZ[ci]
        off = CH_OFF[ci]
        b1_pk[:, :sz, 2 * ci] = b1p[:, off : off + sz]
        b1_pk[:, :sz, 2 * ci + 1] = b1p[:, off + sz : off + 2 * sz]
    d["b1_pk"] = b1_pk
    d["w2"] = c(np.asarray(inputs["L0_ffn_w2"]))
    d["lng_pk"] = c(np.asarray(inputs["L0_ffn_ln_g"]).reshape(2, 8, 128).transpose(0, 2, 1))
    d["lnb_pk"] = c(np.asarray(inputs["L0_ffn_ln_b"]).reshape(2, 8, 128).transpose(0, 2, 1))
    d["b2_pk"] = c(np.asarray(inputs["L0_ffn_b2"]).reshape(2, 8, 128).transpose(0, 2, 1))
    qs_t = np.tile(np.asarray(inputs["L0_ffn_q_scale"], np.float32), (1, HEADS))
    ks_t = np.tile(np.asarray(inputs["L0_ffn_k_scale"], np.float32), (1, HEADS))
    d["qscale_bc"] = c(np.broadcast_to(qs_t[:, None, :], (2, 128, INNER)))
    d["kscale_bc"] = c(np.broadcast_to(ks_t[:, None, :], (2, 128, INNER)))
    csq, ssq, csk, ssk = _rotary_tables()
    d["csq"], d["ssq"], d["csk"], d["ssk"] = csq, ssq, csk, ssk
    d["ident"] = np.eye(128, dtype=np.float32)
    d["ones_row"] = np.ones((1, 128), np.float32)
    d["ones_col"] = np.ones((128, 1), np.float32)
    ii, jj = np.indices((128, 128))
    d["b_diag"] = np.where(jj <= ii, 0.0, NEG).astype(np.float32)
    d["b_old"] = np.where(jj >= ii, 0.0, NEG).astype(np.float32)
    d["epses"] = np.tile(np.asarray([[1e-5, 1e-24]], np.float32), (128, 1))
    d["ones_nd"] = np.full((128, 128), -1.0 / D, np.float32)
    d["ones_pd"] = np.full((128, 128), 1.0 / D, np.float32)
    return d


class KB:

    def __init__(self, dev=False):
        self.dev = dev
        self.nc = bass.Bass("TRN2", target_bir_lowering=False, debug=False,
                            detect_race_conditions=False)
        self.stack = ExitStack()
        self.sems = {}
        self.all_sems = {}
        self.bar_n = 0
        self.sbuf_cols = 0

    def sem(self, name):
        if name not in self.sems:
            self.sems[name] = self.stack.enter_context(self.nc.semaphore(name))
            self.all_sems[name] = self.sems[name]
        return self.sems[name]

    def barrier(self, block, flush=None):
        h = self.sem("bar")
        self.bar_n += 5
        tgt = self.bar_n
        flush = flush or {}
        for name in ENGS:
            def body(e, name=name):
                for sm, val in flush.get(name, ()):
                    e.wait_ge(sm, val)
                e.nop().then_inc(h, 1)
                e.wait_ge(h, tgt)
            getattr(block, name)(body)

    def recycle(self, block, prefixes, force=False):
        if not force and len(self.sems) < 72:
            return
        keep = ("bar", "cld")
        names = [n for n in self.sems
                 if n not in keep and not n.startswith("tap")]
        if not names:
            return
        handles = [self.sems.pop(n) for n in names]

        def g(g):
            self.nc.clear_and_free_semaphores(handles)
        block.gpsimd(g)
        self.barrier(block)

    def sb(self, st, name, shape, dtype):
        return st.enter_context(self.nc.sbuf_tensor(name, shape, dtype))

    def ps(self, st, name, shape, dtype=F32):
        return st.enter_context(self.nc.psum_tensor(name, shape, dtype))



def phase_W(kb, block, din, dr):
    nc = kb.nc
    with ExitStack() as st:
        wv_sb = kb.sb(st, "w_wv", [128, NDC * D], BF16)
        wot_sb = kb.sb(st, "w_wot", [128, NDC * D], BF16)
        mvo_st = kb.sb(st, "w_mvo", [128, 2 * D], BF16)
        psA = kb.ps(st, "w_psA", [128, 512], F32)
        psB = kb.ps(st, "w_psB", [128, 512], F32)
        s_ld = kb.sem("w_ld")
        s_mm = kb.sem("w_mm")
        s_ev = kb.sem("w_ev")
        s_st = kb.sem("w_st")

        def g(g):
            for l in range(2):
                g.dma_start(out=dr["w1p_bf"][l], in_=din["w1p"][l]).then_inc(kb.sem("w_cast"), 16)
                g.dma_start(out=dr["w2_bf"][l], in_=din["w2"][l]).then_inc(kb.sem("w_cast"), 16)
            g.dma_start(
                out=wv_sb[:].rearrange("p (k m) -> p k m", k=NDC),
                in_=din["Wv"].rearrange("(k p) m -> p k m", p=128),
            ).then_inc(s_ld, 16)
            g.dma_start(
                out=wot_sb[:].rearrange("p (k m) -> p k m", k=NDC),
                in_=din["WoT"].rearrange("(k p) m -> p k m", p=128),
            ).then_inc(s_ld, 16)
        block.gpsimd(g)

        def t(tn):
            tn.wait_ge(s_ld, 32)
            n = 0
            for mc in range(NDC):
                for nr in range(2):
                    n += 1
                    if n > 2:
                        tn.wait_ge(s_ev, n - 2)
                    pt = psA if nr == 0 else psB
                    for kc in range(NDC):
                        mm = tn.matmul(
                            pt[:],
                            wv_sb[:, kc * D + mc * 128 : kc * D + (mc + 1) * 128],
                            wot_sb[:, kc * D + nr * 512 : kc * D + nr * 512 + 512],
                            start=(kc == 0), stop=(kc == NDC - 1),
                        )
                    mm.then_inc(s_mm, 1)
        block.tensor(t)

        def v(v):
            n = 0
            for mc in range(NDC):
                if mc >= 2:
                    v.wait_ge(s_st, 16 * (mc - 1))
                for nr in range(2):
                    n += 1
                    v.wait_ge(s_mm, n)
                    v.tensor_copy(
                        mvo_st[:, (mc % 2) * D + nr * 512 : (mc % 2) * D + nr * 512 + 512],
                        (psA if nr == 0 else psB)[:],
                    ).then_inc(s_ev, 1)
        block.vector(v)

        def sy(sy):
            for mc in range(NDC):
                sy.wait_ge(s_ev, 2 * (mc + 1))
                sy.dma_start(
                    out=dr["mvod"][mc * 128 : (mc + 1) * 128, :],
                    in_=mvo_st[:, (mc % 2) * D : (mc % 2) * D + D],
                ).then_inc(s_st, 16)
            sy.wait_ge(s_st, 16 * NDC)
        block.sync(sy)
        return {
            "gpsimd": [(kb.sem("w_cast"), 64), (s_ld, 32)],
            "tensor": [(s_mm, 16)],
            "vector": [(s_ev, 16)],
            "sync": [(s_st, 16 * NDC)],
        }


def emit_ln_tiles(kb, block, st, tag, b, din, dst_writer, want_bf=True):
    nc = kb.nc
    xt = kb.sb(st, tag + "_xt", [128, 2 * D], F32)
    sq = kb.sb(st, tag + "_sq", [128, D], F32)
    xn = kb.sb(st, tag + "_xn", [128, 2 * D], F32)
    rows = kb.sb(st, tag + "_rows", [128, 8], F32)
    idf = kb.sb(st, tag + "_idf", [128, 128], F32)
    psT = [kb.ps(st, tag + f"_psT{i}", [128, 512], F32) for i in range(4)]

    s_x = kb.sem(tag + "_x")
    s_sq = kb.sem(tag + "_sq")
    s_v = kb.sem(tag + "_v")
    s_r = kb.sem(tag + "_r")
    s_xn = kb.sem(tag + "_xn")
    s_tr = kb.sem(tag + "_tr")
    s_ev = kb.sem(tag + "_ev")
    s_idf = kb.sem(tag + "_id")

    def sy(sy):
        sy.dma_start(out=idf[:], in_=din["ident"]).then_inc(s_idf, 16)
        for tc in range(NT):
            if tc >= 2:
                sy.wait_ge(s_xn, tc - 1)
            sy.dma_start(
                out=xt[:, (tc % 2) * D : (tc % 2) * D + D],
                in_=din["xsh"][b, tc * 128 : (tc + 1) * 128, :],
            ).then_inc(s_x, 16)
    block.sync(sy)

    def s(s):
        for tc in range(NT + 1):
            if tc < NT:
                s.wait_ge(s_x, 16 * (tc + 1))
                if tc >= 1:
                    s.wait_ge(s_v, 5 * tc)
                    s.wait_ge(s_xn, tc)
                s.activation(sq[:], xt[:, (tc % 2) * D : (tc % 2) * D + D],
                             AF.Square).then_inc(s_sq, 1)
                s.wait_ge(s_v, 5 * tc + 5)
                s.activation(rows[:, 5:6], rows[:, 4:5], AF.Ln, bias=kb.epses[:, 0:1]
                             ).then_inc(s_r, 1)
                s.wait_ge(s_r, 2 * tc + 1)
                s.activation(rows[:, 6:7], rows[:, 5:6], AF.Exp, scale=-0.5
                             ).then_inc(s_r, 1)
            if tc >= 1:
                for half in range(2):
                    s.wait_ge(s_tr, 2 * (tc - 1) + half + 1)
                    ins = dst_writer(s, tc - 1, half, psT[(2 * (tc - 1) + half) % 4])
                    ins.then_inc(s_ev, 1)
    block.scalar(s)

    def v(v):
        for tc in range(NT):
            xtv = xt[:, (tc % 2) * D : (tc % 2) * D + D]
            v.wait_ge(s_x, 16 * (tc + 1))
            if tc >= 1:
                v.wait_ge(s_xn, tc)
            v.reduce_sum(rows[:, 2:3], xtv, axis=AX.X, negate=True).then_inc(s_v, 1)
            v.wait_ge(s_v, 5 * tc + 1)
            v.tensor_scalar(rows[:, 2:3], rows[:, 2:3], 1.0 / D, None, ALU.mult
                            ).then_inc(s_v, 1)
            v.wait_ge(s_sq, tc + 1)
            v.reduce_sum(rows[:, 1:2], sq[:], axis=AX.X).then_inc(s_v, 1)
            v.wait_ge(s_v, 5 * tc + 3)
            v.tensor_tensor(rows[:, 3:4], rows[:, 2:3], rows[:, 2:3], ALU.mult
                            ).then_inc(s_v, 1)
            v.wait_ge(s_v, 5 * tc + 4)
            if tc >= 1:
                v.wait_ge(s_r, 2 * tc - 1)
            v.scalar_tensor_tensor(
                rows[:, 4:5], rows[:, 1:2], 1.0 / D, rows[:, 3:4],
                ALU.mult, ALU.subtract,
            ).then_inc(s_v, 1)
            v.wait_ge(s_r, 2 * (tc + 1))
            if tc >= 2:
                v.wait_ge(s_tr, 2 * (tc - 1))
            v.tensor_scalar(
                xn[:, (tc % 2) * D : (tc % 2) * D + D], xtv,
                rows[:, 2:3], rows[:, 6:7], ALU.add, ALU.mult,
            ).then_inc(s_xn, 1)
    block.vector(v)

    def t(tn):
        tn.wait_ge(s_idf, 16)
        for tc in range(NT):
            tn.wait_ge(s_xn, tc + 1)
            for half in range(2):
                if 2 * tc + half >= 4:
                    tn.wait_ge(s_ev, 2 * tc + half - 3)
                pt = psT[(2 * tc + half) % 4]
                for j in range(4):
                    dc = 4 * half + j
                    tr = tn.transpose(
                        pt[:, j * 128 : (j + 1) * 128],
                        xn[:, (tc % 2) * D + dc * 128 : (tc % 2) * D + (dc + 1) * 128],
                        idf[:],
                    )
                tr.then_inc(s_tr, 1)
    block.tensor(t)

    return {
        "sync": [(s_x, 16 * NT), (s_idf, 16)],
        "scalar": [(s_sq, NT), (s_r, 2 * NT), (s_ev, 2 * NT)],
        "vector": [(s_v, 5 * NT), (s_xn, NT)],
        "tensor": [(s_tr, 2 * NT)],
    }


_IDF = {}


def din_identf_ap(kb):
    return _IDF[id(kb)]


def phase_P0(kb, block, din, dr, xnT_bf, b):
    with ExitStack() as st:
        def writer(s, tc, half, pt):
            dst = xnT_bf[:].rearrange("p (c t) -> p c t", c=NDC)[
                :, 4 * half : 4 * half + 4, tc * 128 : (tc + 1) * 128
            ]
            return s.copy(dst, pt[:].rearrange("p (c t) -> p c t", c=4))
        flush = emit_ln_tiles(kb, block, st, f"p0{b}", b, din, writer)
    return flush


def phase_A1(kb, block, din, dr, xnT_bf, b):
    nc = kb.nc
    with ExitStack() as st:
        w_sb = kb.sb(st, f"a1w{b}", [128, 2 * NDC * D], BF16)
        mvo_sb = kb.sb(st, f"a1m{b}", [128, NDC * D], BF16)
        stg = kb.sb(st, f"a1s{b}", [128, 4 * 512], BF16)
        psA = kb.ps(st, f"a1pA{b}", [128, 512], F32)
        psB = kb.ps(st, f"a1pB{b}", [128, 512], F32)
        s_w = kb.sem(f"a1w{b}")
        s_mm = kb.sem(f"a1mm{b}")
        s_ev = kb.sem(f"a1ev{b}")
        s_st = kb.sem(f"a1st{b}")

        def g(g):
            for i, nm in enumerate(("WkT", "WqT")):
                g.dma_start(
                    out=w_sb[:, i * NDC * D : (i + 1) * NDC * D].rearrange(
                        "p (k m) -> p k m", k=NDC),
                    in_=din[nm].rearrange("(k p) m -> p k m", p=128),
                ).then_inc(s_w, 16)
            g.dma_start(
                out=mvo_sb[:].rearrange("p (k m) -> p k m", k=NDC),
                in_=dr["mvod"].rearrange("(k p) m -> p k m", p=128),
            ).then_inc(s_w, 16)
        block.gpsimd(g)

        jobs = []
        for i in range(2):
            for mc in range(NDC):
                for tr in range(8):
                    jobs.append(("kq", i, mc, tr))
        for tc in range(NT):
            for nh in range(2):
                jobs.append(("v", 0, tc, nh))

        def t(tn):
            tn.wait_ge(s_w, 48)
            for n, (kind, i, a_, b_) in enumerate(jobs, 1):
                if n > 2:
                    tn.wait_ge(s_ev, n - 2)
                pt = psA if n % 2 else psB
                if kind == "kq":
                    mc, tr = a_, b_
                    for kc in range(NDC):
                        mm = tn.matmul(
                            pt[:],
                            w_sb[:, (i * NDC + kc) * D + mc * 128 : (i * NDC + kc) * D + (mc + 1) * 128],
                            xnT_bf[:, kc * T + tr * 512 : kc * T + tr * 512 + 512],
                            start=(kc == 0), stop=(kc == NDC - 1),
                        )
                else:
                    tc, nh = a_, b_
                    for kc in range(NDC):
                        mm = tn.matmul(
                            pt[:],
                            xnT_bf[:, kc * T + tc * 128 : kc * T + (tc + 1) * 128],
                            mvo_sb[:, kc * D + nh * 512 : kc * D + nh * 512 + 512],
                            start=(kc == 0), stop=(kc == NDC - 1),
                        )
                mm.then_inc(s_mm, 1)
        block.tensor(t)

        def s(s):
            for n, (kind, i, a_, b_) in enumerate(jobs, 1):
                s.wait_ge(s_mm, n)
                if n > 4:
                    s.wait_ge(s_st, 16 * (n - 4))
                pt = psA if n % 2 else psB
                dst = stg[:, ((n - 1) % 4) * 512 : ((n - 1) % 4) * 512 + 512]
                if kind == "kq" and i == 1:
                    ins = s.activation(dst, pt[:], AF.Copy, scale=1.0 / 32.0)
                else:
                    ins = s.copy(dst, pt[:])
                ins.then_inc(s_ev, 1)
        block.scalar(s)

        def sy(sy):
            for n, (kind, i, a_, b_) in enumerate(jobs, 1):
                sy.wait_ge(s_ev, n)
                src = stg[:, ((n - 1) % 4) * 512 : ((n - 1) % 4) * 512 + 512]
                if kind == "kq":
                    mc, tr = a_, b_
                    dstd = (dr["KTd"] if i == 0 else dr["QTd"])[b, mc, :, tr * 512 : tr * 512 + 512]
                else:
                    tc, nh = a_, b_
                    dstd = dr["Vpd"][b, tc * 128 : (tc + 1) * 128, nh * 512 : nh * 512 + 512]
                sy.dma_start(out=dstd, in_=src).then_inc(s_st, 16)
            sy.wait_ge(s_st, 16 * len(jobs))
        block.sync(sy)
        return {
            "gpsimd": [(s_w, 48)],
            "tensor": [(s_mm, len(jobs))],
            "scalar": [(s_ev, len(jobs))],
            "sync": [(s_st, 16 * len(jobs))],
        }



def build(dev=False, phases=("W", "P0", "A1", "A2", "FI", "F0", "F1", "FIN")):
    kb = KB(dev=dev)
    nc = kb.nc

    din = {}
    for nm, shp in [
        ("xsh", [NB, T, D]), ("WqT", [D, D]), ("WkT", [D, D]), ("Wv", [D, D]),
        ("WoT", [D, D]), ("wqkv", [2, D, 3 * INNER]), ("wo", [2, INNER, D]),
        ("w1p", [2, D, H2]), ("b1_pk", [2, 128, 2 * NPAIR]), ("w2", [2, HID, D]),
        ("lng_pk", [2, 128, 8]), ("lnb_pk", [2, 128, 8]), ("b2_pk", [2, 128, 8]),
        ("qscale_bc", [2, 128, INNER]), ("kscale_bc", [2, 128, INNER]),
        ("csq", [T, DH]), ("ssq", [T, DH]), ("csk", [T, DH]), ("ssk", [T, DH]),
        ("ident", [128, 128]), ("ones_row", [1, 128]), ("ones_col", [128, 1]),
        ("b_diag", [128, 128]), ("b_old", [128, 128]), ("epses", [128, 2]),
        ("ones_nd", [128, 128]), ("ones_pd", [128, 128]),
    ]:
        din[nm] = nc.dram_tensor(nm, shp, F32, kind="ExternalInput").ap()

    out = nc.dram_tensor("out", [NB, T, D], F32, kind="ExternalOutput").ap()

    def scratch(name, shape, dtype):
        kind = "ExternalOutput" if dev else "Internal"
        return nc.dram_tensor(name, shape, dtype, kind=kind).ap()

    dr = {
        "w1p_bf": scratch("w1p_bf", [2, D, H2], BF16),
        "w2_bf": scratch("w2_bf", [2, HID, D], BF16),
        "mvod": scratch("mvod", [D, D], BF16),
        "QTd": scratch("QTd", [NB, NDC, 128, T], BF16),
        "KTd": scratch("KTd", [NB, NDC, 128, T], BF16),
        "Vpd": scratch("Vpd", [NB, T, D], BF16),
        "pbuf": scratch("pbuf", [NB, T, D], F32),
        "tapS": scratch("tapS", [128, T], F32),
        "tapP": scratch("tapP", [128, T], BF16),
        "tapC": scratch("tapC", [128, 4], F32),
        "out": out,
    }

    with kb.stack:
        block = kb.stack.enter_context(nc.Block())
        epses = kb.sb(kb.stack, "epses_s", [128, 2], F32)
        kb.epses = epses
        s_c = kb.sem("cld")

        def g0(g):
            g.dma_start(out=epses[:], in_=din["epses"]).then_inc(s_c, 16)
            g.wait_ge(s_c, 16)
        block.gpsimd(g0)
        kb.barrier(block)

        if "W" in phases:
            fl = phase_W(kb, block, din, dr)
            kb.barrier(block, fl)
            kb.recycle(block, ["w_"])

        for b in range(NB):
            with ExitStack() as stb:
                need_xnT = ("P0" in phases) or ("A1" in phases)
                if need_xnT:
                    xnT_bf = kb.sb(stb, f"xnT{b}", [128, NDC * T], BF16)
                if "P0" in phases:
                    fl = phase_P0(kb, block, din, dr, xnT_bf, b)
                    kb.barrier(block, fl)
                    kb.recycle(block, [f"p0{b}"])
                if "A1" in phases:
                    fl = phase_A1(kb, block, din, dr, xnT_bf, b)
                    kb.barrier(block, fl)
                    kb.recycle(block, ["a1"])
            if "A2" in phases:
                fl = phase_A2(kb, block, din, dr, b)
                kb.barrier(block, fl)
                kb.recycle(block, ["a2"])
            if any(p in phases for p in ("FI", "F0", "F1", "FIN", "F0a", "F1a")):
                with ExitStack() as stf:
                    stream = kb.sb(stf, f"stream{b}", [128, NDC * STR_W], F32)

                    def tap(name, b=b, stream=stream):
                        if not dev or b != 0:
                            return
                        tsr = nc.dram_tensor(f"tap_{name}", [128, NDC * STR_W], F32,
                                             kind="ExternalOutput").ap()
                        s_t = kb.sem(f"tap{name}")

                        def gt(g):
                            g.dma_start(out=tsr, in_=stream[:]).then_inc(s_t, 16)
                        block.gpsimd(gt)
                        kb.barrier(block, {"gpsimd": [(s_t, 16)]})

                    if "FI" in phases:
                        fl = phase_FI(kb, block, din, dr, stream, b)
                        kb.barrier(block, fl)
                        kb.recycle(block, [f"fi{b}"])
                        tap("fi")
                    for l in range(2):
                        if f"F{l}" in phases or f"F{l}a" in phases:
                            fl = phase_Fa(kb, block, din, dr, stream, b, l)
                            kb.barrier(block, fl)
                            kb.recycle(block, [f"fa{b}{l}"])
                            tap(f"f{l}a")
                        if f"F{l}" in phases:
                            fl = phase_Fb(kb, block, din, dr, stream, b, l)
                            kb.barrier(block, fl)
                            kb.recycle(block, [f"fb{b}{l}"])
                            tap(f"f{l}b")
                    if "FIN" in phases:
                        fl = phase_FIN(kb, block, din, dr, stream, b)
                        kb.barrier(block, fl)
                        kb.recycle(block, [f"fin{b}", "tap"])


    nc.finalize()
    return kb, dr



def phase_A2(kb, block, din, dr, b):
    nc = kb.nc
    with ExitStack() as st:
        KT_sb = kb.sb(st, f"a2kt{b}", [128, NDC * T], BF16)
        Vp_sb = kb.sb(st, f"a2vp{b}", [128, NT * D], BF16)
        qt = kb.sb(st, f"a2qt{b}", [128, 2 * NDC * 128], BF16)
        P_bf = kb.sb(st, f"a2p{b}", [128, 2 * T], BF16)
        PT_sb = kb.sb(st, f"a2pt{b}", [128, 2 * T], BF16)
        x0t = kb.sb(st, f"a2x0{b}", [128, 2 * D], F32)
        p_sb = kb.sb(st, f"a2pb{b}", [128, 2 * D], F32)
        bdg = kb.sb(st, f"a2bd{b}", [128, 128], F32)
        idb = kb.sb(st, f"a2ib{b}", [128, 128], BF16)
        cols = kb.sb(st, f"a2c{b}", [128, 8], F32)
        cols_rs = kb.sb(st, f"a2cr{b}", [128, 16], F32)
        ps_s = [kb.ps(st, f"a2ps{b}{i}", [128, 512], F32) for i in range(2)]
        ps_t = [kb.ps(st, f"a2pt{b}{i}", [128, 512], BF16) for i in range(2)]
        ps_o = [kb.ps(st, f"a2po{b}{i}", [128, 512], F32) for i in range(4)]

        s_kv = kb.sem(f"a2kv{b}")
        s_qt = kb.sem(f"a2sqt{b}")
        s_x0 = kb.sem(f"a2sx0{b}")
        s_smm = kb.sem(f"a2smm{b}")
        s_dg = kb.sem(f"a2dg{b}")
        s_rs = kb.sem(f"a2rs{b}")
        s_pe = kb.sem(f"a2pe{b}")
        s_rc = kb.sem(f"a2rc{b}")
        s_ptr = kb.sem(f"a2ptr{b}")
        s_ptev = kb.sem(f"a2pte{b}")
        s_pv = kb.sem(f"a2pv{b}")
        s_pev = kb.sem(f"a2pev{b}")
        s_pst = kb.sem(f"a2pst{b}")

        def R(qc):
            return qc // 4 + 1

        def G(qc):
            return (qc + 4) // 4

        range_meta = []
        smm_at = {-1: 0}
        sev_at = {-1: 0}
        grp_at = {-1: 0, -2: 0}
        tot = 0
        gtot = 0
        for qc in range(NT):
            for r in range(R(qc)):
                range_meta.append((qc, r == R(qc) - 1))
            tot += R(qc)
            smm_at[qc] = tot
            sev_at[qc] = tot
            gtot += G(qc)
            grp_at[qc] = gtot

        def g(g):
            g.dma_start(
                out=KT_sb[:].rearrange("p (c t) -> p c t", c=NDC),
                in_=dr["KTd"][b].transpose([1, 0, 2]),
            ).then_inc(s_kv, 16)
            for hh in range(2):
                g.dma_start(
                    out=Vp_sb[:].rearrange("p (c d) -> p c d", c=NT)[:, hh * 16 : (hh + 1) * 16],
                    in_=dr["Vpd"][b, hh * 2048 : (hh + 1) * 2048, :].rearrange(
                        "(c p) d -> p c d", p=128),
                ).then_inc(s_kv, 16)
            g.dma_start(out=bdg[:], in_=din["b_diag"]).then_inc(s_kv, 16)
            g.dma_start(out=idb[:], in_=din["ident"]).then_inc(s_kv, 16)
        block.gpsimd(g)

        def sy(sy):
            for qc in range(NT):
                if qc >= 2:
                    sy.wait_ge(s_pev, qc - 1)
                sy.dma_start(
                    out=x0t[:, (qc % 2) * D : (qc % 2) * D + D],
                    in_=din["xsh"][b, qc * 128 : (qc + 1) * 128, :],
                ).then_inc(s_x0, 16)
                if qc >= 1:
                    sy.wait_ge(s_pev, qc)
                    sy.dma_start(
                        out=dr["pbuf"][b, (qc - 1) * 128 : qc * 128, :],
                        in_=p_sb[:, ((qc - 1) % 2) * D : ((qc - 1) % 2) * D + D],
                    ).then_inc(s_pst, 16)
            sy.wait_ge(s_pev, NT)
            sy.dma_start(
                out=dr["pbuf"][b, (NT - 1) * 128 : NT * 128, :],
                in_=p_sb[:, ((NT - 1) % 2) * D : ((NT - 1) % 2) * D + D],
            ).then_inc(s_pst, 16)
            sy.wait_ge(s_pst, 16 * NT)
        block.sync(sy)

        def trpv(tn, qc):
            nk = qc + 1
            ngr = G(qc)
            tn.wait_ge(s_pe, smm_at[qc])
            for gi in range(ngr):
                gidx = grp_at[qc - 1] + gi + 1
                if gidx > 2:
                    tn.wait_ge(s_ptev, gidx - 2)
                for j in range(min(4, nk - 4 * gi)):
                    kc = 4 * gi + j
                    tr = tn.transpose(
                        ps_t[(gidx - 1) % 2][:, j * 128 : (j + 1) * 128],
                        P_bf[:, (qc % 2) * T + kc * 128 : (qc % 2) * T + (kc + 1) * 128],
                        idb[:],
                    )
                tr.then_inc(s_ptr, 1)
            if qc >= 2:
                tn.wait_ge(s_pev, qc - 1)
            for gi in range(ngr):
                gidx = grp_at[qc - 1] + gi + 1
                tn.wait_ge(s_ptev, gidx)
                for j in range(min(4, nk - 4 * gi)):
                    kc = 4 * gi + j
                    for nh in range(2):
                        mm = tn.matmul(
                            ps_o[(qc % 2) * 2 + nh][:],
                            PT_sb[:, (qc % 2) * T + kc * 128 : (qc % 2) * T + (kc + 1) * 128],
                            Vp_sb[:, kc * D + nh * 512 : kc * D + nh * 512 + 512],
                            start=(kc == 0), stop=(kc == nk - 1),
                        )
            mm.then_inc(s_pv, 1)

        def t(tn):
            tn.wait_ge(s_kv, 80)
            rtot = 0
            for qc in range(NT):
                nk = qc + 1
                tn.wait_ge(s_qt, 16 * (qc + 1))
                if qc >= 2:
                    tn.wait_ge(s_ptr, grp_at[qc - 2])
                for r in range(R(qc)):
                    rtot += 1
                    if rtot > 2:
                        tn.wait_ge(s_pe, rtot - 2)
                    w = min(512, nk * 128 - r * 512)
                    pt = ps_s[(rtot - 1) % 2]
                    for kc in range(NDC):
                        mm = tn.matmul(
                            pt[:, 0:w],
                            qt[:, (qc % 2) * 1024 + kc * 128 : (qc % 2) * 1024 + (kc + 1) * 128],
                            KT_sb[:, kc * T + r * 512 : kc * T + r * 512 + w],
                            start=(kc == 0), stop=(kc == NDC - 1),
                        )
                    mm.then_inc(s_smm, 1)
                if qc >= 1:
                    trpv(tn, qc - 1)
            trpv(tn, NT - 1)
        block.tensor(t)

        def s(s):
            s.dma_start(
                out=qt[:, (0 % 2) * 1024 : (0 % 2) * 1024 + 1024].rearrange(
                    "p (c t) -> p c t", c=NDC),
                in_=dr["QTd"][b][:, :, 0 * 128 : (0 + 1) * 128].transpose([1, 0, 2]),
            ).then_inc(s_qt, 16)
            s.dma_start(
                out=qt[:, (1 % 2) * 1024 : (1 % 2) * 1024 + 1024].rearrange(
                    "p (c t) -> p c t", c=NDC),
                in_=dr["QTd"][b][:, :, 1 * 128 : (1 + 1) * 128].transpose([1, 0, 2]),
            ).then_inc(s_qt, 16)
            rtot = 0
            for qc in range(NT):
                nk = qc + 1
                if qc >= 2:
                    s.wait_ge(s_ptr, grp_at[qc - 2])
                    s.wait_ge(s_rc, qc - 1)
                for r in range(R(qc)):
                    rtot += 1
                    s.wait_ge(s_smm, rtot)
                    if r == R(qc) - 1:
                        s.wait_ge(s_dg, qc + 1)
                    w = min(512, nk * 128 - r * 512)
                    pt = ps_s[(rtot - 1) % 2]
                    ins = s.activation(
                        P_bf[:, (qc % 2) * T + r * 512 : (qc % 2) * T + r * 512 + w],
                        pt[:, 0:w], AF.Exp,
                        accum_out=cols_rs[:, (qc % 2) * 8 + r : (qc % 2) * 8 + r + 1],
                    )
                    ins.then_inc(s_pe, 1)
                if qc + 2 < NT:
                    s.wait_ge(s_smm, smm_at[qc])
                    s.dma_start(
                        out=qt[:, ((qc + 2) % 2) * 1024 : ((qc + 2) % 2) * 1024 + 1024].rearrange(
                            "p (c t) -> p c t", c=NDC),
                        in_=dr["QTd"][b][:, :, (qc + 2) * 128 : ((qc + 2) + 1) * 128].transpose([1, 0, 2]),
                    ).then_inc(s_qt, 16)
        block.scalar(s)

        def v(v):
            for qc in range(NT):
                nk = qc + 1
                rtot = smm_at[qc]
                v.wait_ge(s_smm, rtot)
                lr = R(qc) - 1
                w = nk * 128 - lr * 512
                pt = ps_s[(rtot - 1) % 2]
                v.tensor_tensor(
                    pt[:, w - 128 : w],
                    pt[:, w - 128 : w], bdg[:], ALU.add,
                ).then_inc(s_dg, 1)
                if qc >= 1:
                    pq = qc - 1
                    for gi in range(G(pq)):
                        gidx = grp_at[pq - 1] + gi + 1
                        v.wait_ge(s_ptr, gidx)
                        if gidx > 2:
                            v.wait_ge(s_pv, pq)
                        wg = min(512, pq * 128 + 128 - gi * 512)
                        ins = v.tensor_copy(
                            PT_sb[:, (pq % 2) * T + gi * 512 : (pq % 2) * T + gi * 512 + wg],
                            ps_t[(gidx - 1) % 2][:, 0:wg],
                        )
                        ins.then_inc(s_ptev, 1)
                    v.wait_ge(s_pv, pq + 1)
                    v.wait_ge(s_x0, 16 * (pq + 1))
                    v.wait_ge(s_rc, pq + 1)
                    if pq >= 2:
                        v.wait_ge(s_pst, 16 * (pq - 1))
                    for nh in range(2):
                        ins = v.scalar_tensor_tensor(
                            p_sb[:, (pq % 2) * D + nh * 512 : (pq % 2) * D + nh * 512 + 512],
                            ps_o[(pq % 2) * 2 + nh][:], cols[:, 4 + pq % 2 : 5 + pq % 2],
                            x0t[:, (pq % 2) * D + nh * 512 : (pq % 2) * D + nh * 512 + 512],
                            ALU.mult, ALU.add,
                        )
                    ins.then_inc(s_pev, 1)
                v.wait_ge(s_pe, smm_at[qc])
                ins = v.reduce_sum(cols[:, 2 + qc % 2 : 3 + qc % 2],
                                   cols_rs[:, (qc % 2) * 8 : (qc % 2) * 8 + R(qc)],
                                   axis=AX.X)
                ins.then_inc(s_rs, 1)
                v.wait_ge(s_rs, qc + 1)
                v.reciprocal(cols[:, 4 + qc % 2 : 5 + qc % 2],
                             cols[:, 2 + qc % 2 : 3 + qc % 2]).then_inc(s_rc, 1)
            pq = NT - 1
            for gi in range(G(pq)):
                gidx = grp_at[pq - 1] + gi + 1
                v.wait_ge(s_ptr, gidx)
                if gidx > 2:
                    v.wait_ge(s_pv, pq)
                wg = min(512, pq * 128 + 128 - gi * 512)
                ins = v.tensor_copy(
                    PT_sb[:, (pq % 2) * T + gi * 512 : (pq % 2) * T + gi * 512 + wg],
                    ps_t[(gidx - 1) % 2][:, 0:wg],
                )
                ins.then_inc(s_ptev, 1)
            v.wait_ge(s_pv, pq + 1)
            v.wait_ge(s_x0, 16 * (pq + 1))
            v.wait_ge(s_rc, pq + 1)
            v.wait_ge(s_pst, 16 * (pq - 1))
            for nh in range(2):
                ins = v.scalar_tensor_tensor(
                    p_sb[:, (pq % 2) * D + nh * 512 : (pq % 2) * D + nh * 512 + 512],
                    ps_o[(pq % 2) * 2 + nh][:], cols[:, 4 + pq % 2 : 5 + pq % 2],
                    x0t[:, (pq % 2) * D + nh * 512 : (pq % 2) * D + nh * 512 + 512],
                    ALU.mult, ALU.add,
                )
            ins.then_inc(s_pev, 1)
        block.vector(v)

        return {
            "gpsimd": [(s_kv, 80)],
            "sync": [(s_pst, 16 * NT)],
            "tensor": [(s_smm, sum(R(q) for q in range(NT))), (s_pv, NT),
                       (s_ptr, grp_at[NT - 1])],
            "scalar": [(s_pe, sum(R(q) for q in range(NT))), (s_qt, 16 * NT)],
            "vector": [(s_pev, NT), (s_ptev, grp_at[NT - 1]), (s_rc, NT),
                       (s_rs, NT), (s_dg, NT)],
        }


def phase_FI(kb, block, din, dr, stream, b):
    with ExitStack() as st:
        s_z = kb.sem(f"fi{b}z")

        def vz(v):
            v.memset(
                stream[:].rearrange("p (c w) -> p c w", c=NDC)[:, :, 0:1], 0.0
            ).then_inc(s_z, 1)
        block.vector(vz)

        def writer(s, tc, half, pt):
            dst = stream[:].rearrange("p (c w) -> p c w", c=NDC)[
                :, 4 * half : 4 * half + 4, 1 + tc * 128 : 1 + (tc + 1) * 128
            ]
            return s.copy(dst, pt[:].rearrange("p (c t) -> p c t", c=4))
        flush = emit_ln_tiles(kb, block, st, f"fi{b}", b, din, writer)
        flush.setdefault("vector", []).append((s_z, 1))
    return flush


FA_NCH = [NT]
FA_DBG = set()


def phase_Fa(kb, block, din, dr, stream, b, l):
    tag = f"fa{b}{l}"
    NCH = FA_NCH[0]
    with ExitStack() as st:
        wqkv_sb = kb.sb(st, tag + "wq", [128, NDC * 3 * INNER], BF16)
        wo_sb = kb.sb(st, tag + "wo", [128, 4 * D], BF16)
        qsc_bc = kb.sb(st, tag + "qs", [128, INNER], BF16)
        ksc_bc = kb.sb(st, tag + "ks", [128, INNER], BF16)
        bmask = kb.sb(st, tag + "bm", [128, 256], F32)
        idb = kb.sb(st, tag + "ib", [128, 128], BF16)
        tbl = kb.sb(st, tag + "tb", [128, 2 * 4 * 64], F32)
        xsh = kb.sb(st, tag + "xs", [128, D], BF16)
        qkf = kb.sb(st, tag + "qk", [128, 2 * INNER], F32)
        nrm = kb.sb(st, tag + "nr", [128, 16], F32)
        rst = kb.sb(st, tag + "rs", [128, 16], F32)
        tmp1 = kb.sb(st, tag + "t1", [128, INNER], F32)
        sqsc = kb.sb(st, tag + "sq", [128, DH], F32)
        qrot = kb.sb(st, tag + "qr", [128, INNER], BF16)
        krot = kb.sb(st, tag + "kr", [128, INNER], BF16)
        qT = kb.sb(st, tag + "qt", [64, HEADS * 128], BF16)
        kring = kb.sb(st, tag + "kg", [64, HEADS * 5 * 128], BF16)
        vring = kb.sb(st, tag + "vg", [128, 5 * 512], BF16)
        Pbn = kb.sb(st, tag + "pb", [128, 3 * 640], BF16)
        PT_sb = kb.sb(st, tag + "pt", [128, 2 * 640], BF16)
        oT_buf = kb.sb(st, tag + "ob", [128, 4 * 512], BF16)
        ncol = kb.sb(st, tag + "nc", [128, 24], F32)
        scr5 = kb.sb(st, tag + "s5", [128, 256], F32)
        stash = kb.sb(st, tag + "sh", [128, 4], F32)

        ps_qkv = kb.ps(st, tag + "pq", [128, 512], F32)
        ps_sa = [kb.ps(st, tag + f"psa{i}", [128, 512], F32) for i in range(2)]
        ps_sbp = kb.ps(st, tag + "psb", [128, 256], F32)
        ps_tr = [kb.ps(st, tag + f"ptr{i}", [128, 1024], BF16) for i in range(2)]
        ps_oT = kb.ps(st, tag + "pot", [128, 512], F32)
        ps_wo = kb.ps(st, tag + "pwo", [128, 512], F32)

        s_ini = kb.sem(tag + "ini")
        s_in2 = kb.sem(tag + "in2")
        s_in3 = kb.sem(tag + "in3")
        s_tb = kb.sem(tag + "tbl")
        s_xc = kb.sem(tag + "xc")
        s_qkvm = kb.sem(tag + "qm")
        s_qev = kb.sem(tag + "qe")
        s_sqq = kb.sem(tag + "q2")
        s_rst = kb.sem(tag + "rt")
        s_rot = kb.sem(tag + "ro")
        s_qktr = kb.sem(tag + "qk2")
        s_trev = kb.sem(tag + "tv")
        s_smm = kb.sem(tag + "sm")
        s_msk = kb.sem(tag + "mk")
        s_peh = kb.sem(tag + "pe")
        s_rci = kb.sem(tag + "rc")
        s_pnb = kb.sem(tag + "pn")
        s_ptr = kb.sem(tag + "pr")
        s_ptev = kb.sem(tag + "pv")
        s_pvm = kb.sem(tag + "pm")
        s_oev = kb.sem(tag + "oe")
        s_wom = kb.sem(tag + "wm")
        s_wov = kb.sem(tag + "wv")
        s_sta = kb.sem(tag + "sa")
        s_spc = kb.sem(tag + "sp")
        s_vv = kb.sem(tag + "vv")

        def nsl(tc):
            return min(tc + 1, 5)

        def spans(tc, h):
            n = nsl(tc)
            segs = [(h * 640 + (tc % 5) * 128, 128, 0)]
            lo = tc - n + 1
            si = 1
            while si < n:
                s0 = (lo + si - 1) % 5
                run = min(n - si, 5 - s0)
                segs.append((h * 640 + s0 * 128, run * 128, si))
                si += run
            return segs

        def g(g):
            g.dma_start(
                out=wqkv_sb[:].rearrange("p (k m) -> p k m", k=NDC),
                in_=din["wqkv"][l].rearrange("(k p) m -> p k m", p=128),
            ).then_inc(s_ini, 16)
            g.dma_start(
                out=wo_sb[:].rearrange("p (k m) -> p k m", k=4),
                in_=din["wo"][l].rearrange("(k p) m -> p k m", p=128),
            ).then_inc(s_ini, 16)
            g.dma_start(out=qsc_bc[:], in_=din["qscale_bc"][l]).then_inc(s_in3, 16)
            g.dma_start(out=ksc_bc[:], in_=din["kscale_bc"][l]).then_inc(s_in3, 16)
            g.dma_start(out=idb[:], in_=din["ident"]).then_inc(s_in3, 16)
            for tc in range(NCH):
                if tc >= 1:
                    g.wait_ge(s_qkvm, 3 * tc)
                bound = tc % 4 == 0 and tc >= 4
                if bound:
                    g.wait_ge(s_sta, tc // 4)
                xshv = xsh[:].rearrange("p (d x) -> p d x", d=NDC)
                stv = stream[:].rearrange("p (d w) -> p d w", d=NDC)
                g.tensor_copy(xshv[:, 0:4, :],
                              stv[:, 0:4, tc * 128 + 1 : tc * 128 + 129])
                if bound:
                    g.tensor_copy(xshv[:, 4:8, 0:1],
                                  stash[:].rearrange("p (d x) -> p d x", d=4))
                    ins = g.tensor_copy(
                        xshv[:, 4:8, 1:128],
                        stv[:, 4:8, tc * 128 + 1 : tc * 128 + 128])
                else:
                    ins = g.tensor_copy(
                        xshv[:, 4:8, :],
                        stv[:, 4:8, tc * 128 : tc * 128 + 128])
                ins.then_inc(s_xc, 1)
        block.gpsimd(g)

        def sy(sy):
            sy.dma_start(out=bmask[:, 0:128], in_=din["b_diag"]).then_inc(s_in2, 16)
            sy.dma_start(out=bmask[:, 128:256], in_=din["b_old"]).then_inc(s_in2, 16)
            for gi in range(NCH):
                if gi >= 2:
                    sy.wait_ge(s_rot, 2 * (gi - 1))
                for ti, nmt in enumerate(("csq", "ssq", "csk", "ssk")):
                    sy.dma_start(
                        out=tbl[:, (gi % 2) * 256 + ti * 64 : (gi % 2) * 256 + (ti + 1) * 64],
                        in_=din[nmt][gi * 128 : (gi + 1) * 128, :],
                    ).then_inc(s_tb, 16)
        block.sync(sy)

        def t(tn):
            tn.wait_ge(s_ini, 32)
            tn.wait_ge(s_in3, 48)
            pend_wo = []
            pend_pp = []

            def emit_wo(k):
                for _ in range(k):
                    if pend_wo:
                        pend_wo.pop(0)()

            def ptrpv(ptc, h):
                pn = nsl(ptc)
                if FA_DBG & {"nosbp", "nosbpexp"}:
                    pn = min(pn, 4)
                gh = ptc * HEADS + h
                p3 = (gh % 3) * 640
                p2 = (gh % 2) * 640
                trb = ps_tr[gh % 2]
                tn.wait_ge(s_pnb, gh + 1)
                if gh >= 2:
                    tn.wait_ge(s_ptev, gh - 1)
                if h == 0:
                    tn.wait_ge(s_qev, 3 * (ptc + 1))
                for si in range(pn):
                    tr = tn.transpose(
                        trb[:, si * 128 : (si + 1) * 128],
                        Pbn[:, p3 + si * 128 : p3 + (si + 1) * 128], idb[:])
                tr.then_inc(s_ptr, 1)
                tn.wait_ge(s_ptev, gh + 1)
                if h == 0 and ptc >= 1:
                    tn.wait_ge(s_oev, ptc)
                for si in range(pn):
                    s_ = ptc if si == 0 else ptc - pn + si
                    tp = (0, 64) if h % 2 else None
                    mm = tn.matmul(
                        ps_oT[64 * (h % 2) : 64 * (h % 2) + 64,
                              (h // 2) * 128 : (h // 2) * 128 + 128],
                        vring[:, (s_ % 5) * 512 + h * 64 : (s_ % 5) * 512 + (h + 1) * 64],
                        PT_sb[:, p2 + si * 128 : p2 + (si + 1) * 128],
                        start=(si == 0), stop=(si == pn - 1),
                        tile_position=tp)
                mm.then_inc(s_pvm, 1)

            for tc in range(NCH):
                gh0 = tc * HEADS
                n = nsl(tc)
                tn.wait_ge(s_xc, tc + 1)
                if tc >= 1:
                    tn.wait_ge(s_qev, 3 * tc)
                for seg in range(3):
                    if seg >= 1:
                        tn.wait_ge(s_qev, 3 * tc + seg)
                    for dc in range(NDC):
                        mm = tn.matmul(
                            ps_qkv[:], xsh[:, dc * 128 : (dc + 1) * 128],
                            wqkv_sb[:, dc * 1536 + seg * 512 : dc * 1536 + (seg + 1) * 512],
                            start=(dc == 0), stop=(dc == NDC - 1))
                    mm.then_inc(s_qkvm, 1)
                    if pend_pp:
                        ptrpv(*pend_pp.pop(0))
                while pend_pp:
                    ptrpv(*pend_pp.pop(0))
                emit_wo(8)
                tn.wait_ge(s_rot, 2 * (tc + 1))
                if tc >= 1:
                    tn.wait_ge(s_trev, 4 * tc)
                    tn.wait_ge(s_ptev, 8 * tc)
                for ki, buf in enumerate((qrot, krot)):
                    for h in range(HEADS):
                        tp = (0, 64) if h % 2 else None
                        tr = tn.transpose(
                            ps_tr[ki][64 * (h % 2) : 64 * (h % 2) + 64,
                                      (h // 2) * 128 : (h // 2) * 128 + 128],
                            buf[:, h * 64 : (h + 1) * 64], idb[:],
                            tile_position=tp)
                tr.then_inc(s_qktr, 1)
                for h in range(HEADS):
                    gh = gh0 + h
                    if gh >= 2:
                        tn.wait_ge(s_peh, gh - 1)
                    if h == 0:
                        tn.wait_ge(s_trev, 4 * (tc + 1))
                    sb0 = (gh % 2) * 128
                    for col0, ncols, si0 in spans(tc, h):
                        if "nosbp" in FA_DBG and si0 + ncols // 128 > 4:
                            ncols -= 128
                            if ncols == 0:
                                continue
                        c = col0
                        rem = ncols
                        si = si0
                        while rem > 0:
                            if si < 4:
                                take = min(rem, (4 - si) * 128)
                                dst = ps_sa[gh % 2][:, si * 128 : si * 128 + take]
                            else:
                                take = rem
                                dst = ps_sbp[:, sb0 : sb0 + take]
                            mm = tn.matmul(
                                dst, qT[:, h * 128 : (h + 1) * 128],
                                kring[:, c : c + take],
                                start=True, stop=True)
                            c += take
                            si += take // 128
                            rem -= take
                    mm.then_inc(s_smm, 1)
                    pend_pp.append((tc, h))
                    while len(pend_pp) > 2:
                        ptrpv(*pend_pp.pop(0))
                if tc == NCH - 1:
                    while pend_pp:
                        ptrpv(*pend_pp.pop(0))
                if tc % 4 == 3:
                    gi = tc // 4

                    def mk(dc, gi=gi):
                        def go():
                            tn.wait_ge(s_oev, 4 * gi + 4)
                            if dc == 0:
                                if gi >= 1:
                                    tn.wait_ge(s_wov, 8 * gi)
                            else:
                                tn.wait_ge(s_wov, 8 * gi + dc)
                            for kc in range(4):
                                mm = tn.matmul(
                                    ps_wo[:],
                                    wo_sb[:, kc * D + dc * 128 : kc * D + (dc + 1) * 128],
                                    oT_buf[:, kc * 512 : (kc + 1) * 512],
                                    start=(kc == 0), stop=(kc == 3))
                            mm.then_inc(s_wom, 1)
                        return go
                    for dc in range(NDC):
                        pend_wo.append(mk(dc))
            emit_wo(len(pend_wo))
        block.tensor(t)

        spc_n = [0]

        def s(s):
            for tc in range(NCH):
                gh0 = tc * HEADS
                n = nsl(tc)
                s.wait_ge(s_qkvm, 3 * tc + 1)
                if tc >= 1:
                    s.wait_ge(s_rot, 2 * tc)
                ins = s.copy(qkf[:, 0:512], ps_qkv[:])
                ins.then_inc(s_qev, 1)
                s.wait_ge(s_qkvm, 3 * tc + 2)
                ins = s.copy(qkf[:, 512:1024], ps_qkv[:])
                ins.then_inc(s_qev, 1)
                s.wait_ge(s_qkvm, 3 * tc + 3)
                if tc >= 1:
                    s.wait_ge(s_pvm, 8 * tc)
                ins = s.copy(vring[:, (tc % 5) * 512 : (tc % 5) * 512 + 512], ps_qkv[:])
                ins.then_inc(s_qev, 1)
                s.wait_ge(s_qev, 3 * tc + 2)
                for hh in range(16):
                    ins = s.activation(sqsc[:], qkf[:, hh * 64 : (hh + 1) * 64],
                                       AF.Square, accum_out=nrm[:, hh : hh + 1])
                ins.then_inc(s_sqq, 1)
                s.wait_ge(s_sqq, tc + 1)
                s.activation(rst[:], nrm[:], AF.Ln, bias=kb.epses[:, 1:2]).then_inc(s_rst, 1)
                s.wait_ge(s_rst, 2 * tc + 1)
                s.activation(rst[:], rst[:], AF.Exp, scale=-0.5).then_inc(s_rst, 1)
                for h in range(HEADS):
                    gh = gh0 + h
                    p3 = (gh % 3) * 640
                    nb = (gh % 3) * 8
                    sb0 = (gh % 2) * 128
                    sbp_on = n == 5 and not (FA_DBG & {"nosbp", "nosbpexp"})
                    if sbp_on:
                        s.wait_ge(s_smm, gh + 1)
                        ins = s.copy(scr5[:, sb0 : sb0 + 128],
                                     ps_sbp[:, sb0 : sb0 + 128])
                        ins.then_inc(s_spc, 1)
                        spc_n[0] += 1
                    s.wait_ge(s_msk, gh + 1)
                    if gh >= 3:
                        s.wait_ge(s_ptr, gh - 2)
                    na = min(n, 4) * 128
                    ins = s.activation(
                        Pbn[:, p3 : p3 + na], ps_sa[gh % 2][:, 0:na], AF.Exp,
                        accum_out=ncol[:, nb : nb + 1])
                    if sbp_on:
                        s.wait_ge(s_spc, spc_n[0])
                        ins = s.activation(
                            Pbn[:, p3 + 512 : p3 + 640],
                            scr5[:, sb0 : sb0 + 128], AF.Exp,
                            accum_out=ncol[:, nb + 1 : nb + 2])
                    ins.then_inc(s_peh, 1)
        block.scalar(s)

        vv = [0]

        def v(v):
            pend_pt = []

            def flush_pt(k):
                for _ in range(k):
                    if pend_pt:
                        ptc, pgh = pend_pt.pop(0)
                        pn = nsl(ptc)
                        p2v = (pgh % 2) * 640
                        v.wait_ge(s_ptr, pgh + 1)
                        if pgh >= 2:
                            v.wait_ge(s_pvm, pgh - 1)
                        ins = v.tensor_copy(PT_sb[:, p2v : p2v + pn * 128],
                                            ps_tr[pgh % 2][:, 0 : pn * 128])
                        ins.then_inc(s_ptev, 1)

            def VI(ins):
                vv[0] += 1
                ins.then_inc(s_vv, 1)
            v.wait_ge(s_in2, 32)
            v.wait_ge(s_in3, 32)
            rot_end = {}

            pend_add = []

            def drain_add(k):
                for _ in range(k):
                    if pend_add:
                        pend_add.pop(0)()

            def tail(ptc):
                v.wait_ge(s_pvm, 8 * (ptc + 1))
                ins = v.tensor_copy(
                    oT_buf[:].rearrange("p (j g x) -> p j g x", j=4, g=4)[
                        :, :, ptc % 4, :],
                    ps_oT[:].rearrange("p (j x) -> p j x", j=4))
                ins.then_inc(s_oev, 1)
                if ptc % 4 == 3:
                    gi = ptc // 4

                    def mka(dc, gi=gi):
                        def go():
                            v.wait_ge(s_wom, 8 * gi + dc + 1)
                            sl = stream[:, dc * STR_W + 1 + gi * 512 :
                                        dc * STR_W + 1 + (gi + 1) * 512]
                            ins = v.tensor_tensor(sl, sl, ps_wo[:], ALU.add)
                            ins.then_inc(s_wov, 1)
                        return go
                    for dc in range(NDC):
                        pend_add.append(mka(dc))

            for tc in range(NCH):
                gh0 = tc * HEADS
                n = nsl(tc)
                flush_pt(len(pend_pt))
                if tc >= 1:
                    tail(tc - 1)
                if tc >= 1:
                    v.wait_ge(s_vv, rot_end[tc - 1])
                    v.wait_ge(s_rot, 2 * tc)
                v.wait_ge(s_rst, 2 * (tc + 1))
                rq = rst[:, 0:8].unsqueeze(2).broadcast_to([128, 8, 64]
                     ).rearrange("p h (two half) -> p h two half", two=2)
                rk = rst[:, 8:16].unsqueeze(2).broadcast_to([128, 8, 64]
                     ).rearrange("p h (two half) -> p h two half", two=2)

                def q4(x):
                    return x.rearrange("p (h two half) -> p h two half", h=8, two=2)

                def sw4(x):
                    return q4(x)[:, :, ::-1, :]

                def tb4(ts):
                    return ts.rearrange("p (two half) -> p two half", two=2
                           ).unsqueeze(1).broadcast_to([128, 8, 2, 32])
                VI(v.tensor_tensor(q4(qkf[:, 0:512]), q4(qkf[:, 0:512]), rq, ALU.mult))
                drain_add(1)
                VI(v.tensor_tensor(q4(qkf[:, 512:1024]), q4(qkf[:, 512:1024]), rk, ALU.mult))
                drain_add(1)
                v.wait_ge(s_vv, vv[0])
                VI(v.tensor_tensor(q4(qkf[:, 0:512]), q4(qkf[:, 0:512]), q4(qsc_bc[:]), ALU.mult))
                drain_add(1)
                VI(v.tensor_tensor(q4(qkf[:, 512:1024]), q4(qkf[:, 512:1024]), q4(ksc_bc[:]), ALU.mult))
                drain_add(1)
                v.wait_ge(s_tb, 64 * (tc + 1))
                tb0 = (tc % 2) * 256
                cq = tb4(tbl[:, tb0 + 0 : tb0 + 64])
                sq_ = tb4(tbl[:, tb0 + 64 : tb0 + 128])
                ck = tb4(tbl[:, tb0 + 128 : tb0 + 192])
                sk = tb4(tbl[:, tb0 + 192 : tb0 + 256])
                v.wait_ge(s_vv, vv[0])
                VI(v.tensor_tensor(q4(tmp1[:]), sw4(qkf[:, 0:512]), sq_, ALU.mult))
                drain_add(1)
                v.wait_ge(s_vv, vv[0])
                VI(v.tensor_tensor(q4(qkf[:, 0:512]), q4(qkf[:, 0:512]), cq, ALU.mult))
                drain_add(1)
                v.wait_ge(s_vv, vv[0])
                ins = v.tensor_add(qrot[:], tmp1[:], qkf[:, 0:512])
                ins.then_inc(s_rot, 1)
                drain_add(1)
                VI(v.tensor_tensor(q4(tmp1[:]), sw4(qkf[:, 512:1024]), sk, ALU.mult))
                v.wait_ge(s_vv, vv[0])
                VI(v.tensor_tensor(q4(qkf[:, 512:1024]), q4(qkf[:, 512:1024]), ck, ALU.mult))
                drain_add(1)
                v.wait_ge(s_vv, vv[0])
                ins = v.tensor_add(krot[:], tmp1[:], qkf[:, 512:1024])
                ins.then_inc(s_rot, 1)
                rot_end[tc] = vv[0]
                v.wait_ge(s_qktr, tc + 1)
                v.wait_ge(s_smm, 8 * tc)
                qTv = qT[:].rearrange("p (h x) -> p h x", h=HEADS)
                krv = kring[:].rearrange("p (h s x) -> p h s x", h=HEADS, s=5)[
                    :, :, tc % 5, :]
                ins = v.tensor_copy(qTv[:, 0::2, :],
                                    ps_tr[0][0:64, 0:512].rearrange("p (j x) -> p j x", j=4))
                ins.then_inc(s_trev, 1)
                ins = v.tensor_copy(qTv[:, 1::2, :],
                                    ps_tr[0][64:128, 0:512].rearrange("p (j x) -> p j x", j=4))
                ins.then_inc(s_trev, 1)
                ins = v.tensor_copy(krv[:, 0::2, :],
                                    ps_tr[1][0:64, 0:512].rearrange("p (j x) -> p j x", j=4))
                ins.then_inc(s_trev, 1)
                ins = v.tensor_copy(krv[:, 1::2, :],
                                    ps_tr[1][64:128, 0:512].rearrange("p (j x) -> p j x", j=4))
                ins.then_inc(s_trev, 1)

                def vpartB(pgh):
                    nb = (pgh % 3) * 8
                    p3 = (pgh % 3) * 640
                    pn = nsl(pgh // HEADS)
                    if FA_DBG & {"nosbp", "nosbpexp"}:
                        pn = min(pn, 4)
                    v.wait_ge(s_peh, pgh + 1)
                    if pn == 5:
                        VI(v.tensor_add(ncol[:, nb : nb + 1], ncol[:, nb : nb + 1],
                                        ncol[:, nb + 1 : nb + 2]))
                        v.wait_ge(s_vv, vv[0])
                    ins = v.reciprocal(ncol[:, nb + 2 : nb + 3],
                                       ncol[:, nb : nb + 1])
                    ins.then_inc(s_rci, 1)
                    v.wait_ge(s_rci, pgh + 1)
                    ins = v.tensor_scalar(
                        Pbn[:, p3 : p3 + pn * 128],
                        Pbn[:, p3 : p3 + pn * 128],
                        ncol[:, nb + 2 : nb + 3], None, ALU.mult)
                    ins.then_inc(s_pnb, 1)

                mw = 256 if (tc >= 4 and "nobold" not in FA_DBG) else 128
                for h in range(HEADS):
                    gh = gh0 + h
                    v.wait_ge(s_smm, gh + 1)
                    ins = v.tensor_tensor(ps_sa[gh % 2][:, 0:mw],
                                          ps_sa[gh % 2][:, 0:mw],
                                          bmask[:, 0:mw], ALU.add)
                    ins.then_inc(s_msk, 1)
                    if h >= 1:
                        vpartB(gh - 1)
                    pend_pt.append((tc, gh))
                    flush_pt(len(pend_pt) - 2)
                vpartB(gh0 + HEADS - 1)
                if tc % 4 == 3 and tc < NCH - 1:
                    gi = tc // 4
                    for dc in range(4, 8):
                        ins = v.tensor_copy(
                            stash[:, dc - 4 : dc - 3],
                            stream[:, dc * STR_W + (gi + 1) * 512 :
                                   dc * STR_W + (gi + 1) * 512 + 1])
                    ins.then_inc(s_sta, 1)
                if tc == NCH - 1:
                    flush_pt(len(pend_pt))
                    tail(tc)
                    drain_add(len(pend_add))
        block.vector(v)

        ngr = NCH // 4
        return {
            "gpsimd": [(s_xc, NCH), (s_ini, 32), (s_in3, 48)],
            "sync": [(s_tb, 64 * NCH), (s_in2, 32)],
            "tensor": [(s_qkvm, 3 * NCH), (s_qktr, NCH), (s_smm, 8 * NCH),
                       (s_wom, 8 * ngr), (s_pvm, 8 * NCH), (s_ptr, 8 * NCH)],
            "scalar": [(s_qev, 3 * NCH), (s_sqq, NCH), (s_rst, 2 * NCH),
                       (s_peh, 8 * NCH), (s_ptev, 8 * NCH), (s_spc, spc_n[0])],
            "vector": [(s_rot, 2 * NCH), (s_trev, 4 * NCH),
                       (s_msk, 8 * NCH), (s_sta, ngr - 1), (s_vv, vv[0]),
                       (s_wov, 8 * ngr), (s_oev, NCH), (s_pnb, 8 * NCH),
                       (s_rci, 8 * NCH)],
        }


def phase_Fb(kb, block, din, dr, stream, b, l):
    tag = f"fb{b}{l}"
    NTR = 8
    with ExitStack() as st:
        h2bf = kb.sb(st, tag + "h2", [128, NDC * 512], BF16)
        z_bf = kb.sb(st, tag + "zb", [128, NDC * 512], BF16)
        sq2 = kb.sb(st, tag + "sq", [128, NDC * 512], BF16)
        R1sb = kb.sb(st, tag + "r1", [128, 512], F32)
        R2sb = kb.sb(st, tag + "r2", [128, 512], F32)
        musb = kb.sb(st, tag + "mu", [128, 512], F32)
        ztmp = kb.sb(st, tag + "zt", [128, 512], F32)
        btmp = musb
        h4_buf = kb.sb(st, tag + "h4", [128, NPAIR * 512], BF16)
        w1t = kb.sb(st, tag + "w1", [128, 3 * NDC * 256], BF16)
        w2t = kb.sb(st, tag + "w2", [128, 2 * NPAIR * 128], BF16)
        gel = kb.sb(st, tag + "ge", [128, 2 * 512], BF16)
        b1c = kb.sb(st, tag + "b1", [128, 2 * NPAIR], F32)
        stash = kb.sb(st, tag + "sh", [128, 4], F32)
        lngc = kb.sb(st, tag + "lg", [128, 8], F32)
        lnbc = kb.sb(st, tag + "lb", [128, 8], F32)
        b2c = kb.sb(st, tag + "b2", [128, 8], F32)
        onb_nd = kb.sb(st, tag + "on", [128, 128], BF16)

        ps_mu = kb.ps(st, tag + "pmu", [128, 512], F32)
        ps_ms = kb.ps(st, tag + "pms", [128, 512], F32)
        ps_bc1 = kb.ps(st, tag + "pb1", [128, 512], F32)
        ps_bc2 = kb.ps(st, tag + "pb2", [128, 512], F32)
        ps_a = kb.ps(st, tag + "pa0", [128, 512], F32)
        ps_g = kb.ps(st, tag + "pg0", [128, 512], F32)
        ps_a2 = kb.ps(st, tag + "pa1", [128, 512], F32)
        ps_g2 = kb.ps(st, tag + "pg1", [128, 512], F32)

        s_ini = kb.sem(tag + "ini")
        s_in2 = kb.sem(tag + "in2")
        s_h2c = kb.sem(tag + "hc")
        s_sq2 = kb.sem(tag + "sq")
        s_stm = kb.sem(tag + "st")
        s_msv = kb.sem(tag + "mv")
        s_rld = kb.sem(tag + "rl")
        s_r2 = kb.sem(tag + "m2")
        s_z = kb.sem(tag + "zz")
        s_w1 = kb.sem(tag + "w1")
        s_h3m = kb.sem(tag + "h3")
        s_gel = kb.sem(tag + "gl")
        s_h4 = kb.sem(tag + "h4")
        s_w2 = kb.sem(tag + "w2")
        s_xm = kb.sem(tag + "xm")
        s_xadd = kb.sem(tag + "xa")
        s_stb = kb.sem(tag + "sb")

        def sy(sy):
            sy.dma_start(out=b1c[:], in_=din["b1_pk"][l]).then_inc(s_ini, 16)
            sy.dma_start(out=lngc[:], in_=din["lng_pk"][l]).then_inc(s_ini, 16)
            sy.dma_start(out=lnbc[:], in_=din["lnb_pk"][l]).then_inc(s_ini, 16)
            sy.dma_start(out=b2c[:], in_=din["b2_pk"][l]).then_inc(s_ini, 16)
        block.sync(sy)

        def g(g):
            g.dma_start(out=onb_nd[:], in_=din["ones_nd"]).then_inc(s_in2, 16)
            def cast_op(tr):
                if tr >= 1:
                    g.wait_ge(s_stm, 2 * tr)
                    g.wait_ge(s_sq2, tr)
                    g.wait_ge(s_stb, tr)
                h2v = h2bf[:].rearrange("p (d x) -> p d x", d=NDC)
                stv = stream[:].rearrange("p (d w) -> p d w", d=NDC)
                g.tensor_copy(h2v[:, 0:4, :],
                              stv[:, 0:4, tr * 512 + 1 : tr * 512 + 513])
                if tr >= 1:
                    g.tensor_copy(
                        h2v[:, 4:8, 0:1],
                        stash[:].rearrange("p (d x) -> p d x", d=4))
                    ins = g.tensor_copy(
                        h2v[:, 4:8, 1:512],
                        stv[:, 4:8, tr * 512 + 1 : tr * 512 + 512])
                else:
                    ins = g.tensor_copy(h2v[:, 4:8, :],
                                        stv[:, 4:8, 0:512])
                ins.then_inc(s_h2c, 1)

            def w1_loads(tr, lo=0, hi=NPAIR):
                for kc in range(lo, hi):
                    sz = PAIR_SZ[kc]
                    gk = tr * NPAIR + kc
                    if gk >= 3:
                        g.wait_ge(s_h3m, gk - 2)
                    g.dma_start(
                        out=w1t[:, ((gk % 3) * NDC) * 256 : ((gk % 3) * NDC) * 256 + NDC * 2 * sz
                                ].rearrange("p (c m) -> p c m", c=NDC),
                        in_=dr["w1p_bf"][l].rearrange("(c p) m -> p c m", p=128)[
                            :, :, CH_OFF[kc] : CH_OFF[kc] + 2 * sz],
                    ).then_inc(s_w1, 16)

            cast_op(0)
            w1_loads(0, 0, 14)
            for tr in range(NTR):
                if tr < NTR - 1:
                    cast_op(tr + 1)
                w1_loads(tr, 14, NPAIR)
                if tr < NTR - 1:
                    w1_loads(tr + 1, 0, 3)
                for dc in range(NDC):
                    if tr * NDC + dc >= 2:
                        g.wait_ge(s_xm, tr * NDC + dc - 1)
                    g.dma_start(
                        out=w2t[:, (dc % 2) * NPAIR * 128 : (dc % 2) * NPAIR * 128 + 21 * 128
                                ].rearrange("p (c m) -> p c m", c=21),
                        in_=dr["w2_bf"][l][0:2688, dc * 128 : (dc + 1) * 128].rearrange(
                            "(c p) m -> p c m", p=128),
                    ).then_inc(s_w2, 16)
                    g.dma_start(
                        out=w2t[0:42, (dc % 2) * NPAIR * 128 + 21 * 128 : (dc % 2) * NPAIR * 128 + 22 * 128],
                        in_=dr["w2_bf"][l][2688:2730, dc * 128 : (dc + 1) * 128],
                    ).then_inc(s_w2, 16)
                if tr < NTR - 1:
                    w1_loads(tr + 1, 3, 14)
        block.gpsimd(g)

        def t(tn):
            tn.wait_ge(s_ini, 64)
            tn.wait_ge(s_in2, 16)

            def stats_mm(T):
                tn.wait_ge(s_h2c, T + 1)
                if T >= 1:
                    tn.wait_ge(s_msv, 3 * T - 2)
                for dc in range(NDC):
                    mm = tn.matmul(ps_mu[:], onb_nd[:],
                                   h2bf[:, dc * 512 : (dc + 1) * 512],
                                   start=(dc == 0), stop=(dc == NDC - 1))
                mm.then_inc(s_stm, 1)
                tn.wait_ge(s_sq2, T + 1)
                if T >= 1:
                    tn.wait_ge(s_rld, 2 * T - 1)
                for dc in range(NDC):
                    mm = tn.matmul(ps_ms[:], onb_nd[:],
                                   sq2[:, dc * 512 : (dc + 1) * 512],
                                   start=(dc == 0), stop=(dc == NDC - 1))
                mm.then_inc(s_stm, 1)

            stats_mm(0)
            for tr in range(NTR):
                for kc in range(NPAIR):
                    sz = PAIR_SZ[kc]
                    gk = tr * NPAIR + kc
                    tn.wait_ge(s_w1, 16 * (gk + 1))
                    tn.wait_ge(s_z, 8 * (tr + 1))
                    if gk >= 2:
                        tn.wait_ge(s_h4, gk - 1)
                    pa = ps_a if kc % 2 == 0 else ps_a2
                    pg = ps_g if kc % 2 == 0 else ps_g2
                    for dc in range(NDC):
                        base = (gk % 3) * NDC * 256 + dc * 2 * sz
                        tn.matmul(pa[0:sz, :], w1t[:, base : base + sz],
                                  z_bf[:, dc * 512 : (dc + 1) * 512],
                                  start=(dc == 0), stop=(dc == NDC - 1))
                        mm = tn.matmul(pg[0:sz, :], w1t[:, base + sz : base + 2 * sz],
                                       z_bf[:, dc * 512 : (dc + 1) * 512],
                                       start=(dc == 0), stop=(dc == NDC - 1))
                    mm.then_inc(s_h3m, 1)
                if tr < NTR - 1:
                    stats_mm(tr + 1)
                for dc in range(NDC):
                    gd = tr * NDC + dc
                    tn.wait_ge(s_w2, 32 * (gd + 1))
                    px = ps_bc1 if dc % 2 == 0 else ps_bc2
                    if dc >= 2:
                        tn.wait_ge(s_xadd, 8 * tr + dc - 1)
                    for kc in range(NPAIR):
                        sz = PAIR_SZ[kc]
                        tn.wait_ge(s_h4, tr * NPAIR + kc + 1)
                        mm = tn.matmul(
                            px[:],
                            w2t[0:sz, (dc % 2) * NPAIR * 128 + kc * 128 : (dc % 2) * NPAIR * 128 + (kc + 1) * 128],
                            h4_buf[0:sz, kc * 512 : (kc + 1) * 512],
                            start=(kc == 0), stop=(kc == NPAIR - 1),
                        )
                    mm.then_inc(s_xm, 1)
        block.tensor(t)

        def s(s):
            def sq_op(T):
                s.wait_ge(s_h2c, T + 1)
                if T >= 1:
                    s.wait_ge(s_stm, 2 * T)
                s.activation(sq2[:], h2bf[:], AF.Square).then_inc(s_sq2, 1)

            def rstd_op(T):
                s.wait_ge(s_msv, 3 * (T + 1))
                if T >= 1:
                    s.wait_ge(s_z, 8 * T)
                s.activation(R1sb[:], ps_ms[:], AF.Ln, scale=-1.0,
                             bias=kb.epses[:, 0:1]).then_inc(s_rld, 1)
                s.wait_ge(s_rld, 2 * T + 1)
                s.activation(R1sb[:], R1sb[:], AF.Exp,
                             scale=-0.5).then_inc(s_rld, 1)

            sq_op(0)
            rstd_op(0)
            for tr in range(NTR):
                for kc in range(NPAIR):
                    sz = PAIR_SZ[kc]
                    gk = tr * NPAIR + kc
                    if kc == 10 and tr < NTR - 1:
                        sq_op(tr + 1)
                    s.wait_ge(s_h3m, gk + 1)
                    if gk >= 2:
                        s.wait_ge(s_h4, gk - 1)
                    pg = ps_g if kc % 2 == 0 else ps_g2
                    s.activation(gel[0:sz, (kc % 2) * 512 : (kc % 2) * 512 + 512],
                                 pg[0:sz, :], AF.Gelu,
                                 bias=b1c[0:sz, 2 * kc + 1 : 2 * kc + 2]).then_inc(s_gel, 1)
                if tr < NTR - 1:
                    rstd_op(tr + 1)
        block.scalar(s)

        def v(v):
            def stats_vec(T):
                v.wait_ge(s_stm, 2 * (T + 1))
                if T >= 1:
                    v.wait_ge(s_r2, T)
                    v.wait_ge(s_z, NDC * T)
                ins = v.tensor_copy(musb[:], ps_mu[:])
                ins.then_inc(s_msv, 1)
                v.wait_ge(s_msv, 3 * T + 1)
                ins = v.tensor_tensor(ztmp[:], musb[:], musb[:], ALU.mult)
                ins.then_inc(s_msv, 1)
                v.wait_ge(s_msv, 3 * T + 2)
                ins = v.tensor_tensor(ps_ms[:], ps_ms[:], ztmp[:], ALU.add)
                ins.then_inc(s_msv, 1)

            def r2_op(T):
                v.wait_ge(s_rld, 2 * (T + 1))
                ins = v.tensor_tensor(R2sb[:], musb[:], R1sb[:], ALU.mult)
                ins.then_inc(s_r2, 1)

            def z_op(T):
                v.wait_ge(s_rld, 2 * (T + 1))
                v.wait_ge(s_r2, T + 1)
                if T >= 1:
                    v.wait_ge(s_h3m, NPAIR * T)
                for dc in range(NDC):
                    gz = T * NDC + dc
                    off = dc * STR_W + T * 512 + (1 if dc < 4 else 0)
                    if dc >= 4 and T >= 1:
                        v.scalar_tensor_tensor(
                            ztmp[:, 0:1], stash[:, dc - 4 : dc - 3],
                            lngc[:, dc : dc + 1], R1sb[:, 0:1], ALU.mult, ALU.mult)
                        i1 = v.scalar_tensor_tensor(
                            ztmp[:, 1:512], stream[:, off + 1 : off + 512],
                            lngc[:, dc : dc + 1], R1sb[:, 1:512], ALU.mult, ALU.mult)
                    else:
                        i1 = v.scalar_tensor_tensor(
                            ztmp[:], stream[:, off : off + 512], lngc[:, dc : dc + 1],
                            R1sb[:], ALU.mult, ALU.mult)
                    i2 = v.tensor_scalar(btmp[:], R2sb[:], lngc[:, dc : dc + 1],
                                         lnbc[:, dc : dc + 1], ALU.mult, ALU.add)
                    i2.then_inc(s_zt, 1)
                    v.wait_ge(s_zt, gz + 1)
                    ins = v.tensor_tensor(z_bf[:, dc * 512 : (dc + 1) * 512],
                                          ztmp[:], btmp[:], ALU.add)
                    ins.then_inc(s_z, 1)

            s_zt = kb.sem(tag + "ztm")
            stats_vec(0)
            r2_op(0)
            z_op(0)
            for tr in range(NTR):
                if tr < NTR - 1:
                    v.wait_ge(s_h2c, tr + 1)
                    for dc in range(4, 8):
                        ins = v.tensor_copy(
                            stash[:, dc - 4 : dc - 3],
                            stream[:, dc * STR_W + (tr + 1) * 512 : dc * STR_W + (tr + 1) * 512 + 1])
                    ins.then_inc(s_stb, 1)
                for kc in range(NPAIR):
                    sz = PAIR_SZ[kc]
                    gk = tr * NPAIR + kc
                    v.wait_ge(s_gel, gk + 1)
                    if tr >= 1 and kc == 0:
                        v.wait_ge(s_xm, NDC * tr)
                    pa = ps_a if kc % 2 == 0 else ps_a2
                    ins = v.scalar_tensor_tensor(
                        h4_buf[0:sz, kc * 512 : (kc + 1) * 512],
                        pa[0:sz, :], b1c[0:sz, 2 * kc : 2 * kc + 1],
                        gel[0:sz, (kc % 2) * 512 : (kc % 2) * 512 + 512],
                        ALU.add, ALU.mult)
                    ins.then_inc(s_h4, 1)
                if tr < NTR - 1:
                    stats_vec(tr + 1)
                    r2_op(tr + 1)
                    z_op(tr + 1)
                for dc in range(NDC):
                    gd = tr * NDC + dc
                    v.wait_ge(s_xm, gd + 1)
                    px = ps_bc1 if dc % 2 == 0 else ps_bc2
                    sl = stream[:, dc * STR_W + 1 + tr * 512 : dc * STR_W + 1 + (tr + 1) * 512]
                    ins = v.scalar_tensor_tensor(sl, px[:], b2c[:, dc : dc + 1], sl,
                                                 ALU.add, ALU.add)
                    ins.then_inc(s_xadd, 1)
        block.vector(v)

        return {
            "sync": [(s_ini, 64)],
            "gpsimd": [(s_h2c, NTR), (s_w1, 16 * NPAIR * NTR), (s_w2, 32 * NDC * NTR),
                       (s_in2, 16)],
            "tensor": [(s_xm, NDC * NTR), (s_stm, 2 * NTR)],
            "scalar": [(s_gel, NPAIR * NTR), (s_sq2, NTR), (s_rld, 2 * NTR)],
            "vector": [(s_xadd, NDC * NTR), (s_h4, NPAIR * NTR), (s_z, NDC * NTR),
                       (s_msv, 3 * NTR), (s_r2, NTR), (s_stb, NTR - 1)],
        }


def phase_FIN(kb, block, din, dr, stream, b):
    with ExitStack() as st:
        ptile = kb.sb(st, f"fin{b}p", [128, 2 * D], F32)
        otile = kb.sb(st, f"fin{b}o", [128, 2 * D], F32)
        idf = kb.sb(st, f"fin{b}i", [128, 128], F32)
        psF = [kb.ps(st, f"fin{b}ps{i}", [128, 512], F32) for i in range(4)]
        s_p = kb.sem(f"fin{b}pl")
        s_tr = kb.sem(f"fin{b}tr")
        s_ad = kb.sem(f"fin{b}ad")
        s_st = kb.sem(f"fin{b}st")
        s_idf = kb.sem(f"fin{b}id")

        def sy(sy):
            sy.dma_start(out=idf[:], in_=din["ident"]).then_inc(s_idf, 16)
            for tc in range(NT):
                if tc >= 2:
                    sy.wait_ge(s_ad, 2 * (tc - 1))
                sy.dma_start(
                    out=ptile[:, (tc % 2) * D : (tc % 2) * D + D],
                    in_=dr["pbuf"][b, tc * 128 : (tc + 1) * 128, :],
                ).then_inc(s_p, 16)
                sy.wait_ge(s_ad, 2 * (tc + 1))
                sy.dma_start(
                    out=dr["out"][b, tc * 128 : (tc + 1) * 128, :],
                    in_=otile[:, (tc % 2) * D : (tc % 2) * D + D],
                ).then_inc(s_st, 16)
            sy.wait_ge(s_st, 16 * NT)
        block.sync(sy)

        def t(tn):
            tn.wait_ge(s_idf, 16)
            for tc in range(NT):
                for half in range(2):
                    gi = 2 * tc + half
                    if gi >= 4:
                        tn.wait_ge(s_ad, gi - 3)
                    pt = psF[gi % 4]
                    for j in range(4):
                        dc = 4 * half + j
                        tr = tn.transpose(
                            pt[:, j * 128 : (j + 1) * 128],
                            stream[:, dc * STR_W + 1 + tc * 128 : dc * STR_W + 1 + (tc + 1) * 128],
                            idf[:],
                        )
                    tr.then_inc(s_tr, 1)
        block.tensor(t)

        def v(v):
            for tc in range(NT):
                v.wait_ge(s_p, 16 * (tc + 1))
                if tc >= 2:
                    v.wait_ge(s_st, 16 * (tc - 1))
                for half in range(2):
                    gi = 2 * tc + half
                    v.wait_ge(s_tr, gi + 1)
                    v.tensor_tensor(
                        otile[:, (tc % 2) * D + half * 512 : (tc % 2) * D + half * 512 + 512],
                        psF[gi % 4][:],
                        ptile[:, (tc % 2) * D + half * 512 : (tc % 2) * D + half * 512 + 512],
                        ALU.add,
                    ).then_inc(s_ad, 1)
        block.vector(v)

        return {
            "sync": [(s_st, 16 * NT), (s_p, 16 * NT), (s_idf, 16)],
            "tensor": [(s_tr, 2 * NT)],
            "vector": [(s_ad, 2 * NT)],
        }



def kernel(**inputs):
    hp = host_prep(inputs)
    x = np.ascontiguousarray(np.asarray(inputs["L0_attn_in_features"], np.float32))
    kb, dr = build(dev=False)
    in_maps = []
    for c in range(NCORES):
        m = {"xsh": np.ascontiguousarray(x[NB * c : NB * (c + 1)])}
        m.update(hp)
        in_maps.append(m)
    res = run_bass_kernel_spmd(kb.nc, in_maps, core_ids=list(range(NCORES)))
    return np.concatenate([r["out"] for r in res.results], axis=0)


# version 26
# speedup vs baseline: 1.0077x; 1.0050x over previous
import numpy as np
import ml_dtypes
from contextlib import ExitStack

import concourse.bass as bass
import concourse.mybir as mybir
from concourse.bass_utils import run_bass_kernel_spmd

F32 = mybir.dt.float32
BF16 = mybir.dt.bfloat16
AF = mybir.ActivationFunctionType
ALU = mybir.AluOpType
AX = mybir.AxisListType
bfdt = ml_dtypes.bfloat16

NCORES = 8
NB = 2
T = 4096
D = 1024
NT = T // 128
NDC = D // 128
HID = 2730
H2 = 5460
HEADS = 8
DH = 64
INNER = 512
WIN = 512
NEG = -1.0e30
NPAIR = 22
PAIR_SZ = [128] * 21 + [42]
CH_OFF = np.cumsum([0] + [2 * s for s in PAIR_SZ]).tolist()
STR_W = T + 1

ENGS = ("tensor", "vector", "scalar", "gpsimd", "sync")


def _rotary_tables():
    half = np.arange(0, DH, 2, dtype=np.float64)
    inv_freq = 1.0 / 10000.0 ** (half / DH)
    t = np.arange(T, dtype=np.float64)
    freqs = t[:, None] * inv_freq[None, :]
    freqs = np.concatenate([freqs, freqs], axis=-1)
    base = (half + 0.4 * DH) / (1.4 * DH)
    power = (t - T // 2) / float(WIN // 2)
    scale = base[None, :] ** power[:, None]
    scale = np.concatenate([scale, scale], axis=-1)
    cos, sin = np.cos(freqs), np.sin(freqs)
    sgn = np.concatenate([-np.ones(DH // 2), np.ones(DH // 2)])
    csq = (8.0 * cos * scale).astype(np.float32)
    ssq = (8.0 * sin * scale * sgn).astype(np.float32)
    csk = (cos / scale).astype(np.float32)
    ssk = (sin / scale * sgn).astype(np.float32)
    return csq, ssq, csk, ssk


def host_prep(inputs):
    c = np.ascontiguousarray
    d = {}
    d["WqT"] = c(np.asarray(inputs["L0_attn_Wq"]).T)
    d["WkT"] = c(np.asarray(inputs["L0_attn_Wk"]).T)
    d["Wv"] = c(np.asarray(inputs["L0_attn_Wv"]))
    d["WoT"] = c(np.asarray(inputs["L0_attn_Wo"]).T)
    d["wqkv"] = c(np.asarray(inputs["L0_ffn_wqkv"]))
    d["wo"] = c(np.asarray(inputs["L0_ffn_wo"]))
    w1 = np.asarray(inputs["L0_ffn_w1"])
    b1 = np.asarray(inputs["L0_ffn_b1"])
    perm = []
    for ci in range(NPAIR):
        sz = PAIR_SZ[ci]
        perm.extend(range(128 * ci, 128 * ci + sz))
        perm.extend(range(HID + 128 * ci, HID + 128 * ci + sz))
    perm = np.asarray(perm)
    d["w1p"] = c(w1[:, :, perm])
    b1p = b1[:, perm]
    b1_pk = np.zeros((2, 128, 2 * NPAIR), np.float32)
    for ci in range(NPAIR):
        sz = PAIR_SZ[ci]
        off = CH_OFF[ci]
        b1_pk[:, :sz, 2 * ci] = b1p[:, off : off + sz]
        b1_pk[:, :sz, 2 * ci + 1] = b1p[:, off + sz : off + 2 * sz]
    d["b1_pk"] = b1_pk
    d["w2"] = c(np.asarray(inputs["L0_ffn_w2"]))
    d["lng_pk"] = c(np.asarray(inputs["L0_ffn_ln_g"]).reshape(2, 8, 128).transpose(0, 2, 1))
    d["lnb_pk"] = c(np.asarray(inputs["L0_ffn_ln_b"]).reshape(2, 8, 128).transpose(0, 2, 1))
    d["b2_pk"] = c(np.asarray(inputs["L0_ffn_b2"]).reshape(2, 8, 128).transpose(0, 2, 1))
    qs_t = np.tile(np.asarray(inputs["L0_ffn_q_scale"], np.float32), (1, HEADS))
    ks_t = np.tile(np.asarray(inputs["L0_ffn_k_scale"], np.float32), (1, HEADS))
    d["qscale_bc"] = c(np.broadcast_to(qs_t[:, None, :], (2, 128, INNER)))
    d["kscale_bc"] = c(np.broadcast_to(ks_t[:, None, :], (2, 128, INNER)))
    csq, ssq, csk, ssk = _rotary_tables()
    d["tbl4"] = np.ascontiguousarray(
        np.concatenate([csq, ssq, csk, ssk], axis=1).astype(np.float32))
    d["ident"] = np.eye(128, dtype=np.float32)
    d["ones_row"] = np.ones((1, 128), np.float32)
    d["ones_col"] = np.ones((128, 1), np.float32)
    ii, jj = np.indices((128, 128))
    d["b_diag"] = np.where(jj <= ii, 0.0, NEG).astype(np.float32)
    d["b_old"] = np.where(jj >= ii, 0.0, NEG).astype(np.float32)
    d["epses"] = np.tile(np.asarray([[1e-5, 1e-24]], np.float32), (128, 1))
    d["ones_nd"] = np.full((128, 128), -1.0 / D, np.float32)
    d["ones_pd"] = np.full((128, 128), 1.0 / D, np.float32)
    return d


class KB:

    def __init__(self, dev=False):
        self.dev = dev
        self.nc = bass.Bass("TRN2", target_bir_lowering=False, debug=False,
                            detect_race_conditions=False)
        self.stack = ExitStack()
        self.sems = {}
        self.all_sems = {}
        self.bar_n = 0
        self.sbuf_cols = 0

    def sem(self, name):
        if name not in self.sems:
            self.sems[name] = self.stack.enter_context(self.nc.semaphore(name))
            self.all_sems[name] = self.sems[name]
        return self.sems[name]

    def barrier(self, block, flush=None):
        h = self.sem("bar")
        self.bar_n += 5
        tgt = self.bar_n
        flush = flush or {}
        for name in ENGS:
            def body(e, name=name):
                for sm, val in flush.get(name, ()):
                    e.wait_ge(sm, val)
                e.nop().then_inc(h, 1)
                e.wait_ge(h, tgt)
            getattr(block, name)(body)

    def recycle(self, block, prefixes, force=False):
        if not force and len(self.sems) < 72:
            return
        keep = ("bar", "cld")
        names = [n for n in self.sems
                 if n not in keep and not n.startswith("tap")]
        if not names:
            return
        handles = [self.sems.pop(n) for n in names]

        def g(g):
            self.nc.clear_and_free_semaphores(handles)
        block.gpsimd(g)
        self.barrier(block)

    def sb(self, st, name, shape, dtype):
        return st.enter_context(self.nc.sbuf_tensor(name, shape, dtype))

    def ps(self, st, name, shape, dtype=F32):
        return st.enter_context(self.nc.psum_tensor(name, shape, dtype))



def phase_W(kb, block, din, dr):
    nc = kb.nc
    with ExitStack() as st:
        wv_sb = kb.sb(st, "w_wv", [128, NDC * D], BF16)
        wot_sb = kb.sb(st, "w_wot", [128, NDC * D], BF16)
        mvo_st = kb.sb(st, "w_mvo", [128, 2 * D], BF16)
        psA = kb.ps(st, "w_psA", [128, 512], F32)
        psB = kb.ps(st, "w_psB", [128, 512], F32)
        s_ld = kb.sem("w_ld")
        s_mm = kb.sem("w_mm")
        s_ev = kb.sem("w_ev")
        s_st = kb.sem("w_st")

        def g(g):
            for l in range(2):
                g.dma_start(out=dr["w1p_bf"][l], in_=din["w1p"][l]).then_inc(kb.sem("w_cast"), 16)
                g.dma_start(out=dr["w2_bf"][l], in_=din["w2"][l]).then_inc(kb.sem("w_cast"), 16)
            g.dma_start(
                out=wv_sb[:].rearrange("p (k m) -> p k m", k=NDC),
                in_=din["Wv"].rearrange("(k p) m -> p k m", p=128),
            ).then_inc(s_ld, 16)
            g.dma_start(
                out=wot_sb[:].rearrange("p (k m) -> p k m", k=NDC),
                in_=din["WoT"].rearrange("(k p) m -> p k m", p=128),
            ).then_inc(s_ld, 16)
        block.gpsimd(g)

        def t(tn):
            tn.wait_ge(s_ld, 32)
            n = 0
            for mc in range(NDC):
                for nr in range(2):
                    n += 1
                    if n > 2:
                        tn.wait_ge(s_ev, n - 2)
                    pt = psA if nr == 0 else psB
                    for kc in range(NDC):
                        mm = tn.matmul(
                            pt[:],
                            wv_sb[:, kc * D + mc * 128 : kc * D + (mc + 1) * 128],
                            wot_sb[:, kc * D + nr * 512 : kc * D + nr * 512 + 512],
                            start=(kc == 0), stop=(kc == NDC - 1),
                        )
                    mm.then_inc(s_mm, 1)
        block.tensor(t)

        def v(v):
            n = 0
            for mc in range(NDC):
                if mc >= 2:
                    v.wait_ge(s_st, 16 * (mc - 1))
                for nr in range(2):
                    n += 1
                    v.wait_ge(s_mm, n)
                    v.tensor_copy(
                        mvo_st[:, (mc % 2) * D + nr * 512 : (mc % 2) * D + nr * 512 + 512],
                        (psA if nr == 0 else psB)[:],
                    ).then_inc(s_ev, 1)
        block.vector(v)

        def sy(sy):
            for mc in range(NDC):
                sy.wait_ge(s_ev, 2 * (mc + 1))
                sy.dma_start(
                    out=dr["mvod"][mc * 128 : (mc + 1) * 128, :],
                    in_=mvo_st[:, (mc % 2) * D : (mc % 2) * D + D],
                ).then_inc(s_st, 16)
            sy.wait_ge(s_st, 16 * NDC)
        block.sync(sy)
        return {
            "gpsimd": [(kb.sem("w_cast"), 64), (s_ld, 32)],
            "tensor": [(s_mm, 16)],
            "vector": [(s_ev, 16)],
            "sync": [(s_st, 16 * NDC)],
        }


def emit_ln_tiles(kb, block, st, tag, b, din, dst_writer, want_bf=True):
    nc = kb.nc
    xt = kb.sb(st, tag + "_xt", [128, 2 * D], F32)
    sq = kb.sb(st, tag + "_sq", [128, D], F32)
    xn = kb.sb(st, tag + "_xn", [128, 2 * D], F32)
    rows = kb.sb(st, tag + "_rows", [128, 8], F32)
    idf = kb.sb(st, tag + "_idf", [128, 128], F32)
    psT = [kb.ps(st, tag + f"_psT{i}", [128, 512], F32) for i in range(4)]

    s_x = kb.sem(tag + "_x")
    s_sq = kb.sem(tag + "_sq")
    s_v = kb.sem(tag + "_v")
    s_r = kb.sem(tag + "_r")
    s_xn = kb.sem(tag + "_xn")
    s_tr = kb.sem(tag + "_tr")
    s_ev = kb.sem(tag + "_ev")
    s_idf = kb.sem(tag + "_id")

    def sy(sy):
        sy.dma_start(out=idf[:], in_=din["ident"]).then_inc(s_idf, 16)
        for tc in range(NT):
            if tc >= 2:
                sy.wait_ge(s_xn, tc - 1)
            sy.dma_start(
                out=xt[:, (tc % 2) * D : (tc % 2) * D + D],
                in_=din["xsh"][b, tc * 128 : (tc + 1) * 128, :],
            ).then_inc(s_x, 16)
    block.sync(sy)

    def s(s):
        for tc in range(NT + 1):
            if tc < NT:
                s.wait_ge(s_x, 16 * (tc + 1))
                if tc >= 1:
                    s.wait_ge(s_v, 5 * tc)
                    s.wait_ge(s_xn, tc)
                s.activation(sq[:], xt[:, (tc % 2) * D : (tc % 2) * D + D],
                             AF.Square).then_inc(s_sq, 1)
                s.wait_ge(s_v, 5 * tc + 5)
                s.activation(rows[:, 5:6], rows[:, 4:5], AF.Ln, bias=kb.epses[:, 0:1]
                             ).then_inc(s_r, 1)
                s.wait_ge(s_r, 2 * tc + 1)
                s.activation(rows[:, 6:7], rows[:, 5:6], AF.Exp, scale=-0.5
                             ).then_inc(s_r, 1)
            if tc >= 1:
                for half in range(2):
                    s.wait_ge(s_tr, 2 * (tc - 1) + half + 1)
                    ins = dst_writer(s, tc - 1, half, psT[(2 * (tc - 1) + half) % 4])
                    ins.then_inc(s_ev, 1)
    block.scalar(s)

    def v(v):
        for tc in range(NT):
            xtv = xt[:, (tc % 2) * D : (tc % 2) * D + D]
            v.wait_ge(s_x, 16 * (tc + 1))
            if tc >= 1:
                v.wait_ge(s_xn, tc)
            v.reduce_sum(rows[:, 2:3], xtv, axis=AX.X, negate=True).then_inc(s_v, 1)
            v.wait_ge(s_v, 5 * tc + 1)
            v.tensor_scalar(rows[:, 2:3], rows[:, 2:3], 1.0 / D, None, ALU.mult
                            ).then_inc(s_v, 1)
            v.wait_ge(s_sq, tc + 1)
            v.reduce_sum(rows[:, 1:2], sq[:], axis=AX.X).then_inc(s_v, 1)
            v.wait_ge(s_v, 5 * tc + 3)
            v.tensor_tensor(rows[:, 3:4], rows[:, 2:3], rows[:, 2:3], ALU.mult
                            ).then_inc(s_v, 1)
            v.wait_ge(s_v, 5 * tc + 4)
            if tc >= 1:
                v.wait_ge(s_r, 2 * tc - 1)
            v.scalar_tensor_tensor(
                rows[:, 4:5], rows[:, 1:2], 1.0 / D, rows[:, 3:4],
                ALU.mult, ALU.subtract,
            ).then_inc(s_v, 1)
            v.wait_ge(s_r, 2 * (tc + 1))
            if tc >= 2:
                v.wait_ge(s_tr, 2 * (tc - 1))
            v.tensor_scalar(
                xn[:, (tc % 2) * D : (tc % 2) * D + D], xtv,
                rows[:, 2:3], rows[:, 6:7], ALU.add, ALU.mult,
            ).then_inc(s_xn, 1)
    block.vector(v)

    def t(tn):
        tn.wait_ge(s_idf, 16)
        for tc in range(NT):
            tn.wait_ge(s_xn, tc + 1)
            for half in range(2):
                if 2 * tc + half >= 4:
                    tn.wait_ge(s_ev, 2 * tc + half - 3)
                pt = psT[(2 * tc + half) % 4]
                for j in range(4):
                    dc = 4 * half + j
                    tr = tn.transpose(
                        pt[:, j * 128 : (j + 1) * 128],
                        xn[:, (tc % 2) * D + dc * 128 : (tc % 2) * D + (dc + 1) * 128],
                        idf[:],
                    )
                tr.then_inc(s_tr, 1)
    block.tensor(t)

    return {
        "sync": [(s_x, 16 * NT), (s_idf, 16)],
        "scalar": [(s_sq, NT), (s_r, 2 * NT), (s_ev, 2 * NT)],
        "vector": [(s_v, 5 * NT), (s_xn, NT)],
        "tensor": [(s_tr, 2 * NT)],
    }


_IDF = {}


def din_identf_ap(kb):
    return _IDF[id(kb)]


def phase_P0(kb, block, din, dr, xnT_bf, b):
    with ExitStack() as st:
        def writer(s, tc, half, pt):
            dst = xnT_bf[:].rearrange("p (c t) -> p c t", c=NDC)[
                :, 4 * half : 4 * half + 4, tc * 128 : (tc + 1) * 128
            ]
            return s.copy(dst, pt[:].rearrange("p (c t) -> p c t", c=4))
        flush = emit_ln_tiles(kb, block, st, f"p0{b}", b, din, writer)
    return flush


def phase_A1(kb, block, din, dr, xnT_bf, b):
    nc = kb.nc
    with ExitStack() as st:
        w_sb = kb.sb(st, f"a1w{b}", [128, 2 * NDC * D], BF16)
        mvo_sb = kb.sb(st, f"a1m{b}", [128, NDC * D], BF16)
        stg = kb.sb(st, f"a1s{b}", [128, 4 * 512], BF16)
        psA = kb.ps(st, f"a1pA{b}", [128, 512], F32)
        psB = kb.ps(st, f"a1pB{b}", [128, 512], F32)
        s_w = kb.sem(f"a1w{b}")
        s_mm = kb.sem(f"a1mm{b}")
        s_ev = kb.sem(f"a1ev{b}")
        s_st = kb.sem(f"a1st{b}")

        def g(g):
            for i, nm in enumerate(("WkT", "WqT")):
                g.dma_start(
                    out=w_sb[:, i * NDC * D : (i + 1) * NDC * D].rearrange(
                        "p (k m) -> p k m", k=NDC),
                    in_=din[nm].rearrange("(k p) m -> p k m", p=128),
                ).then_inc(s_w, 16)
            g.dma_start(
                out=mvo_sb[:].rearrange("p (k m) -> p k m", k=NDC),
                in_=dr["mvod"].rearrange("(k p) m -> p k m", p=128),
            ).then_inc(s_w, 16)
        block.gpsimd(g)

        jobs = []
        for i in range(2):
            for mc in range(NDC):
                for tr in range(8):
                    jobs.append(("kq", i, mc, tr))
        for tc in range(NT):
            for nh in range(2):
                jobs.append(("v", 0, tc, nh))

        def t(tn):
            tn.wait_ge(s_w, 48)
            for n, (kind, i, a_, b_) in enumerate(jobs, 1):
                if n > 2:
                    tn.wait_ge(s_ev, n - 2)
                pt = psA if n % 2 else psB
                if kind == "kq":
                    mc, tr = a_, b_
                    for kc in range(NDC):
                        mm = tn.matmul(
                            pt[:],
                            w_sb[:, (i * NDC + kc) * D + mc * 128 : (i * NDC + kc) * D + (mc + 1) * 128],
                            xnT_bf[:, kc * T + tr * 512 : kc * T + tr * 512 + 512],
                            start=(kc == 0), stop=(kc == NDC - 1),
                        )
                else:
                    tc, nh = a_, b_
                    for kc in range(NDC):
                        mm = tn.matmul(
                            pt[:],
                            xnT_bf[:, kc * T + tc * 128 : kc * T + (tc + 1) * 128],
                            mvo_sb[:, kc * D + nh * 512 : kc * D + nh * 512 + 512],
                            start=(kc == 0), stop=(kc == NDC - 1),
                        )
                mm.then_inc(s_mm, 1)
        block.tensor(t)

        def s(s):
            for n, (kind, i, a_, b_) in enumerate(jobs, 1):
                s.wait_ge(s_mm, n)
                if n > 4:
                    s.wait_ge(s_st, 16 * (n - 4))
                pt = psA if n % 2 else psB
                dst = stg[:, ((n - 1) % 4) * 512 : ((n - 1) % 4) * 512 + 512]
                if kind == "kq" and i == 1:
                    ins = s.activation(dst, pt[:], AF.Copy, scale=1.0 / 32.0)
                else:
                    ins = s.copy(dst, pt[:])
                ins.then_inc(s_ev, 1)
        block.scalar(s)

        def sy(sy):
            for n, (kind, i, a_, b_) in enumerate(jobs, 1):
                sy.wait_ge(s_ev, n)
                src = stg[:, ((n - 1) % 4) * 512 : ((n - 1) % 4) * 512 + 512]
                if kind == "kq":
                    mc, tr = a_, b_
                    dstd = (dr["KTd"] if i == 0 else dr["QTd"])[b, mc, :, tr * 512 : tr * 512 + 512]
                else:
                    tc, nh = a_, b_
                    dstd = dr["Vpd"][b, tc * 128 : (tc + 1) * 128, nh * 512 : nh * 512 + 512]
                sy.dma_start(out=dstd, in_=src).then_inc(s_st, 16)
            sy.wait_ge(s_st, 16 * len(jobs))
        block.sync(sy)
        return {
            "gpsimd": [(s_w, 48)],
            "tensor": [(s_mm, len(jobs))],
            "scalar": [(s_ev, len(jobs))],
            "sync": [(s_st, 16 * len(jobs))],
        }



def build(dev=False, phases=("W", "P0", "A1", "A2", "FI", "F0", "F1", "FIN")):
    kb = KB(dev=dev)
    nc = kb.nc

    din = {}
    for nm, shp in [
        ("xsh", [NB, T, D]), ("WqT", [D, D]), ("WkT", [D, D]), ("Wv", [D, D]),
        ("WoT", [D, D]), ("wqkv", [2, D, 3 * INNER]), ("wo", [2, INNER, D]),
        ("w1p", [2, D, H2]), ("b1_pk", [2, 128, 2 * NPAIR]), ("w2", [2, HID, D]),
        ("lng_pk", [2, 128, 8]), ("lnb_pk", [2, 128, 8]), ("b2_pk", [2, 128, 8]),
        ("qscale_bc", [2, 128, INNER]), ("kscale_bc", [2, 128, INNER]),
        ("tbl4", [T, 4 * DH]),
        ("ident", [128, 128]), ("ones_row", [1, 128]), ("ones_col", [128, 1]),
        ("b_diag", [128, 128]), ("b_old", [128, 128]), ("epses", [128, 2]),
        ("ones_nd", [128, 128]), ("ones_pd", [128, 128]),
    ]:
        din[nm] = nc.dram_tensor(nm, shp, F32, kind="ExternalInput").ap()

    out = nc.dram_tensor("out", [NB, T, D], F32, kind="ExternalOutput").ap()

    def scratch(name, shape, dtype):
        kind = "ExternalOutput" if dev else "Internal"
        return nc.dram_tensor(name, shape, dtype, kind=kind).ap()

    dr = {
        "w1p_bf": scratch("w1p_bf", [2, D, H2], BF16),
        "w2_bf": scratch("w2_bf", [2, HID, D], BF16),
        "mvod": scratch("mvod", [D, D], BF16),
        "QTd": scratch("QTd", [NB, NDC, 128, T], BF16),
        "KTd": scratch("KTd", [NB, NDC, 128, T], BF16),
        "Vpd": scratch("Vpd", [NB, T, D], BF16),
        "pbuf": scratch("pbuf", [NB, T, D], F32),
        "tapS": scratch("tapS", [128, T], F32),
        "tapP": scratch("tapP", [128, T], BF16),
        "tapC": scratch("tapC", [128, 4], F32),
        "out": out,
    }

    with kb.stack:
        block = kb.stack.enter_context(nc.Block())
        epses = kb.sb(kb.stack, "epses_s", [128, 2], F32)
        kb.epses = epses
        s_c = kb.sem("cld")

        def g0(g):
            g.dma_start(out=epses[:], in_=din["epses"]).then_inc(s_c, 16)
            g.wait_ge(s_c, 16)
        block.gpsimd(g0)
        kb.barrier(block)

        if "W" in phases:
            fl = phase_W(kb, block, din, dr)
            kb.barrier(block, fl)
            kb.recycle(block, ["w_"])

        for b in range(NB):
            with ExitStack() as stb:
                need_xnT = ("P0" in phases) or ("A1" in phases)
                if need_xnT:
                    xnT_bf = kb.sb(stb, f"xnT{b}", [128, NDC * T], BF16)
                if "P0" in phases:
                    fl = phase_P0(kb, block, din, dr, xnT_bf, b)
                    kb.barrier(block, fl)
                    kb.recycle(block, [f"p0{b}"])
                if "A1" in phases:
                    fl = phase_A1(kb, block, din, dr, xnT_bf, b)
                    kb.barrier(block, fl)
                    kb.recycle(block, ["a1"])
            if "A2" in phases:
                fl = phase_A2(kb, block, din, dr, b)
                kb.barrier(block, fl)
                kb.recycle(block, ["a2"])
            if any(p in phases for p in ("FI", "F0", "F1", "FIN", "F0a", "F1a")):
                with ExitStack() as stf:
                    stream = kb.sb(stf, f"stream{b}", [128, NDC * STR_W], F32)

                    def tap(name, b=b, stream=stream):
                        if not dev or b != 0:
                            return
                        tsr = nc.dram_tensor(f"tap_{name}", [128, NDC * STR_W], F32,
                                             kind="ExternalOutput").ap()
                        s_t = kb.sem(f"tap{name}")

                        def gt(g):
                            g.dma_start(out=tsr, in_=stream[:]).then_inc(s_t, 16)
                        block.gpsimd(gt)
                        kb.barrier(block, {"gpsimd": [(s_t, 16)]})

                    if "FI" in phases:
                        fl = phase_FI(kb, block, din, dr, stream, b)
                        kb.barrier(block, fl)
                        kb.recycle(block, [f"fi{b}"])
                        tap("fi")
                    for l in range(2):
                        if f"F{l}" in phases or f"F{l}a" in phases:
                            fl = phase_Fa(kb, block, din, dr, stream, b, l)
                            kb.barrier(block, fl)
                            kb.recycle(block, [f"fa{b}{l}"])
                            tap(f"f{l}a")
                        if f"F{l}" in phases:
                            fl = phase_Fb(kb, block, din, dr, stream, b, l)
                            kb.barrier(block, fl)
                            kb.recycle(block, [f"fb{b}{l}"])
                            tap(f"f{l}b")
                    if "FIN" in phases:
                        fl = phase_FIN(kb, block, din, dr, stream, b)
                        kb.barrier(block, fl)
                        kb.recycle(block, [f"fin{b}", "tap"])


    nc.finalize()
    return kb, dr



def phase_A2(kb, block, din, dr, b):
    nc = kb.nc
    with ExitStack() as st:
        KT_sb = kb.sb(st, f"a2kt{b}", [128, NDC * T], BF16)
        Vp_sb = kb.sb(st, f"a2vp{b}", [128, NT * D], BF16)
        qt = kb.sb(st, f"a2qt{b}", [128, 2 * NDC * 128], BF16)
        P_bf = kb.sb(st, f"a2p{b}", [128, 2 * T], BF16)
        PT_sb = kb.sb(st, f"a2pt{b}", [128, 2 * T], BF16)
        x0t = kb.sb(st, f"a2x0{b}", [128, 2 * D], F32)
        p_sb = kb.sb(st, f"a2pb{b}", [128, 2 * D], F32)
        bdg = kb.sb(st, f"a2bd{b}", [128, 128], F32)
        idb = kb.sb(st, f"a2ib{b}", [128, 128], BF16)
        cols = kb.sb(st, f"a2c{b}", [128, 8], F32)
        cols_rs = kb.sb(st, f"a2cr{b}", [128, 16], F32)
        ps_s = [kb.ps(st, f"a2ps{b}{i}", [128, 512], F32) for i in range(2)]
        ps_t = [kb.ps(st, f"a2pt{b}{i}", [128, 512], BF16) for i in range(2)]
        ps_o = [kb.ps(st, f"a2po{b}{i}", [128, 512], F32) for i in range(4)]

        s_kv = kb.sem(f"a2kv{b}")
        s_qt = kb.sem(f"a2sqt{b}")
        s_x0 = kb.sem(f"a2sx0{b}")
        s_smm = kb.sem(f"a2smm{b}")
        s_dg = kb.sem(f"a2dg{b}")
        s_rs = kb.sem(f"a2rs{b}")
        s_pe = kb.sem(f"a2pe{b}")
        s_rc = kb.sem(f"a2rc{b}")
        s_ptr = kb.sem(f"a2ptr{b}")
        s_ptev = kb.sem(f"a2pte{b}")
        s_pv = kb.sem(f"a2pv{b}")
        s_pev = kb.sem(f"a2pev{b}")
        s_pst = kb.sem(f"a2pst{b}")

        def R(qc):
            return qc // 4 + 1

        def G(qc):
            return (qc + 4) // 4

        range_meta = []
        smm_at = {-1: 0}
        sev_at = {-1: 0}
        grp_at = {-1: 0, -2: 0}
        tot = 0
        gtot = 0
        for qc in range(NT):
            for r in range(R(qc)):
                range_meta.append((qc, r == R(qc) - 1))
            tot += R(qc)
            smm_at[qc] = tot
            sev_at[qc] = tot
            gtot += G(qc)
            grp_at[qc] = gtot

        def g(g):
            g.dma_start(
                out=KT_sb[:].rearrange("p (c t) -> p c t", c=NDC),
                in_=dr["KTd"][b].transpose([1, 0, 2]),
            ).then_inc(s_kv, 16)
            for hh in range(2):
                g.dma_start(
                    out=Vp_sb[:].rearrange("p (c d) -> p c d", c=NT)[:, hh * 16 : (hh + 1) * 16],
                    in_=dr["Vpd"][b, hh * 2048 : (hh + 1) * 2048, :].rearrange(
                        "(c p) d -> p c d", p=128),
                ).then_inc(s_kv, 16)
            g.dma_start(out=bdg[:], in_=din["b_diag"]).then_inc(s_kv, 16)
            g.dma_start(out=idb[:], in_=din["ident"]).then_inc(s_kv, 16)
        block.gpsimd(g)

        def sy(sy):
            for qc in range(NT):
                if qc >= 2:
                    sy.wait_ge(s_pev, qc - 1)
                sy.dma_start(
                    out=x0t[:, (qc % 2) * D : (qc % 2) * D + D],
                    in_=din["xsh"][b, qc * 128 : (qc + 1) * 128, :],
                ).then_inc(s_x0, 16)
                if qc >= 1:
                    sy.wait_ge(s_pev, qc)
                    sy.dma_start(
                        out=dr["pbuf"][b, (qc - 1) * 128 : qc * 128, :],
                        in_=p_sb[:, ((qc - 1) % 2) * D : ((qc - 1) % 2) * D + D],
                    ).then_inc(s_pst, 16)
            sy.wait_ge(s_pev, NT)
            sy.dma_start(
                out=dr["pbuf"][b, (NT - 1) * 128 : NT * 128, :],
                in_=p_sb[:, ((NT - 1) % 2) * D : ((NT - 1) % 2) * D + D],
            ).then_inc(s_pst, 16)
            sy.wait_ge(s_pst, 16 * NT)
        block.sync(sy)

        def trpv(tn, qc):
            nk = qc + 1
            ngr = G(qc)
            tn.wait_ge(s_pe, smm_at[qc])
            for gi in range(ngr):
                gidx = grp_at[qc - 1] + gi + 1
                if gidx > 2:
                    tn.wait_ge(s_ptev, gidx - 2)
                for j in range(min(4, nk - 4 * gi)):
                    kc = 4 * gi + j
                    tr = tn.transpose(
                        ps_t[(gidx - 1) % 2][:, j * 128 : (j + 1) * 128],
                        P_bf[:, (qc % 2) * T + kc * 128 : (qc % 2) * T + (kc + 1) * 128],
                        idb[:],
                    )
                tr.then_inc(s_ptr, 1)
            if qc >= 2:
                tn.wait_ge(s_pev, qc - 1)
            for gi in range(ngr):
                gidx = grp_at[qc - 1] + gi + 1
                tn.wait_ge(s_ptev, gidx)
                for j in range(min(4, nk - 4 * gi)):
                    kc = 4 * gi + j
                    for nh in range(2):
                        mm = tn.matmul(
                            ps_o[(qc % 2) * 2 + nh][:],
                            PT_sb[:, (qc % 2) * T + kc * 128 : (qc % 2) * T + (kc + 1) * 128],
                            Vp_sb[:, kc * D + nh * 512 : kc * D + nh * 512 + 512],
                            start=(kc == 0), stop=(kc == nk - 1),
                        )
            mm.then_inc(s_pv, 1)

        def t(tn):
            tn.wait_ge(s_kv, 80)
            rtot = 0
            for qc in range(NT):
                nk = qc + 1
                tn.wait_ge(s_qt, 16 * (qc + 1))
                if qc >= 2:
                    tn.wait_ge(s_ptr, grp_at[qc - 2])
                for r in range(R(qc)):
                    rtot += 1
                    if rtot > 2:
                        tn.wait_ge(s_pe, rtot - 2)
                    w = min(512, nk * 128 - r * 512)
                    pt = ps_s[(rtot - 1) % 2]
                    for kc in range(NDC):
                        mm = tn.matmul(
                            pt[:, 0:w],
                            qt[:, (qc % 2) * 1024 + kc * 128 : (qc % 2) * 1024 + (kc + 1) * 128],
                            KT_sb[:, kc * T + r * 512 : kc * T + r * 512 + w],
                            start=(kc == 0), stop=(kc == NDC - 1),
                        )
                    mm.then_inc(s_smm, 1)
                if qc >= 1:
                    trpv(tn, qc - 1)
            trpv(tn, NT - 1)
        block.tensor(t)

        def s(s):
            s.dma_start(
                out=qt[:, (0 % 2) * 1024 : (0 % 2) * 1024 + 1024].rearrange(
                    "p (c t) -> p c t", c=NDC),
                in_=dr["QTd"][b][:, :, 0 * 128 : (0 + 1) * 128].transpose([1, 0, 2]),
            ).then_inc(s_qt, 16)
            s.dma_start(
                out=qt[:, (1 % 2) * 1024 : (1 % 2) * 1024 + 1024].rearrange(
                    "p (c t) -> p c t", c=NDC),
                in_=dr["QTd"][b][:, :, 1 * 128 : (1 + 1) * 128].transpose([1, 0, 2]),
            ).then_inc(s_qt, 16)
            rtot = 0
            for qc in range(NT):
                nk = qc + 1
                if qc >= 2:
                    s.wait_ge(s_ptr, grp_at[qc - 2])
                    s.wait_ge(s_rc, qc - 1)
                for r in range(R(qc)):
                    rtot += 1
                    s.wait_ge(s_smm, rtot)
                    if r == R(qc) - 1:
                        s.wait_ge(s_dg, qc + 1)
                    w = min(512, nk * 128 - r * 512)
                    pt = ps_s[(rtot - 1) % 2]
                    ins = s.activation(
                        P_bf[:, (qc % 2) * T + r * 512 : (qc % 2) * T + r * 512 + w],
                        pt[:, 0:w], AF.Exp,
                        accum_out=cols_rs[:, (qc % 2) * 8 + r : (qc % 2) * 8 + r + 1],
                    )
                    ins.then_inc(s_pe, 1)
                if qc + 2 < NT:
                    s.wait_ge(s_smm, smm_at[qc])
                    s.dma_start(
                        out=qt[:, ((qc + 2) % 2) * 1024 : ((qc + 2) % 2) * 1024 + 1024].rearrange(
                            "p (c t) -> p c t", c=NDC),
                        in_=dr["QTd"][b][:, :, (qc + 2) * 128 : ((qc + 2) + 1) * 128].transpose([1, 0, 2]),
                    ).then_inc(s_qt, 16)
        block.scalar(s)

        def v(v):
            for qc in range(NT):
                nk = qc + 1
                rtot = smm_at[qc]
                v.wait_ge(s_smm, rtot)
                lr = R(qc) - 1
                w = nk * 128 - lr * 512
                pt = ps_s[(rtot - 1) % 2]
                v.tensor_tensor(
                    pt[:, w - 128 : w],
                    pt[:, w - 128 : w], bdg[:], ALU.add,
                ).then_inc(s_dg, 1)
                if qc >= 1:
                    pq = qc - 1
                    for gi in range(G(pq)):
                        gidx = grp_at[pq - 1] + gi + 1
                        v.wait_ge(s_ptr, gidx)
                        if gidx > 2:
                            v.wait_ge(s_pv, pq)
                        wg = min(512, pq * 128 + 128 - gi * 512)
                        ins = v.tensor_copy(
                            PT_sb[:, (pq % 2) * T + gi * 512 : (pq % 2) * T + gi * 512 + wg],
                            ps_t[(gidx - 1) % 2][:, 0:wg],
                        )
                        ins.then_inc(s_ptev, 1)
                    v.wait_ge(s_pv, pq + 1)
                    v.wait_ge(s_x0, 16 * (pq + 1))
                    v.wait_ge(s_rc, pq + 1)
                    if pq >= 2:
                        v.wait_ge(s_pst, 16 * (pq - 1))
                    for nh in range(2):
                        ins = v.scalar_tensor_tensor(
                            p_sb[:, (pq % 2) * D + nh * 512 : (pq % 2) * D + nh * 512 + 512],
                            ps_o[(pq % 2) * 2 + nh][:], cols[:, 4 + pq % 2 : 5 + pq % 2],
                            x0t[:, (pq % 2) * D + nh * 512 : (pq % 2) * D + nh * 512 + 512],
                            ALU.mult, ALU.add,
                        )
                    ins.then_inc(s_pev, 1)
                v.wait_ge(s_pe, smm_at[qc])
                ins = v.reduce_sum(cols[:, 2 + qc % 2 : 3 + qc % 2],
                                   cols_rs[:, (qc % 2) * 8 : (qc % 2) * 8 + R(qc)],
                                   axis=AX.X)
                ins.then_inc(s_rs, 1)
                v.wait_ge(s_rs, qc + 1)
                v.reciprocal(cols[:, 4 + qc % 2 : 5 + qc % 2],
                             cols[:, 2 + qc % 2 : 3 + qc % 2]).then_inc(s_rc, 1)
            pq = NT - 1
            for gi in range(G(pq)):
                gidx = grp_at[pq - 1] + gi + 1
                v.wait_ge(s_ptr, gidx)
                if gidx > 2:
                    v.wait_ge(s_pv, pq)
                wg = min(512, pq * 128 + 128 - gi * 512)
                ins = v.tensor_copy(
                    PT_sb[:, (pq % 2) * T + gi * 512 : (pq % 2) * T + gi * 512 + wg],
                    ps_t[(gidx - 1) % 2][:, 0:wg],
                )
                ins.then_inc(s_ptev, 1)
            v.wait_ge(s_pv, pq + 1)
            v.wait_ge(s_x0, 16 * (pq + 1))
            v.wait_ge(s_rc, pq + 1)
            v.wait_ge(s_pst, 16 * (pq - 1))
            for nh in range(2):
                ins = v.scalar_tensor_tensor(
                    p_sb[:, (pq % 2) * D + nh * 512 : (pq % 2) * D + nh * 512 + 512],
                    ps_o[(pq % 2) * 2 + nh][:], cols[:, 4 + pq % 2 : 5 + pq % 2],
                    x0t[:, (pq % 2) * D + nh * 512 : (pq % 2) * D + nh * 512 + 512],
                    ALU.mult, ALU.add,
                )
            ins.then_inc(s_pev, 1)
        block.vector(v)

        return {
            "gpsimd": [(s_kv, 80)],
            "sync": [(s_pst, 16 * NT)],
            "tensor": [(s_smm, sum(R(q) for q in range(NT))), (s_pv, NT),
                       (s_ptr, grp_at[NT - 1])],
            "scalar": [(s_pe, sum(R(q) for q in range(NT))), (s_qt, 16 * NT)],
            "vector": [(s_pev, NT), (s_ptev, grp_at[NT - 1]), (s_rc, NT),
                       (s_rs, NT), (s_dg, NT)],
        }


def phase_FI(kb, block, din, dr, stream, b):
    with ExitStack() as st:
        s_z = kb.sem(f"fi{b}z")

        def vz(v):
            v.memset(
                stream[:].rearrange("p (c w) -> p c w", c=NDC)[:, :, 0:1], 0.0
            ).then_inc(s_z, 1)
        block.vector(vz)

        def writer(s, tc, half, pt):
            dst = stream[:].rearrange("p (c w) -> p c w", c=NDC)[
                :, 4 * half : 4 * half + 4, 1 + tc * 128 : 1 + (tc + 1) * 128
            ]
            return s.copy(dst, pt[:].rearrange("p (c t) -> p c t", c=4))
        flush = emit_ln_tiles(kb, block, st, f"fi{b}", b, din, writer)
        flush.setdefault("vector", []).append((s_z, 1))
    return flush


FA_NCH = [NT]
FA_DBG = set()


def phase_Fa(kb, block, din, dr, stream, b, l):
    tag = f"fa{b}{l}"
    NCH = FA_NCH[0]
    with ExitStack() as st:
        wqkv_sb = kb.sb(st, tag + "wq", [128, NDC * 3 * INNER], BF16)
        wo_sb = kb.sb(st, tag + "wo", [128, 4 * D], BF16)
        qsc_bc = kb.sb(st, tag + "qs", [128, INNER], BF16)
        ksc_bc = kb.sb(st, tag + "ks", [128, INNER], BF16)
        bmask = kb.sb(st, tag + "bm", [128, 256], F32)
        idb = kb.sb(st, tag + "ib", [128, 128], BF16)
        tbl = kb.sb(st, tag + "tb", [128, 2 * 4 * 64], F32)
        xsh = kb.sb(st, tag + "xs", [128, D], BF16)
        qkf = kb.sb(st, tag + "qk", [128, 2 * INNER], F32)
        nrm = kb.sb(st, tag + "nr", [128, 16], F32)
        rst = kb.sb(st, tag + "rs", [128, 16], F32)
        tmp1 = kb.sb(st, tag + "t1", [128, INNER], F32)
        sqsc = kb.sb(st, tag + "sq", [128, DH], F32)
        qrot = kb.sb(st, tag + "qr", [128, INNER], BF16)
        krot = kb.sb(st, tag + "kr", [128, INNER], BF16)
        qT = kb.sb(st, tag + "qt", [64, HEADS * 128], BF16)
        kring = kb.sb(st, tag + "kg", [64, HEADS * 5 * 128], BF16)
        vring = kb.sb(st, tag + "vg", [128, 5 * 512], BF16)
        Pbn = kb.sb(st, tag + "pb", [128, 3 * 640], BF16)
        PT_sb = kb.sb(st, tag + "pt", [128, 2 * 640], BF16)
        oT_buf = kb.sb(st, tag + "ob", [128, 4 * 512], BF16)
        ncol = kb.sb(st, tag + "nc", [128, 24], F32)
        scr5 = kb.sb(st, tag + "s5", [128, 256], F32)
        stash = kb.sb(st, tag + "sh", [128, 4], F32)

        ps_qkv = kb.ps(st, tag + "pq", [128, 512], F32)
        ps_sa = [kb.ps(st, tag + f"psa{i}", [128, 512], F32) for i in range(2)]
        ps_sbp = kb.ps(st, tag + "psb", [128, 256], F32)
        ps_tr = [kb.ps(st, tag + f"ptr{i}", [128, 1024], BF16) for i in range(2)]
        ps_oT = kb.ps(st, tag + "pot", [128, 512], F32)
        ps_wo = kb.ps(st, tag + "pwo", [128, 512], F32)

        s_ini = kb.sem(tag + "ini")
        s_in2 = kb.sem(tag + "in2")
        s_in3 = kb.sem(tag + "in3")
        s_tb = kb.sem(tag + "tbl")
        s_xc = kb.sem(tag + "xc")
        s_qkvm = kb.sem(tag + "qm")
        s_qev = kb.sem(tag + "qe")
        s_sqq = kb.sem(tag + "q2")
        s_rst = kb.sem(tag + "rt")
        s_rot = kb.sem(tag + "ro")
        s_qktr = kb.sem(tag + "qk2")
        s_trev = kb.sem(tag + "tv")
        s_smm = kb.sem(tag + "sm")
        s_msk = kb.sem(tag + "mk")
        s_peh = kb.sem(tag + "pe")
        s_rci = kb.sem(tag + "rc")
        s_pnb = kb.sem(tag + "pn")
        s_ptr = kb.sem(tag + "pr")
        s_ptev = kb.sem(tag + "pv")
        s_pvm = kb.sem(tag + "pm")
        s_oev = kb.sem(tag + "oe")
        s_wom = kb.sem(tag + "wm")
        s_wov = kb.sem(tag + "wv")
        s_sta = kb.sem(tag + "sa")
        s_spc = kb.sem(tag + "sp")
        s_vv = kb.sem(tag + "vv")

        def nsl(tc):
            return min(tc + 1, 5)

        def spans(tc, h):
            n = nsl(tc)
            segs = [(h * 640 + (tc % 5) * 128, 128, 0)]
            lo = tc - n + 1
            si = 1
            while si < n:
                s0 = (lo + si - 1) % 5
                run = min(n - si, 5 - s0)
                segs.append((h * 640 + s0 * 128, run * 128, si))
                si += run
            return segs

        def g(g):
            g.dma_start(
                out=wqkv_sb[:].rearrange("p (k m) -> p k m", k=NDC),
                in_=din["wqkv"][l].rearrange("(k p) m -> p k m", p=128),
            ).then_inc(s_ini, 16)
            g.dma_start(
                out=wo_sb[:].rearrange("p (k m) -> p k m", k=4),
                in_=din["wo"][l].rearrange("(k p) m -> p k m", p=128),
            ).then_inc(s_ini, 16)
            g.dma_start(out=qsc_bc[:], in_=din["qscale_bc"][l]).then_inc(s_in3, 16)
            g.dma_start(out=ksc_bc[:], in_=din["kscale_bc"][l]).then_inc(s_in3, 16)
            g.dma_start(out=idb[:], in_=din["ident"]).then_inc(s_in3, 16)
            for tc in range(NCH):
                if tc >= 1:
                    g.wait_ge(s_qkvm, 3 * tc)
                bound = tc % 4 == 0 and tc >= 4
                if bound:
                    g.wait_ge(s_sta, tc // 4)
                for dc in range(NDC):
                    off = dc * STR_W + tc * 128 + (1 if dc < 4 else 0)
                    if dc >= 4 and bound:
                        g.tensor_copy(xsh[:, dc * 128 : dc * 128 + 1],
                                      stash[:, dc - 4 : dc - 3])
                        ins = g.tensor_copy(
                            xsh[:, dc * 128 + 1 : (dc + 1) * 128],
                            stream[:, off + 1 : off + 128])
                    else:
                        ins = g.tensor_copy(
                            xsh[:, dc * 128 : (dc + 1) * 128],
                            stream[:, off : off + 128])
                ins.then_inc(s_xc, 1)
        block.gpsimd(g)

        def sy(sy):
            sy.dma_start(out=bmask[:, 0:128], in_=din["b_diag"]).then_inc(s_in2, 16)
            sy.dma_start(out=bmask[:, 128:256], in_=din["b_old"]).then_inc(s_in2, 16)
            for gi in range(NCH):
                if gi >= 2:
                    sy.wait_ge(s_rot, 2 * (gi - 1))
                sy.dma_start(
                    out=tbl[:, (gi % 2) * 256 : (gi % 2) * 256 + 256],
                    in_=din["tbl4"][gi * 128 : (gi + 1) * 128, :],
                ).then_inc(s_tb, 16)
        block.sync(sy)

        def t(tn):
            tn.wait_ge(s_ini, 32)
            tn.wait_ge(s_in3, 48)
            pend_wo = []
            pend_pp = []

            def emit_wo(k):
                for _ in range(k):
                    if pend_wo:
                        pend_wo.pop(0)()

            def ptrpv(ptc, h):
                pn = nsl(ptc)
                if FA_DBG & {"nosbp", "nosbpexp"}:
                    pn = min(pn, 4)
                gh = ptc * HEADS + h
                p3 = (gh % 3) * 640
                p2 = (gh % 2) * 640
                trb = ps_tr[gh % 2]
                tn.wait_ge(s_pnb, gh + 1)
                if gh >= 2:
                    tn.wait_ge(s_ptev, gh - 1)
                if h == 0:
                    tn.wait_ge(s_qev, 3 * (ptc + 1))
                for si in range(pn):
                    tr = tn.transpose(
                        trb[:, si * 128 : (si + 1) * 128],
                        Pbn[:, p3 + si * 128 : p3 + (si + 1) * 128], idb[:])
                tr.then_inc(s_ptr, 1)
                tn.wait_ge(s_ptev, gh + 1)
                if h == 0 and ptc >= 1:
                    tn.wait_ge(s_oev, ptc)
                for si in range(pn):
                    s_ = ptc if si == 0 else ptc - pn + si
                    tp = (0, 64) if h % 2 else None
                    mm = tn.matmul(
                        ps_oT[64 * (h % 2) : 64 * (h % 2) + 64,
                              (h // 2) * 128 : (h // 2) * 128 + 128],
                        vring[:, (s_ % 5) * 512 + h * 64 : (s_ % 5) * 512 + (h + 1) * 64],
                        PT_sb[:, p2 + si * 128 : p2 + (si + 1) * 128],
                        start=(si == 0), stop=(si == pn - 1),
                        tile_position=tp)
                mm.then_inc(s_pvm, 1)

            for tc in range(NCH):
                gh0 = tc * HEADS
                n = nsl(tc)
                tn.wait_ge(s_xc, tc + 1)
                if tc >= 1:
                    tn.wait_ge(s_qev, 3 * tc)
                for seg in range(3):
                    if seg >= 1:
                        tn.wait_ge(s_qev, 3 * tc + seg)
                    for dc in range(NDC):
                        mm = tn.matmul(
                            ps_qkv[:], xsh[:, dc * 128 : (dc + 1) * 128],
                            wqkv_sb[:, dc * 1536 + seg * 512 : dc * 1536 + (seg + 1) * 512],
                            start=(dc == 0), stop=(dc == NDC - 1))
                    mm.then_inc(s_qkvm, 1)
                    if pend_pp:
                        ptrpv(*pend_pp.pop(0))
                while pend_pp:
                    ptrpv(*pend_pp.pop(0))
                emit_wo(8)
                tn.wait_ge(s_rot, 2 * (tc + 1))
                if tc >= 1:
                    tn.wait_ge(s_trev, 4 * tc)
                    tn.wait_ge(s_ptev, 8 * tc)
                for ki, buf in enumerate((qrot, krot)):
                    for h in range(HEADS):
                        tp = (0, 64) if h % 2 else None
                        tr = tn.transpose(
                            ps_tr[ki][64 * (h % 2) : 64 * (h % 2) + 64,
                                      (h // 2) * 128 : (h // 2) * 128 + 128],
                            buf[:, h * 64 : (h + 1) * 64], idb[:],
                            tile_position=tp)
                tr.then_inc(s_qktr, 1)
                for h in range(HEADS):
                    gh = gh0 + h
                    if gh >= 2:
                        tn.wait_ge(s_peh, gh - 1)
                    if h == 0:
                        tn.wait_ge(s_trev, 4 * (tc + 1))
                    sb0 = (gh % 2) * 128
                    for col0, ncols, si0 in spans(tc, h):
                        if "nosbp" in FA_DBG and si0 + ncols // 128 > 4:
                            ncols -= 128
                            if ncols == 0:
                                continue
                        c = col0
                        rem = ncols
                        si = si0
                        while rem > 0:
                            if si < 4:
                                take = min(rem, (4 - si) * 128)
                                dst = ps_sa[gh % 2][:, si * 128 : si * 128 + take]
                            else:
                                take = rem
                                dst = ps_sbp[:, sb0 : sb0 + take]
                            mm = tn.matmul(
                                dst, qT[:, h * 128 : (h + 1) * 128],
                                kring[:, c : c + take],
                                start=True, stop=True)
                            c += take
                            si += take // 128
                            rem -= take
                    mm.then_inc(s_smm, 1)
                    pend_pp.append((tc, h))
                    while len(pend_pp) > 2:
                        ptrpv(*pend_pp.pop(0))
                if tc == NCH - 1:
                    while pend_pp:
                        ptrpv(*pend_pp.pop(0))
                if tc % 4 == 3:
                    gi = tc // 4

                    def mk(dc, gi=gi):
                        def go():
                            tn.wait_ge(s_oev, 4 * gi + 4)
                            if dc == 0:
                                if gi >= 1:
                                    tn.wait_ge(s_wov, 8 * gi)
                            else:
                                tn.wait_ge(s_wov, 8 * gi + dc)
                            for kc in range(4):
                                mm = tn.matmul(
                                    ps_wo[:],
                                    wo_sb[:, kc * D + dc * 128 : kc * D + (dc + 1) * 128],
                                    oT_buf[:, kc * 512 : (kc + 1) * 512],
                                    start=(kc == 0), stop=(kc == 3))
                            mm.then_inc(s_wom, 1)
                        return go
                    for dc in range(NDC):
                        pend_wo.append(mk(dc))
            emit_wo(len(pend_wo))
        block.tensor(t)

        spc_n = [0]

        def s(s):
            for tc in range(NCH):
                gh0 = tc * HEADS
                n = nsl(tc)
                s.wait_ge(s_qkvm, 3 * tc + 1)
                if tc >= 1:
                    s.wait_ge(s_rot, 2 * tc)
                ins = s.copy(qkf[:, 0:512], ps_qkv[:])
                ins.then_inc(s_qev, 1)
                s.wait_ge(s_qkvm, 3 * tc + 2)
                ins = s.copy(qkf[:, 512:1024], ps_qkv[:])
                ins.then_inc(s_qev, 1)
                s.wait_ge(s_qkvm, 3 * tc + 3)
                if tc >= 1:
                    s.wait_ge(s_pvm, 8 * tc)
                ins = s.copy(vring[:, (tc % 5) * 512 : (tc % 5) * 512 + 512], ps_qkv[:])
                ins.then_inc(s_qev, 1)
                s.wait_ge(s_qev, 3 * tc + 2)
                for hh in range(16):
                    ins = s.activation(sqsc[:], qkf[:, hh * 64 : (hh + 1) * 64],
                                       AF.Square, accum_out=nrm[:, hh : hh + 1])
                ins.then_inc(s_sqq, 1)
                s.wait_ge(s_sqq, tc + 1)
                s.activation(rst[:], nrm[:], AF.Ln, bias=kb.epses[:, 1:2]).then_inc(s_rst, 1)
                s.wait_ge(s_rst, 2 * tc + 1)
                s.activation(rst[:], rst[:], AF.Exp, scale=-0.5).then_inc(s_rst, 1)
                for h in range(HEADS):
                    gh = gh0 + h
                    p3 = (gh % 3) * 640
                    nb = (gh % 3) * 8
                    sb0 = (gh % 2) * 128
                    sbp_on = n == 5 and not (FA_DBG & {"nosbp", "nosbpexp"})
                    if sbp_on:
                        s.wait_ge(s_smm, gh + 1)
                        ins = s.copy(scr5[:, sb0 : sb0 + 128],
                                     ps_sbp[:, sb0 : sb0 + 128])
                        ins.then_inc(s_spc, 1)
                        spc_n[0] += 1
                    s.wait_ge(s_msk, gh + 1)
                    if gh >= 3:
                        s.wait_ge(s_ptr, gh - 2)
                    na = min(n, 4) * 128
                    ins = s.activation(
                        Pbn[:, p3 : p3 + na], ps_sa[gh % 2][:, 0:na], AF.Exp,
                        accum_out=ncol[:, nb : nb + 1])
                    if sbp_on:
                        s.wait_ge(s_spc, spc_n[0])
                        ins = s.activation(
                            Pbn[:, p3 + 512 : p3 + 640],
                            scr5[:, sb0 : sb0 + 128], AF.Exp,
                            accum_out=ncol[:, nb + 1 : nb + 2])
                    ins.then_inc(s_peh, 1)
        block.scalar(s)

        vv = [0]

        def v(v):
            pend_pt = []

            def flush_pt(k):
                for _ in range(k):
                    if pend_pt:
                        ptc, pgh = pend_pt.pop(0)
                        pn = nsl(ptc)
                        p2v = (pgh % 2) * 640
                        v.wait_ge(s_ptr, pgh + 1)
                        if pgh >= 2:
                            v.wait_ge(s_pvm, pgh - 1)
                        ins = v.tensor_copy(PT_sb[:, p2v : p2v + pn * 128],
                                            ps_tr[pgh % 2][:, 0 : pn * 128])
                        ins.then_inc(s_ptev, 1)

            def VI(ins):
                vv[0] += 1
                ins.then_inc(s_vv, 1)
            v.wait_ge(s_in2, 32)
            v.wait_ge(s_in3, 32)
            rot_end = {}

            pend_add = []

            def drain_add(k):
                for _ in range(k):
                    if pend_add:
                        pend_add.pop(0)()

            def tail(ptc):
                v.wait_ge(s_pvm, 8 * (ptc + 1))
                ins = v.tensor_copy(
                    oT_buf[:].rearrange("p (j g x) -> p j g x", j=4, g=4)[
                        :, :, ptc % 4, :],
                    ps_oT[:].rearrange("p (j x) -> p j x", j=4))
                ins.then_inc(s_oev, 1)
                if ptc % 4 == 3:
                    gi = ptc // 4

                    def mka(dc, gi=gi):
                        def go():
                            v.wait_ge(s_wom, 8 * gi + dc + 1)
                            sl = stream[:, dc * STR_W + 1 + gi * 512 :
                                        dc * STR_W + 1 + (gi + 1) * 512]
                            ins = v.tensor_tensor(sl, sl, ps_wo[:], ALU.add)
                            ins.then_inc(s_wov, 1)
                        return go
                    for dc in range(NDC):
                        pend_add.append(mka(dc))

            for tc in range(NCH):
                gh0 = tc * HEADS
                n = nsl(tc)
                flush_pt(len(pend_pt))
                if tc >= 1:
                    tail(tc - 1)
                if tc >= 1:
                    v.wait_ge(s_vv, rot_end[tc - 1])
                    v.wait_ge(s_rot, 2 * tc)
                v.wait_ge(s_rst, 2 * (tc + 1))
                rq = rst[:, 0:8].unsqueeze(2).broadcast_to([128, 8, 64]
                     ).rearrange("p h (two half) -> p h two half", two=2)
                rk = rst[:, 8:16].unsqueeze(2).broadcast_to([128, 8, 64]
                     ).rearrange("p h (two half) -> p h two half", two=2)

                def q4(x):
                    return x.rearrange("p (h two half) -> p h two half", h=8, two=2)

                def sw4(x):
                    return q4(x)[:, :, ::-1, :]

                def tb4(ts):
                    return ts.rearrange("p (two half) -> p two half", two=2
                           ).unsqueeze(1).broadcast_to([128, 8, 2, 32])
                VI(v.tensor_tensor(q4(qkf[:, 0:512]), q4(qkf[:, 0:512]), rq, ALU.mult))
                drain_add(1)
                VI(v.tensor_tensor(q4(qkf[:, 512:1024]), q4(qkf[:, 512:1024]), rk, ALU.mult))
                drain_add(1)
                v.wait_ge(s_vv, vv[0])
                VI(v.tensor_tensor(q4(qkf[:, 0:512]), q4(qkf[:, 0:512]), q4(qsc_bc[:]), ALU.mult))
                drain_add(1)
                VI(v.tensor_tensor(q4(qkf[:, 512:1024]), q4(qkf[:, 512:1024]), q4(ksc_bc[:]), ALU.mult))
                drain_add(1)
                v.wait_ge(s_tb, 16 * (tc + 1))
                tb0 = (tc % 2) * 256
                cq = tb4(tbl[:, tb0 + 0 : tb0 + 64])
                sq_ = tb4(tbl[:, tb0 + 64 : tb0 + 128])
                ck = tb4(tbl[:, tb0 + 128 : tb0 + 192])
                sk = tb4(tbl[:, tb0 + 192 : tb0 + 256])
                v.wait_ge(s_vv, vv[0])
                VI(v.tensor_tensor(q4(tmp1[:]), sw4(qkf[:, 0:512]), sq_, ALU.mult))
                drain_add(1)
                v.wait_ge(s_vv, vv[0])
                VI(v.tensor_tensor(q4(qkf[:, 0:512]), q4(qkf[:, 0:512]), cq, ALU.mult))
                drain_add(1)
                v.wait_ge(s_vv, vv[0])
                ins = v.tensor_add(qrot[:], tmp1[:], qkf[:, 0:512])
                ins.then_inc(s_rot, 1)
                drain_add(1)
                VI(v.tensor_tensor(q4(tmp1[:]), sw4(qkf[:, 512:1024]), sk, ALU.mult))
                v.wait_ge(s_vv, vv[0])
                VI(v.tensor_tensor(q4(qkf[:, 512:1024]), q4(qkf[:, 512:1024]), ck, ALU.mult))
                drain_add(1)
                v.wait_ge(s_vv, vv[0])
                ins = v.tensor_add(krot[:], tmp1[:], qkf[:, 512:1024])
                ins.then_inc(s_rot, 1)
                rot_end[tc] = vv[0]
                v.wait_ge(s_qktr, tc + 1)
                v.wait_ge(s_smm, 8 * tc)
                qTv = qT[:].rearrange("p (h x) -> p h x", h=HEADS)
                krv = kring[:].rearrange("p (h s x) -> p h s x", h=HEADS, s=5)[
                    :, :, tc % 5, :]
                ins = v.tensor_copy(qTv[:, 0::2, :],
                                    ps_tr[0][0:64, 0:512].rearrange("p (j x) -> p j x", j=4))
                ins.then_inc(s_trev, 1)
                ins = v.tensor_copy(qTv[:, 1::2, :],
                                    ps_tr[0][64:128, 0:512].rearrange("p (j x) -> p j x", j=4))
                ins.then_inc(s_trev, 1)
                ins = v.tensor_copy(krv[:, 0::2, :],
                                    ps_tr[1][0:64, 0:512].rearrange("p (j x) -> p j x", j=4))
                ins.then_inc(s_trev, 1)
                ins = v.tensor_copy(krv[:, 1::2, :],
                                    ps_tr[1][64:128, 0:512].rearrange("p (j x) -> p j x", j=4))
                ins.then_inc(s_trev, 1)

                def vpartB(pgh):
                    nb = (pgh % 3) * 8
                    p3 = (pgh % 3) * 640
                    pn = nsl(pgh // HEADS)
                    if FA_DBG & {"nosbp", "nosbpexp"}:
                        pn = min(pn, 4)
                    v.wait_ge(s_peh, pgh + 1)
                    if pn == 5:
                        VI(v.tensor_add(ncol[:, nb : nb + 1], ncol[:, nb : nb + 1],
                                        ncol[:, nb + 1 : nb + 2]))
                        v.wait_ge(s_vv, vv[0])
                    ins = v.reciprocal(ncol[:, nb + 2 : nb + 3],
                                       ncol[:, nb : nb + 1])
                    ins.then_inc(s_rci, 1)
                    v.wait_ge(s_rci, pgh + 1)
                    ins = v.tensor_scalar(
                        Pbn[:, p3 : p3 + pn * 128],
                        Pbn[:, p3 : p3 + pn * 128],
                        ncol[:, nb + 2 : nb + 3], None, ALU.mult)
                    ins.then_inc(s_pnb, 1)

                mw = 256 if (tc >= 4 and "nobold" not in FA_DBG) else 128
                for h in range(HEADS):
                    gh = gh0 + h
                    v.wait_ge(s_smm, gh + 1)
                    ins = v.tensor_tensor(ps_sa[gh % 2][:, 0:mw],
                                          ps_sa[gh % 2][:, 0:mw],
                                          bmask[:, 0:mw], ALU.add)
                    ins.then_inc(s_msk, 1)
                    if h >= 1:
                        vpartB(gh - 1)
                    pend_pt.append((tc, gh))
                    flush_pt(len(pend_pt) - 2)
                vpartB(gh0 + HEADS - 1)
                if tc % 4 == 3 and tc < NCH - 1:
                    gi = tc // 4
                    for dc in range(4, 8):
                        ins = v.tensor_copy(
                            stash[:, dc - 4 : dc - 3],
                            stream[:, dc * STR_W + (gi + 1) * 512 :
                                   dc * STR_W + (gi + 1) * 512 + 1])
                    ins.then_inc(s_sta, 1)
                if tc == NCH - 1:
                    flush_pt(len(pend_pt))
                    tail(tc)
                    drain_add(len(pend_add))
        block.vector(v)

        ngr = NCH // 4
        return {
            "gpsimd": [(s_xc, NCH), (s_ini, 32), (s_in3, 48)],
            "sync": [(s_tb, 16 * NCH), (s_in2, 32)],
            "tensor": [(s_qkvm, 3 * NCH), (s_qktr, NCH), (s_smm, 8 * NCH),
                       (s_wom, 8 * ngr), (s_pvm, 8 * NCH), (s_ptr, 8 * NCH)],
            "scalar": [(s_qev, 3 * NCH), (s_sqq, NCH), (s_rst, 2 * NCH),
                       (s_peh, 8 * NCH), (s_ptev, 8 * NCH), (s_spc, spc_n[0])],
            "vector": [(s_rot, 2 * NCH), (s_trev, 4 * NCH),
                       (s_msk, 8 * NCH), (s_sta, ngr - 1), (s_vv, vv[0]),
                       (s_wov, 8 * ngr), (s_oev, NCH), (s_pnb, 8 * NCH),
                       (s_rci, 8 * NCH)],
        }


def phase_Fb(kb, block, din, dr, stream, b, l):
    tag = f"fb{b}{l}"
    NTR = 8
    with ExitStack() as st:
        h2bf = kb.sb(st, tag + "h2", [128, NDC * 512], BF16)
        z_bf = kb.sb(st, tag + "zb", [128, NDC * 512], BF16)
        sq2 = kb.sb(st, tag + "sq", [128, NDC * 512], BF16)
        R1sb = kb.sb(st, tag + "r1", [128, 512], F32)
        R2sb = kb.sb(st, tag + "r2", [128, 512], F32)
        musb = kb.sb(st, tag + "mu", [128, 512], F32)
        ztmp = kb.sb(st, tag + "zt", [128, 512], F32)
        btmp = musb
        h4_buf = kb.sb(st, tag + "h4", [128, NPAIR * 512], BF16)
        w1t = kb.sb(st, tag + "w1", [128, 3 * NDC * 256], BF16)
        w2t = kb.sb(st, tag + "w2", [128, 2 * NPAIR * 128], BF16)
        gel = kb.sb(st, tag + "ge", [128, 2 * 512], BF16)
        b1c = kb.sb(st, tag + "b1", [128, 2 * NPAIR], F32)
        stash = kb.sb(st, tag + "sh", [128, 4], F32)
        lngc = kb.sb(st, tag + "lg", [128, 8], F32)
        lnbc = kb.sb(st, tag + "lb", [128, 8], F32)
        b2c = kb.sb(st, tag + "b2", [128, 8], F32)
        onb_nd = kb.sb(st, tag + "on", [128, 128], BF16)

        ps_mu = kb.ps(st, tag + "pmu", [128, 512], F32)
        ps_ms = kb.ps(st, tag + "pms", [128, 512], F32)
        ps_bc1 = kb.ps(st, tag + "pb1", [128, 512], F32)
        ps_bc2 = kb.ps(st, tag + "pb2", [128, 512], F32)
        ps_a = kb.ps(st, tag + "pa0", [128, 512], F32)
        ps_g = kb.ps(st, tag + "pg0", [128, 512], F32)
        ps_a2 = kb.ps(st, tag + "pa1", [128, 512], F32)
        ps_g2 = kb.ps(st, tag + "pg1", [128, 512], F32)

        s_ini = kb.sem(tag + "ini")
        s_in2 = kb.sem(tag + "in2")
        s_h2c = kb.sem(tag + "hc")
        s_sq2 = kb.sem(tag + "sq")
        s_stm = kb.sem(tag + "st")
        s_msv = kb.sem(tag + "mv")
        s_rld = kb.sem(tag + "rl")
        s_r2 = kb.sem(tag + "m2")
        s_z = kb.sem(tag + "zz")
        s_w1 = kb.sem(tag + "w1")
        s_h3m = kb.sem(tag + "h3")
        s_gel = kb.sem(tag + "gl")
        s_h4 = kb.sem(tag + "h4")
        s_w2 = kb.sem(tag + "w2")
        s_xm = kb.sem(tag + "xm")
        s_xadd = kb.sem(tag + "xa")
        s_stb = kb.sem(tag + "sb")

        def sy(sy):
            sy.dma_start(out=b1c[:], in_=din["b1_pk"][l]).then_inc(s_ini, 16)
            sy.dma_start(out=lngc[:], in_=din["lng_pk"][l]).then_inc(s_ini, 16)
            sy.dma_start(out=lnbc[:], in_=din["lnb_pk"][l]).then_inc(s_ini, 16)
            sy.dma_start(out=b2c[:], in_=din["b2_pk"][l]).then_inc(s_ini, 16)
        block.sync(sy)

        def g(g):
            g.dma_start(out=onb_nd[:], in_=din["ones_nd"]).then_inc(s_in2, 16)
            def cast_op(tr):
                if tr >= 1:
                    g.wait_ge(s_stm, 2 * tr)
                    g.wait_ge(s_sq2, tr)
                    g.wait_ge(s_stb, tr)
                h2v = h2bf[:].rearrange("p (d x) -> p d x", d=NDC)
                stv = stream[:].rearrange("p (d w) -> p d w", d=NDC)
                g.tensor_copy(h2v[:, 0:4, :],
                              stv[:, 0:4, tr * 512 + 1 : tr * 512 + 513])
                if tr >= 1:
                    g.tensor_copy(
                        h2v[:, 4:8, 0:1],
                        stash[:].rearrange("p (d x) -> p d x", d=4))
                    ins = g.tensor_copy(
                        h2v[:, 4:8, 1:512],
                        stv[:, 4:8, tr * 512 + 1 : tr * 512 + 512])
                else:
                    ins = g.tensor_copy(h2v[:, 4:8, :],
                                        stv[:, 4:8, 0:512])
                ins.then_inc(s_h2c, 1)

            def w1_loads(tr, lo=0, hi=NPAIR):
                for kc in range(lo, hi):
                    sz = PAIR_SZ[kc]
                    gk = tr * NPAIR + kc
                    if gk >= 3:
                        g.wait_ge(s_h3m, gk - 2)
                    g.dma_start(
                        out=w1t[:, ((gk % 3) * NDC) * 256 : ((gk % 3) * NDC) * 256 + NDC * 2 * sz
                                ].rearrange("p (c m) -> p c m", c=NDC),
                        in_=dr["w1p_bf"][l].rearrange("(c p) m -> p c m", p=128)[
                            :, :, CH_OFF[kc] : CH_OFF[kc] + 2 * sz],
                    ).then_inc(s_w1, 16)

            cast_op(0)
            w1_loads(0, 0, 14)
            for tr in range(NTR):
                if tr < NTR - 1:
                    cast_op(tr + 1)
                w1_loads(tr, 14, NPAIR)
                if tr < NTR - 1:
                    w1_loads(tr + 1, 0, 3)
                for dc in range(NDC):
                    if tr * NDC + dc >= 2:
                        g.wait_ge(s_xm, tr * NDC + dc - 1)
                    g.dma_start(
                        out=w2t[:, (dc % 2) * NPAIR * 128 : (dc % 2) * NPAIR * 128 + 21 * 128
                                ].rearrange("p (c m) -> p c m", c=21),
                        in_=dr["w2_bf"][l][0:2688, dc * 128 : (dc + 1) * 128].rearrange(
                            "(c p) m -> p c m", p=128),
                    ).then_inc(s_w2, 16)
                    g.dma_start(
                        out=w2t[0:42, (dc % 2) * NPAIR * 128 + 21 * 128 : (dc % 2) * NPAIR * 128 + 22 * 128],
                        in_=dr["w2_bf"][l][2688:2730, dc * 128 : (dc + 1) * 128],
                    ).then_inc(s_w2, 16)
                if tr < NTR - 1:
                    w1_loads(tr + 1, 3, 14)
        block.gpsimd(g)

        def t(tn):
            tn.wait_ge(s_ini, 64)
            tn.wait_ge(s_in2, 16)

            def stats_mm(T):
                tn.wait_ge(s_h2c, T + 1)
                if T >= 1:
                    tn.wait_ge(s_msv, 3 * T - 2)
                for dc in range(NDC):
                    mm = tn.matmul(ps_mu[:], onb_nd[:],
                                   h2bf[:, dc * 512 : (dc + 1) * 512],
                                   start=(dc == 0), stop=(dc == NDC - 1))
                mm.then_inc(s_stm, 1)
                tn.wait_ge(s_sq2, T + 1)
                if T >= 1:
                    tn.wait_ge(s_rld, 2 * T - 1)
                for dc in range(NDC):
                    mm = tn.matmul(ps_ms[:], onb_nd[:],
                                   sq2[:, dc * 512 : (dc + 1) * 512],
                                   start=(dc == 0), stop=(dc == NDC - 1))
                mm.then_inc(s_stm, 1)

            stats_mm(0)
            for tr in range(NTR):
                for kc in range(NPAIR):
                    sz = PAIR_SZ[kc]
                    gk = tr * NPAIR + kc
                    tn.wait_ge(s_w1, 16 * (gk + 1))
                    tn.wait_ge(s_z, 8 * (tr + 1))
                    if gk >= 2:
                        tn.wait_ge(s_h4, gk - 1)
                    pa = ps_a if kc % 2 == 0 else ps_a2
                    pg = ps_g if kc % 2 == 0 else ps_g2
                    for dc in range(NDC):
                        base = (gk % 3) * NDC * 256 + dc * 2 * sz
                        tn.matmul(pa[0:sz, :], w1t[:, base : base + sz],
                                  z_bf[:, dc * 512 : (dc + 1) * 512],
                                  start=(dc == 0), stop=(dc == NDC - 1))
                        mm = tn.matmul(pg[0:sz, :], w1t[:, base + sz : base + 2 * sz],
                                       z_bf[:, dc * 512 : (dc + 1) * 512],
                                       start=(dc == 0), stop=(dc == NDC - 1))
                    mm.then_inc(s_h3m, 1)
                if tr < NTR - 1:
                    stats_mm(tr + 1)
                for dc in range(NDC):
                    gd = tr * NDC + dc
                    tn.wait_ge(s_w2, 32 * (gd + 1))
                    px = ps_bc1 if dc % 2 == 0 else ps_bc2
                    if dc >= 2:
                        tn.wait_ge(s_xadd, 8 * tr + dc - 1)
                    for kc in range(NPAIR):
                        sz = PAIR_SZ[kc]
                        tn.wait_ge(s_h4, tr * NPAIR + kc + 1)
                        mm = tn.matmul(
                            px[:],
                            w2t[0:sz, (dc % 2) * NPAIR * 128 + kc * 128 : (dc % 2) * NPAIR * 128 + (kc + 1) * 128],
                            h4_buf[0:sz, kc * 512 : (kc + 1) * 512],
                            start=(kc == 0), stop=(kc == NPAIR - 1),
                        )
                    mm.then_inc(s_xm, 1)
        block.tensor(t)

        def s(s):
            def sq_op(T):
                s.wait_ge(s_h2c, T + 1)
                if T >= 1:
                    s.wait_ge(s_stm, 2 * T)
                s.activation(sq2[:], h2bf[:], AF.Square).then_inc(s_sq2, 1)

            def rstd_op(T):
                s.wait_ge(s_msv, 3 * (T + 1))
                if T >= 1:
                    s.wait_ge(s_z, 8 * T)
                s.activation(R1sb[:], ps_ms[:], AF.Ln, scale=-1.0,
                             bias=kb.epses[:, 0:1]).then_inc(s_rld, 1)
                s.wait_ge(s_rld, 2 * T + 1)
                s.activation(R1sb[:], R1sb[:], AF.Exp,
                             scale=-0.5).then_inc(s_rld, 1)

            sq_op(0)
            rstd_op(0)
            for tr in range(NTR):
                for kc in range(NPAIR):
                    sz = PAIR_SZ[kc]
                    gk = tr * NPAIR + kc
                    if kc == 10 and tr < NTR - 1:
                        sq_op(tr + 1)
                    s.wait_ge(s_h3m, gk + 1)
                    if gk >= 2:
                        s.wait_ge(s_h4, gk - 1)
                    pg = ps_g if kc % 2 == 0 else ps_g2
                    s.activation(gel[0:sz, (kc % 2) * 512 : (kc % 2) * 512 + 512],
                                 pg[0:sz, :], AF.Gelu,
                                 bias=b1c[0:sz, 2 * kc + 1 : 2 * kc + 2]).then_inc(s_gel, 1)
                if tr < NTR - 1:
                    rstd_op(tr + 1)
        block.scalar(s)

        def v(v):
            def stats_vec(T):
                v.wait_ge(s_stm, 2 * (T + 1))
                if T >= 1:
                    v.wait_ge(s_r2, T)
                    v.wait_ge(s_z, NDC * T)
                ins = v.tensor_copy(musb[:], ps_mu[:])
                ins.then_inc(s_msv, 1)
                v.wait_ge(s_msv, 3 * T + 1)
                ins = v.tensor_tensor(ztmp[:], musb[:], musb[:], ALU.mult)
                ins.then_inc(s_msv, 1)
                v.wait_ge(s_msv, 3 * T + 2)
                ins = v.tensor_tensor(ps_ms[:], ps_ms[:], ztmp[:], ALU.add)
                ins.then_inc(s_msv, 1)

            def r2_op(T):
                v.wait_ge(s_rld, 2 * (T + 1))
                ins = v.tensor_tensor(R2sb[:], musb[:], R1sb[:], ALU.mult)
                ins.then_inc(s_r2, 1)

            def z_op(T):
                v.wait_ge(s_rld, 2 * (T + 1))
                v.wait_ge(s_r2, T + 1)
                if T >= 1:
                    v.wait_ge(s_h3m, NPAIR * T)
                for dc in range(NDC):
                    gz = T * NDC + dc
                    off = dc * STR_W + T * 512 + (1 if dc < 4 else 0)
                    if dc >= 4 and T >= 1:
                        v.scalar_tensor_tensor(
                            ztmp[:, 0:1], stash[:, dc - 4 : dc - 3],
                            lngc[:, dc : dc + 1], R1sb[:, 0:1], ALU.mult, ALU.mult)
                        i1 = v.scalar_tensor_tensor(
                            ztmp[:, 1:512], stream[:, off + 1 : off + 512],
                            lngc[:, dc : dc + 1], R1sb[:, 1:512], ALU.mult, ALU.mult)
                    else:
                        i1 = v.scalar_tensor_tensor(
                            ztmp[:], stream[:, off : off + 512], lngc[:, dc : dc + 1],
                            R1sb[:], ALU.mult, ALU.mult)
                    i2 = v.tensor_scalar(btmp[:], R2sb[:], lngc[:, dc : dc + 1],
                                         lnbc[:, dc : dc + 1], ALU.mult, ALU.add)
                    i2.then_inc(s_zt, 1)
                    v.wait_ge(s_zt, gz + 1)
                    ins = v.tensor_tensor(z_bf[:, dc * 512 : (dc + 1) * 512],
                                          ztmp[:], btmp[:], ALU.add)
                    ins.then_inc(s_z, 1)

            s_zt = kb.sem(tag + "ztm")
            stats_vec(0)
            r2_op(0)
            z_op(0)
            for tr in range(NTR):
                if tr < NTR - 1:
                    v.wait_ge(s_h2c, tr + 1)
                    for dc in range(4, 8):
                        ins = v.tensor_copy(
                            stash[:, dc - 4 : dc - 3],
                            stream[:, dc * STR_W + (tr + 1) * 512 : dc * STR_W + (tr + 1) * 512 + 1])
                    ins.then_inc(s_stb, 1)
                for kc in range(NPAIR):
                    sz = PAIR_SZ[kc]
                    gk = tr * NPAIR + kc
                    v.wait_ge(s_gel, gk + 1)
                    if tr >= 1 and kc == 0:
                        v.wait_ge(s_xm, NDC * tr)
                    pa = ps_a if kc % 2 == 0 else ps_a2
                    ins = v.scalar_tensor_tensor(
                        h4_buf[0:sz, kc * 512 : (kc + 1) * 512],
                        pa[0:sz, :], b1c[0:sz, 2 * kc : 2 * kc + 1],
                        gel[0:sz, (kc % 2) * 512 : (kc % 2) * 512 + 512],
                        ALU.add, ALU.mult)
                    ins.then_inc(s_h4, 1)
                if tr < NTR - 1:
                    stats_vec(tr + 1)
                    r2_op(tr + 1)
                    z_op(tr + 1)
                for dc in range(NDC):
                    gd = tr * NDC + dc
                    v.wait_ge(s_xm, gd + 1)
                    px = ps_bc1 if dc % 2 == 0 else ps_bc2
                    sl = stream[:, dc * STR_W + 1 + tr * 512 : dc * STR_W + 1 + (tr + 1) * 512]
                    ins = v.scalar_tensor_tensor(sl, px[:], b2c[:, dc : dc + 1], sl,
                                                 ALU.add, ALU.add)
                    ins.then_inc(s_xadd, 1)
        block.vector(v)

        return {
            "sync": [(s_ini, 64)],
            "gpsimd": [(s_h2c, NTR), (s_w1, 16 * NPAIR * NTR), (s_w2, 32 * NDC * NTR),
                       (s_in2, 16)],
            "tensor": [(s_xm, NDC * NTR), (s_stm, 2 * NTR)],
            "scalar": [(s_gel, NPAIR * NTR), (s_sq2, NTR), (s_rld, 2 * NTR)],
            "vector": [(s_xadd, NDC * NTR), (s_h4, NPAIR * NTR), (s_z, NDC * NTR),
                       (s_msv, 3 * NTR), (s_r2, NTR), (s_stb, NTR - 1)],
        }


def phase_FIN(kb, block, din, dr, stream, b):
    with ExitStack() as st:
        ptile = kb.sb(st, f"fin{b}p", [128, 2 * D], F32)
        otile = kb.sb(st, f"fin{b}o", [128, 2 * D], F32)
        idf = kb.sb(st, f"fin{b}i", [128, 128], F32)
        psF = [kb.ps(st, f"fin{b}ps{i}", [128, 512], F32) for i in range(4)]
        s_p = kb.sem(f"fin{b}pl")
        s_tr = kb.sem(f"fin{b}tr")
        s_ad = kb.sem(f"fin{b}ad")
        s_st = kb.sem(f"fin{b}st")
        s_idf = kb.sem(f"fin{b}id")

        def sy(sy):
            sy.dma_start(out=idf[:], in_=din["ident"]).then_inc(s_idf, 16)
            for tc in range(NT):
                if tc >= 2:
                    sy.wait_ge(s_ad, 2 * (tc - 1))
                sy.dma_start(
                    out=ptile[:, (tc % 2) * D : (tc % 2) * D + D],
                    in_=dr["pbuf"][b, tc * 128 : (tc + 1) * 128, :],
                ).then_inc(s_p, 16)
                sy.wait_ge(s_ad, 2 * (tc + 1))
                sy.dma_start(
                    out=dr["out"][b, tc * 128 : (tc + 1) * 128, :],
                    in_=otile[:, (tc % 2) * D : (tc % 2) * D + D],
                ).then_inc(s_st, 16)
            sy.wait_ge(s_st, 16 * NT)
        block.sync(sy)

        def t(tn):
            tn.wait_ge(s_idf, 16)
            for tc in range(NT):
                for half in range(2):
                    gi = 2 * tc + half
                    if gi >= 4:
                        tn.wait_ge(s_ad, gi - 3)
                    pt = psF[gi % 4]
                    for j in range(4):
                        dc = 4 * half + j
                        tr = tn.transpose(
                            pt[:, j * 128 : (j + 1) * 128],
                            stream[:, dc * STR_W + 1 + tc * 128 : dc * STR_W + 1 + (tc + 1) * 128],
                            idf[:],
                        )
                    tr.then_inc(s_tr, 1)
        block.tensor(t)

        def v(v):
            for tc in range(NT):
                v.wait_ge(s_p, 16 * (tc + 1))
                if tc >= 2:
                    v.wait_ge(s_st, 16 * (tc - 1))
                for half in range(2):
                    gi = 2 * tc + half
                    v.wait_ge(s_tr, gi + 1)
                    v.tensor_tensor(
                        otile[:, (tc % 2) * D + half * 512 : (tc % 2) * D + half * 512 + 512],
                        psF[gi % 4][:],
                        ptile[:, (tc % 2) * D + half * 512 : (tc % 2) * D + half * 512 + 512],
                        ALU.add,
                    ).then_inc(s_ad, 1)
        block.vector(v)

        return {
            "sync": [(s_st, 16 * NT), (s_p, 16 * NT), (s_idf, 16)],
            "tensor": [(s_tr, 2 * NT)],
            "vector": [(s_ad, 2 * NT)],
        }



def kernel(**inputs):
    hp = host_prep(inputs)
    x = np.ascontiguousarray(np.asarray(inputs["L0_attn_in_features"], np.float32))
    kb, dr = build(dev=False)
    in_maps = []
    for c in range(NCORES):
        m = {"xsh": np.ascontiguousarray(x[NB * c : NB * (c + 1)])}
        m.update(hp)
        in_maps.append(m)
    res = run_bass_kernel_spmd(kb.nc, in_maps, core_ids=list(range(NCORES)))
    return np.concatenate([r["out"] for r in res.results], axis=0)


# version 27
# speedup vs baseline: 1.0146x; 1.0068x over previous
import numpy as np
import ml_dtypes
from contextlib import ExitStack

import concourse.bass as bass
import concourse.mybir as mybir
from concourse.bass_utils import run_bass_kernel_spmd

F32 = mybir.dt.float32
BF16 = mybir.dt.bfloat16
AF = mybir.ActivationFunctionType
ALU = mybir.AluOpType
AX = mybir.AxisListType
bfdt = ml_dtypes.bfloat16

NCORES = 8
NB = 2
T = 4096
D = 1024
NT = T // 128
NDC = D // 128
HID = 2730
H2 = 5460
HEADS = 8
DH = 64
INNER = 512
WIN = 512
NEG = -1.0e30
NPAIR = 22
PAIR_SZ = [128] * 21 + [42]
CH_OFF = np.cumsum([0] + [2 * s for s in PAIR_SZ]).tolist()
STR_W = T + 1

ENGS = ("tensor", "vector", "scalar", "gpsimd", "sync")


def _rotary_tables():
    half = np.arange(0, DH, 2, dtype=np.float64)
    inv_freq = 1.0 / 10000.0 ** (half / DH)
    t = np.arange(T, dtype=np.float64)
    freqs = t[:, None] * inv_freq[None, :]
    freqs = np.concatenate([freqs, freqs], axis=-1)
    base = (half + 0.4 * DH) / (1.4 * DH)
    power = (t - T // 2) / float(WIN // 2)
    scale = base[None, :] ** power[:, None]
    scale = np.concatenate([scale, scale], axis=-1)
    cos, sin = np.cos(freqs), np.sin(freqs)
    sgn = np.concatenate([-np.ones(DH // 2), np.ones(DH // 2)])
    csq = (8.0 * cos * scale).astype(np.float32)
    ssq = (8.0 * sin * scale * sgn).astype(np.float32)
    csk = (cos / scale).astype(np.float32)
    ssk = (sin / scale * sgn).astype(np.float32)
    return csq, ssq, csk, ssk


def host_prep(inputs):
    c = np.ascontiguousarray
    d = {}
    d["WqT"] = c(np.asarray(inputs["L0_attn_Wq"]).T)
    d["WkT"] = c(np.asarray(inputs["L0_attn_Wk"]).T)
    d["Wv"] = c(np.asarray(inputs["L0_attn_Wv"]))
    d["WoT"] = c(np.asarray(inputs["L0_attn_Wo"]).T)
    d["wqkv"] = c(np.asarray(inputs["L0_ffn_wqkv"]))
    d["wo"] = c(np.asarray(inputs["L0_ffn_wo"]))
    w1 = np.asarray(inputs["L0_ffn_w1"])
    b1 = np.asarray(inputs["L0_ffn_b1"])
    perm = []
    for ci in range(NPAIR):
        sz = PAIR_SZ[ci]
        perm.extend(range(128 * ci, 128 * ci + sz))
        perm.extend(range(HID + 128 * ci, HID + 128 * ci + sz))
    perm = np.asarray(perm)
    d["w1p"] = c(w1[:, :, perm])
    b1p = b1[:, perm]
    b1_pk = np.zeros((2, 128, 2 * NPAIR), np.float32)
    for ci in range(NPAIR):
        sz = PAIR_SZ[ci]
        off = CH_OFF[ci]
        b1_pk[:, :sz, 2 * ci] = b1p[:, off : off + sz]
        b1_pk[:, :sz, 2 * ci + 1] = b1p[:, off + sz : off + 2 * sz]
    d["b1_pk"] = b1_pk
    d["w2"] = c(np.asarray(inputs["L0_ffn_w2"]))
    d["lng_pk"] = c(np.asarray(inputs["L0_ffn_ln_g"]).reshape(2, 8, 128).transpose(0, 2, 1))
    d["lnb_pk"] = c(np.asarray(inputs["L0_ffn_ln_b"]).reshape(2, 8, 128).transpose(0, 2, 1))
    d["b2_pk"] = c(np.asarray(inputs["L0_ffn_b2"]).reshape(2, 8, 128).transpose(0, 2, 1))
    qs_t = np.tile(np.asarray(inputs["L0_ffn_q_scale"], np.float32), (1, HEADS))
    ks_t = np.tile(np.asarray(inputs["L0_ffn_k_scale"], np.float32), (1, HEADS))
    d["qscale_bc"] = c(np.broadcast_to(qs_t[:, None, :], (2, 128, INNER)))
    d["kscale_bc"] = c(np.broadcast_to(ks_t[:, None, :], (2, 128, INNER)))
    csq, ssq, csk, ssk = _rotary_tables()
    d["csq"], d["ssq"], d["csk"], d["ssk"] = csq, ssq, csk, ssk
    d["ident"] = np.eye(128, dtype=np.float32)
    d["ones_row"] = np.ones((1, 128), np.float32)
    d["ones_col"] = np.ones((128, 1), np.float32)
    ii, jj = np.indices((128, 128))
    d["b_diag"] = np.where(jj <= ii, 0.0, NEG).astype(np.float32)
    d["b_old"] = np.where(jj >= ii, 0.0, NEG).astype(np.float32)
    d["epses"] = np.tile(np.asarray([[1e-5, 1e-24]], np.float32), (128, 1))
    d["ones_nd"] = np.full((128, 128), -1.0 / D, np.float32)
    d["ones_pd"] = np.full((128, 128), 1.0 / D, np.float32)
    return d


class KB:

    def __init__(self, dev=False):
        self.dev = dev
        self.nc = bass.Bass("TRN2", target_bir_lowering=False, debug=False,
                            detect_race_conditions=False)
        self.stack = ExitStack()
        self.sems = {}
        self.all_sems = {}
        self.bar_n = 0
        self.sbuf_cols = 0

    def sem(self, name):
        if name not in self.sems:
            self.sems[name] = self.stack.enter_context(self.nc.semaphore(name))
            self.all_sems[name] = self.sems[name]
        return self.sems[name]

    def barrier(self, block, flush=None):
        h = self.sem("bar")
        self.bar_n += 5
        tgt = self.bar_n
        flush = flush or {}
        for name in ENGS:
            def body(e, name=name):
                for sm, val in flush.get(name, ()):
                    e.wait_ge(sm, val)
                e.nop().then_inc(h, 1)
                e.wait_ge(h, tgt)
            getattr(block, name)(body)

    def recycle(self, block, prefixes, force=False):
        if not force and len(self.sems) < 72:
            return
        keep = ("bar", "cld")
        names = [n for n in self.sems
                 if n not in keep and not n.startswith("tap")]
        if not names:
            return
        handles = [self.sems.pop(n) for n in names]

        def g(g):
            self.nc.clear_and_free_semaphores(handles)
        block.gpsimd(g)
        self.barrier(block)

    def sb(self, st, name, shape, dtype):
        return st.enter_context(self.nc.sbuf_tensor(name, shape, dtype))

    def ps(self, st, name, shape, dtype=F32):
        return st.enter_context(self.nc.psum_tensor(name, shape, dtype))



def phase_W(kb, block, din, dr):
    nc = kb.nc
    with ExitStack() as st:
        wv_sb = kb.sb(st, "w_wv", [128, NDC * D], BF16)
        wot_sb = kb.sb(st, "w_wot", [128, NDC * D], BF16)
        mvo_st = kb.sb(st, "w_mvo", [128, 2 * D], BF16)
        psA = kb.ps(st, "w_psA", [128, 512], F32)
        psB = kb.ps(st, "w_psB", [128, 512], F32)
        s_ld = kb.sem("w_ld")
        s_mm = kb.sem("w_mm")
        s_ev = kb.sem("w_ev")
        s_st = kb.sem("w_st")

        def g(g):
            for l in range(2):
                g.dma_start(out=dr["w1p_bf"][l], in_=din["w1p"][l]).then_inc(kb.sem("w_cast"), 16)
                g.dma_start(out=dr["w2_bf"][l], in_=din["w2"][l]).then_inc(kb.sem("w_cast"), 16)
            g.dma_start(
                out=wv_sb[:].rearrange("p (k m) -> p k m", k=NDC),
                in_=din["Wv"].rearrange("(k p) m -> p k m", p=128),
            ).then_inc(s_ld, 16)
            g.dma_start(
                out=wot_sb[:].rearrange("p (k m) -> p k m", k=NDC),
                in_=din["WoT"].rearrange("(k p) m -> p k m", p=128),
            ).then_inc(s_ld, 16)
        block.gpsimd(g)

        def t(tn):
            tn.wait_ge(s_ld, 32)
            n = 0
            for mc in range(NDC):
                for nr in range(2):
                    n += 1
                    if n > 2:
                        tn.wait_ge(s_ev, n - 2)
                    pt = psA if nr == 0 else psB
                    for kc in range(NDC):
                        mm = tn.matmul(
                            pt[:],
                            wv_sb[:, kc * D + mc * 128 : kc * D + (mc + 1) * 128],
                            wot_sb[:, kc * D + nr * 512 : kc * D + nr * 512 + 512],
                            start=(kc == 0), stop=(kc == NDC - 1),
                        )
                    mm.then_inc(s_mm, 1)
        block.tensor(t)

        def v(v):
            n = 0
            for mc in range(NDC):
                if mc >= 2:
                    v.wait_ge(s_st, 16 * (mc - 1))
                for nr in range(2):
                    n += 1
                    v.wait_ge(s_mm, n)
                    v.tensor_copy(
                        mvo_st[:, (mc % 2) * D + nr * 512 : (mc % 2) * D + nr * 512 + 512],
                        (psA if nr == 0 else psB)[:],
                    ).then_inc(s_ev, 1)
        block.vector(v)

        def sy(sy):
            for mc in range(NDC):
                sy.wait_ge(s_ev, 2 * (mc + 1))
                sy.dma_start(
                    out=dr["mvod"][mc * 128 : (mc + 1) * 128, :],
                    in_=mvo_st[:, (mc % 2) * D : (mc % 2) * D + D],
                ).then_inc(s_st, 16)
            sy.wait_ge(s_st, 16 * NDC)
        block.sync(sy)
        return {
            "gpsimd": [(kb.sem("w_cast"), 64), (s_ld, 32)],
            "tensor": [(s_mm, 16)],
            "vector": [(s_ev, 16)],
            "sync": [(s_st, 16 * NDC)],
        }


def emit_ln_tiles(kb, block, st, tag, b, din, dst_writer, want_bf=True):
    nc = kb.nc
    xt = kb.sb(st, tag + "_xt", [128, 2 * D], F32)
    sq = kb.sb(st, tag + "_sq", [128, D], F32)
    xn = kb.sb(st, tag + "_xn", [128, 2 * D], F32)
    rows = kb.sb(st, tag + "_rows", [128, 8], F32)
    idf = kb.sb(st, tag + "_idf", [128, 128], F32)
    psT = [kb.ps(st, tag + f"_psT{i}", [128, 512], F32) for i in range(4)]

    s_x = kb.sem(tag + "_x")
    s_sq = kb.sem(tag + "_sq")
    s_v = kb.sem(tag + "_v")
    s_r = kb.sem(tag + "_r")
    s_xn = kb.sem(tag + "_xn")
    s_tr = kb.sem(tag + "_tr")
    s_ev = kb.sem(tag + "_ev")
    s_idf = kb.sem(tag + "_id")

    def sy(sy):
        sy.dma_start(out=idf[:], in_=din["ident"]).then_inc(s_idf, 16)
        for tc in range(NT):
            if tc >= 2:
                sy.wait_ge(s_xn, tc - 1)
            sy.dma_start(
                out=xt[:, (tc % 2) * D : (tc % 2) * D + D],
                in_=din["xsh"][b, tc * 128 : (tc + 1) * 128, :],
            ).then_inc(s_x, 16)
    block.sync(sy)

    def s(s):
        for tc in range(NT + 1):
            if tc < NT:
                s.wait_ge(s_x, 16 * (tc + 1))
                if tc >= 1:
                    s.wait_ge(s_v, 5 * tc)
                    s.wait_ge(s_xn, tc)
                s.activation(sq[:], xt[:, (tc % 2) * D : (tc % 2) * D + D],
                             AF.Square).then_inc(s_sq, 1)
                s.wait_ge(s_v, 5 * tc + 5)
                s.activation(rows[:, 5:6], rows[:, 4:5], AF.Ln, bias=kb.epses[:, 0:1]
                             ).then_inc(s_r, 1)
                s.wait_ge(s_r, 2 * tc + 1)
                s.activation(rows[:, 6:7], rows[:, 5:6], AF.Exp, scale=-0.5
                             ).then_inc(s_r, 1)
            if tc >= 1:
                for half in range(2):
                    s.wait_ge(s_tr, 2 * (tc - 1) + half + 1)
                    ins = dst_writer(s, tc - 1, half, psT[(2 * (tc - 1) + half) % 4])
                    ins.then_inc(s_ev, 1)
    block.scalar(s)

    def v(v):
        for tc in range(NT):
            xtv = xt[:, (tc % 2) * D : (tc % 2) * D + D]
            v.wait_ge(s_x, 16 * (tc + 1))
            if tc >= 1:
                v.wait_ge(s_xn, tc)
            v.reduce_sum(rows[:, 2:3], xtv, axis=AX.X, negate=True).then_inc(s_v, 1)
            v.wait_ge(s_v, 5 * tc + 1)
            v.tensor_scalar(rows[:, 2:3], rows[:, 2:3], 1.0 / D, None, ALU.mult
                            ).then_inc(s_v, 1)
            v.wait_ge(s_sq, tc + 1)
            v.reduce_sum(rows[:, 1:2], sq[:], axis=AX.X).then_inc(s_v, 1)
            v.wait_ge(s_v, 5 * tc + 3)
            v.tensor_tensor(rows[:, 3:4], rows[:, 2:3], rows[:, 2:3], ALU.mult
                            ).then_inc(s_v, 1)
            v.wait_ge(s_v, 5 * tc + 4)
            if tc >= 1:
                v.wait_ge(s_r, 2 * tc - 1)
            v.scalar_tensor_tensor(
                rows[:, 4:5], rows[:, 1:2], 1.0 / D, rows[:, 3:4],
                ALU.mult, ALU.subtract,
            ).then_inc(s_v, 1)
            v.wait_ge(s_r, 2 * (tc + 1))
            if tc >= 2:
                v.wait_ge(s_tr, 2 * (tc - 1))
            v.tensor_scalar(
                xn[:, (tc % 2) * D : (tc % 2) * D + D], xtv,
                rows[:, 2:3], rows[:, 6:7], ALU.add, ALU.mult,
            ).then_inc(s_xn, 1)
    block.vector(v)

    def t(tn):
        tn.wait_ge(s_idf, 16)
        for tc in range(NT):
            tn.wait_ge(s_xn, tc + 1)
            for half in range(2):
                if 2 * tc + half >= 4:
                    tn.wait_ge(s_ev, 2 * tc + half - 3)
                pt = psT[(2 * tc + half) % 4]
                for j in range(4):
                    dc = 4 * half + j
                    tr = tn.transpose(
                        pt[:, j * 128 : (j + 1) * 128],
                        xn[:, (tc % 2) * D + dc * 128 : (tc % 2) * D + (dc + 1) * 128],
                        idf[:],
                    )
                tr.then_inc(s_tr, 1)
    block.tensor(t)

    return {
        "sync": [(s_x, 16 * NT), (s_idf, 16)],
        "scalar": [(s_sq, NT), (s_r, 2 * NT), (s_ev, 2 * NT)],
        "vector": [(s_v, 5 * NT), (s_xn, NT)],
        "tensor": [(s_tr, 2 * NT)],
    }


_IDF = {}


def din_identf_ap(kb):
    return _IDF[id(kb)]


def phase_P0(kb, block, din, dr, xnT_bf, b):
    with ExitStack() as st:
        def writer(s, tc, half, pt):
            dst = xnT_bf[:].rearrange("p (c t) -> p c t", c=NDC)[
                :, 4 * half : 4 * half + 4, tc * 128 : (tc + 1) * 128
            ]
            return s.copy(dst, pt[:].rearrange("p (c t) -> p c t", c=4))
        flush = emit_ln_tiles(kb, block, st, f"p0{b}", b, din, writer)
    return flush


def phase_A1(kb, block, din, dr, xnT_bf, b):
    nc = kb.nc
    with ExitStack() as st:
        w_sb = kb.sb(st, f"a1w{b}", [128, 2 * NDC * D], BF16)
        mvo_sb = kb.sb(st, f"a1m{b}", [128, NDC * D], BF16)
        stg = kb.sb(st, f"a1s{b}", [128, 4 * 512], BF16)
        psA = kb.ps(st, f"a1pA{b}", [128, 512], F32)
        psB = kb.ps(st, f"a1pB{b}", [128, 512], F32)
        s_w = kb.sem(f"a1w{b}")
        s_mm = kb.sem(f"a1mm{b}")
        s_ev = kb.sem(f"a1ev{b}")
        s_st = kb.sem(f"a1st{b}")

        def g(g):
            for i, nm in enumerate(("WkT", "WqT")):
                g.dma_start(
                    out=w_sb[:, i * NDC * D : (i + 1) * NDC * D].rearrange(
                        "p (k m) -> p k m", k=NDC),
                    in_=din[nm].rearrange("(k p) m -> p k m", p=128),
                ).then_inc(s_w, 16)
            g.dma_start(
                out=mvo_sb[:].rearrange("p (k m) -> p k m", k=NDC),
                in_=dr["mvod"].rearrange("(k p) m -> p k m", p=128),
            ).then_inc(s_w, 16)
        block.gpsimd(g)

        jobs = []
        for i in range(2):
            for mc in range(NDC):
                for tr in range(8):
                    jobs.append(("kq", i, mc, tr))
        for tc in range(NT):
            for nh in range(2):
                jobs.append(("v", 0, tc, nh))

        def t(tn):
            tn.wait_ge(s_w, 48)
            for n, (kind, i, a_, b_) in enumerate(jobs, 1):
                if n > 2:
                    tn.wait_ge(s_ev, n - 2)
                pt = psA if n % 2 else psB
                if kind == "kq":
                    mc, tr = a_, b_
                    for kc in range(NDC):
                        mm = tn.matmul(
                            pt[:],
                            w_sb[:, (i * NDC + kc) * D + mc * 128 : (i * NDC + kc) * D + (mc + 1) * 128],
                            xnT_bf[:, kc * T + tr * 512 : kc * T + tr * 512 + 512],
                            start=(kc == 0), stop=(kc == NDC - 1),
                        )
                else:
                    tc, nh = a_, b_
                    for kc in range(NDC):
                        mm = tn.matmul(
                            pt[:],
                            xnT_bf[:, kc * T + tc * 128 : kc * T + (tc + 1) * 128],
                            mvo_sb[:, kc * D + nh * 512 : kc * D + nh * 512 + 512],
                            start=(kc == 0), stop=(kc == NDC - 1),
                        )
                mm.then_inc(s_mm, 1)
        block.tensor(t)

        def s(s):
            for n, (kind, i, a_, b_) in enumerate(jobs, 1):
                s.wait_ge(s_mm, n)
                if n > 4:
                    s.wait_ge(s_st, 16 * (n - 4))
                pt = psA if n % 2 else psB
                dst = stg[:, ((n - 1) % 4) * 512 : ((n - 1) % 4) * 512 + 512]
                if kind == "kq" and i == 1:
                    ins = s.activation(dst, pt[:], AF.Copy, scale=1.0 / 32.0)
                else:
                    ins = s.copy(dst, pt[:])
                ins.then_inc(s_ev, 1)
        block.scalar(s)

        def sy(sy):
            for n, (kind, i, a_, b_) in enumerate(jobs, 1):
                sy.wait_ge(s_ev, n)
                src = stg[:, ((n - 1) % 4) * 512 : ((n - 1) % 4) * 512 + 512]
                if kind == "kq":
                    mc, tr = a_, b_
                    dstd = (dr["KTd"] if i == 0 else dr["QTd"])[b, mc, :, tr * 512 : tr * 512 + 512]
                else:
                    tc, nh = a_, b_
                    dstd = dr["Vpd"][b, tc * 128 : (tc + 1) * 128, nh * 512 : nh * 512 + 512]
                sy.dma_start(out=dstd, in_=src).then_inc(s_st, 16)
            sy.wait_ge(s_st, 16 * len(jobs))
        block.sync(sy)
        return {
            "gpsimd": [(s_w, 48)],
            "tensor": [(s_mm, len(jobs))],
            "scalar": [(s_ev, len(jobs))],
            "sync": [(s_st, 16 * len(jobs))],
        }



def build(dev=False, phases=("W", "P0", "A1", "A2", "FI", "F0", "F1", "FIN")):
    kb = KB(dev=dev)
    nc = kb.nc

    din = {}
    for nm, shp in [
        ("xsh", [NB, T, D]), ("WqT", [D, D]), ("WkT", [D, D]), ("Wv", [D, D]),
        ("WoT", [D, D]), ("wqkv", [2, D, 3 * INNER]), ("wo", [2, INNER, D]),
        ("w1p", [2, D, H2]), ("b1_pk", [2, 128, 2 * NPAIR]), ("w2", [2, HID, D]),
        ("lng_pk", [2, 128, 8]), ("lnb_pk", [2, 128, 8]), ("b2_pk", [2, 128, 8]),
        ("qscale_bc", [2, 128, INNER]), ("kscale_bc", [2, 128, INNER]),
        ("csq", [T, DH]), ("ssq", [T, DH]), ("csk", [T, DH]), ("ssk", [T, DH]),
        ("ident", [128, 128]), ("ones_row", [1, 128]), ("ones_col", [128, 1]),
        ("b_diag", [128, 128]), ("b_old", [128, 128]), ("epses", [128, 2]),
        ("ones_nd", [128, 128]), ("ones_pd", [128, 128]),
    ]:
        din[nm] = nc.dram_tensor(nm, shp, F32, kind="ExternalInput").ap()

    out = nc.dram_tensor("out", [NB, T, D], F32, kind="ExternalOutput").ap()

    def scratch(name, shape, dtype):
        kind = "ExternalOutput" if dev else "Internal"
        return nc.dram_tensor(name, shape, dtype, kind=kind).ap()

    dr = {
        "w1p_bf": scratch("w1p_bf", [2, D, H2], BF16),
        "w2_bf": scratch("w2_bf", [2, HID, D], BF16),
        "mvod": scratch("mvod", [D, D], BF16),
        "QTd": scratch("QTd", [NB, NDC, 128, T], BF16),
        "KTd": scratch("KTd", [NB, NDC, 128, T], BF16),
        "Vpd": scratch("Vpd", [NB, T, D], BF16),
        "pbuf": scratch("pbuf", [NB, T, D], F32),
        "tapS": scratch("tapS", [128, T], F32),
        "tapP": scratch("tapP", [128, T], BF16),
        "tapC": scratch("tapC", [128, 4], F32),
        "out": out,
    }

    with kb.stack:
        block = kb.stack.enter_context(nc.Block())
        epses = kb.sb(kb.stack, "epses_s", [128, 2], F32)
        kb.epses = epses
        s_c = kb.sem("cld")

        def g0(g):
            g.dma_start(out=epses[:], in_=din["epses"]).then_inc(s_c, 16)
            g.wait_ge(s_c, 16)
        block.gpsimd(g0)
        kb.barrier(block)

        if "W" in phases:
            fl = phase_W(kb, block, din, dr)
            kb.barrier(block, fl)
            kb.recycle(block, ["w_"])

        for b in range(NB):
            with ExitStack() as stb:
                need_xnT = ("P0" in phases) or ("A1" in phases)
                if need_xnT:
                    xnT_bf = kb.sb(stb, f"xnT{b}", [128, NDC * T], BF16)
                if "P0" in phases:
                    fl = phase_P0(kb, block, din, dr, xnT_bf, b)
                    kb.barrier(block, fl)
                    kb.recycle(block, [f"p0{b}"])
                if "A1" in phases:
                    fl = phase_A1(kb, block, din, dr, xnT_bf, b)
                    kb.barrier(block, fl)
                    kb.recycle(block, ["a1"])
            if "A2" in phases:
                fl = phase_A2(kb, block, din, dr, b)
                kb.barrier(block, fl)
                kb.recycle(block, ["a2"])
            if any(p in phases for p in ("FI", "F0", "F1", "FIN", "F0a", "F1a")):
                with ExitStack() as stf:
                    stream = kb.sb(stf, f"stream{b}", [128, NDC * STR_W], F32)

                    def tap(name, b=b, stream=stream):
                        if not dev or b != 0:
                            return
                        tsr = nc.dram_tensor(f"tap_{name}", [128, NDC * STR_W], F32,
                                             kind="ExternalOutput").ap()
                        s_t = kb.sem(f"tap{name}")

                        def gt(g):
                            g.dma_start(out=tsr, in_=stream[:]).then_inc(s_t, 16)
                        block.gpsimd(gt)
                        kb.barrier(block, {"gpsimd": [(s_t, 16)]})

                    if "FI" in phases:
                        fl = phase_FI(kb, block, din, dr, stream, b)
                        kb.barrier(block, fl)
                        kb.recycle(block, [f"fi{b}"])
                        tap("fi")
                    for l in range(2):
                        if f"F{l}" in phases or f"F{l}a" in phases:
                            fl = phase_Fa(kb, block, din, dr, stream, b, l)
                            kb.barrier(block, fl)
                            kb.recycle(block, [f"fa{b}{l}"])
                            tap(f"f{l}a")
                        if f"F{l}" in phases:
                            fl = phase_Fb(kb, block, din, dr, stream, b, l)
                            kb.barrier(block, fl)
                            kb.recycle(block, [f"fb{b}{l}"])
                            tap(f"f{l}b")
                    if "FIN" in phases:
                        fl = phase_FIN(kb, block, din, dr, stream, b)
                        kb.barrier(block, fl)
                        kb.recycle(block, [f"fin{b}", "tap"])


    nc.finalize()
    return kb, dr



def phase_A2(kb, block, din, dr, b):
    nc = kb.nc
    with ExitStack() as st:
        KT_sb = kb.sb(st, f"a2kt{b}", [128, NDC * T], BF16)
        Vp_sb = kb.sb(st, f"a2vp{b}", [128, NT * D], BF16)
        qt = kb.sb(st, f"a2qt{b}", [128, 2 * NDC * 128], BF16)
        P_bf = kb.sb(st, f"a2p{b}", [128, 2 * T], BF16)
        PT_sb = kb.sb(st, f"a2pt{b}", [128, 2 * T], BF16)
        x0t = kb.sb(st, f"a2x0{b}", [128, 2 * D], F32)
        p_sb = kb.sb(st, f"a2pb{b}", [128, 2 * D], F32)
        bdg = kb.sb(st, f"a2bd{b}", [128, 128], F32)
        idb = kb.sb(st, f"a2ib{b}", [128, 128], BF16)
        cols = kb.sb(st, f"a2c{b}", [128, 8], F32)
        cols_rs = kb.sb(st, f"a2cr{b}", [128, 16], F32)
        ps_s = [kb.ps(st, f"a2ps{b}{i}", [128, 512], F32) for i in range(2)]
        ps_t = [kb.ps(st, f"a2pt{b}{i}", [128, 512], BF16) for i in range(2)]
        ps_o = [kb.ps(st, f"a2po{b}{i}", [128, 512], F32) for i in range(4)]

        s_kv = kb.sem(f"a2kv{b}")
        s_qt = kb.sem(f"a2sqt{b}")
        s_x0 = kb.sem(f"a2sx0{b}")
        s_smm = kb.sem(f"a2smm{b}")
        s_dg = kb.sem(f"a2dg{b}")
        s_rs = kb.sem(f"a2rs{b}")
        s_pe = kb.sem(f"a2pe{b}")
        s_rc = kb.sem(f"a2rc{b}")
        s_ptr = kb.sem(f"a2ptr{b}")
        s_ptev = kb.sem(f"a2pte{b}")
        s_pv = kb.sem(f"a2pv{b}")
        s_pev = kb.sem(f"a2pev{b}")
        s_pst = kb.sem(f"a2pst{b}")

        def R(qc):
            return qc // 4 + 1

        def G(qc):
            return (qc + 4) // 4

        range_meta = []
        smm_at = {-1: 0}
        sev_at = {-1: 0}
        grp_at = {-1: 0, -2: 0}
        tot = 0
        gtot = 0
        for qc in range(NT):
            for r in range(R(qc)):
                range_meta.append((qc, r == R(qc) - 1))
            tot += R(qc)
            smm_at[qc] = tot
            sev_at[qc] = tot
            gtot += G(qc)
            grp_at[qc] = gtot

        def g(g):
            g.dma_start(
                out=KT_sb[:].rearrange("p (c t) -> p c t", c=NDC),
                in_=dr["KTd"][b].transpose([1, 0, 2]),
            ).then_inc(s_kv, 16)
            for hh in range(2):
                g.dma_start(
                    out=Vp_sb[:].rearrange("p (c d) -> p c d", c=NT)[:, hh * 16 : (hh + 1) * 16],
                    in_=dr["Vpd"][b, hh * 2048 : (hh + 1) * 2048, :].rearrange(
                        "(c p) d -> p c d", p=128),
                ).then_inc(s_kv, 16)
            g.dma_start(out=bdg[:], in_=din["b_diag"]).then_inc(s_kv, 16)
            g.dma_start(out=idb[:], in_=din["ident"]).then_inc(s_kv, 16)
        block.gpsimd(g)

        def sy(sy):
            for qc in range(NT):
                if qc >= 2:
                    sy.wait_ge(s_pev, qc - 1)
                sy.dma_start(
                    out=x0t[:, (qc % 2) * D : (qc % 2) * D + D],
                    in_=din["xsh"][b, qc * 128 : (qc + 1) * 128, :],
                ).then_inc(s_x0, 16)
                if qc >= 1:
                    sy.wait_ge(s_pev, qc)
                    sy.dma_start(
                        out=dr["pbuf"][b, (qc - 1) * 128 : qc * 128, :],
                        in_=p_sb[:, ((qc - 1) % 2) * D : ((qc - 1) % 2) * D + D],
                    ).then_inc(s_pst, 16)
            sy.wait_ge(s_pev, NT)
            sy.dma_start(
                out=dr["pbuf"][b, (NT - 1) * 128 : NT * 128, :],
                in_=p_sb[:, ((NT - 1) % 2) * D : ((NT - 1) % 2) * D + D],
            ).then_inc(s_pst, 16)
            sy.wait_ge(s_pst, 16 * NT)
        block.sync(sy)

        def trpv(tn, qc):
            nk = qc + 1
            ngr = G(qc)
            tn.wait_ge(s_pe, smm_at[qc])
            for gi in range(ngr):
                gidx = grp_at[qc - 1] + gi + 1
                if gidx > 2:
                    tn.wait_ge(s_ptev, gidx - 2)
                for j in range(min(4, nk - 4 * gi)):
                    kc = 4 * gi + j
                    tr = tn.transpose(
                        ps_t[(gidx - 1) % 2][:, j * 128 : (j + 1) * 128],
                        P_bf[:, (qc % 2) * T + kc * 128 : (qc % 2) * T + (kc + 1) * 128],
                        idb[:],
                    )
                tr.then_inc(s_ptr, 1)
            if qc >= 2:
                tn.wait_ge(s_pev, qc - 1)
            for gi in range(ngr):
                gidx = grp_at[qc - 1] + gi + 1
                tn.wait_ge(s_ptev, gidx)
                for j in range(min(4, nk - 4 * gi)):
                    kc = 4 * gi + j
                    for nh in range(2):
                        mm = tn.matmul(
                            ps_o[(qc % 2) * 2 + nh][:],
                            PT_sb[:, (qc % 2) * T + kc * 128 : (qc % 2) * T + (kc + 1) * 128],
                            Vp_sb[:, kc * D + nh * 512 : kc * D + nh * 512 + 512],
                            start=(kc == 0), stop=(kc == nk - 1),
                        )
            mm.then_inc(s_pv, 1)

        def t(tn):
            tn.wait_ge(s_kv, 80)
            rtot = 0
            for qc in range(NT):
                nk = qc + 1
                tn.wait_ge(s_qt, 16 * (qc + 1))
                if qc >= 2:
                    tn.wait_ge(s_ptr, grp_at[qc - 2])
                for r in range(R(qc)):
                    rtot += 1
                    if rtot > 2:
                        tn.wait_ge(s_pe, rtot - 2)
                    w = min(512, nk * 128 - r * 512)
                    pt = ps_s[(rtot - 1) % 2]
                    for kc in range(NDC):
                        mm = tn.matmul(
                            pt[:, 0:w],
                            qt[:, (qc % 2) * 1024 + kc * 128 : (qc % 2) * 1024 + (kc + 1) * 128],
                            KT_sb[:, kc * T + r * 512 : kc * T + r * 512 + w],
                            start=(kc == 0), stop=(kc == NDC - 1),
                        )
                    mm.then_inc(s_smm, 1)
                if qc >= 1:
                    trpv(tn, qc - 1)
            trpv(tn, NT - 1)
        block.tensor(t)

        def s(s):
            s.dma_start(
                out=qt[:, (0 % 2) * 1024 : (0 % 2) * 1024 + 1024].rearrange(
                    "p (c t) -> p c t", c=NDC),
                in_=dr["QTd"][b][:, :, 0 * 128 : (0 + 1) * 128].transpose([1, 0, 2]),
            ).then_inc(s_qt, 16)
            s.dma_start(
                out=qt[:, (1 % 2) * 1024 : (1 % 2) * 1024 + 1024].rearrange(
                    "p (c t) -> p c t", c=NDC),
                in_=dr["QTd"][b][:, :, 1 * 128 : (1 + 1) * 128].transpose([1, 0, 2]),
            ).then_inc(s_qt, 16)
            rtot = 0
            for qc in range(NT):
                nk = qc + 1
                if qc >= 2:
                    s.wait_ge(s_ptr, grp_at[qc - 2])
                    s.wait_ge(s_rc, qc - 1)
                for r in range(R(qc)):
                    rtot += 1
                    s.wait_ge(s_smm, rtot)
                    if r == R(qc) - 1:
                        s.wait_ge(s_dg, qc + 1)
                    w = min(512, nk * 128 - r * 512)
                    pt = ps_s[(rtot - 1) % 2]
                    ins = s.activation(
                        P_bf[:, (qc % 2) * T + r * 512 : (qc % 2) * T + r * 512 + w],
                        pt[:, 0:w], AF.Exp,
                        accum_out=cols_rs[:, (qc % 2) * 8 + r : (qc % 2) * 8 + r + 1],
                    )
                    ins.then_inc(s_pe, 1)
                if qc + 2 < NT:
                    s.wait_ge(s_smm, smm_at[qc])
                    s.dma_start(
                        out=qt[:, ((qc + 2) % 2) * 1024 : ((qc + 2) % 2) * 1024 + 1024].rearrange(
                            "p (c t) -> p c t", c=NDC),
                        in_=dr["QTd"][b][:, :, (qc + 2) * 128 : ((qc + 2) + 1) * 128].transpose([1, 0, 2]),
                    ).then_inc(s_qt, 16)
        block.scalar(s)

        def v(v):
            for qc in range(NT):
                nk = qc + 1
                rtot = smm_at[qc]
                v.wait_ge(s_smm, rtot)
                lr = R(qc) - 1
                w = nk * 128 - lr * 512
                pt = ps_s[(rtot - 1) % 2]
                v.tensor_tensor(
                    pt[:, w - 128 : w],
                    pt[:, w - 128 : w], bdg[:], ALU.add,
                ).then_inc(s_dg, 1)
                if qc >= 1:
                    pq = qc - 1
                    for gi in range(G(pq)):
                        gidx = grp_at[pq - 1] + gi + 1
                        v.wait_ge(s_ptr, gidx)
                        if gidx > 2:
                            v.wait_ge(s_pv, pq)
                        wg = min(512, pq * 128 + 128 - gi * 512)
                        ins = v.tensor_copy(
                            PT_sb[:, (pq % 2) * T + gi * 512 : (pq % 2) * T + gi * 512 + wg],
                            ps_t[(gidx - 1) % 2][:, 0:wg],
                        )
                        ins.then_inc(s_ptev, 1)
                    v.wait_ge(s_pv, pq + 1)
                    v.wait_ge(s_x0, 16 * (pq + 1))
                    v.wait_ge(s_rc, pq + 1)
                    if pq >= 2:
                        v.wait_ge(s_pst, 16 * (pq - 1))
                    for nh in range(2):
                        ins = v.scalar_tensor_tensor(
                            p_sb[:, (pq % 2) * D + nh * 512 : (pq % 2) * D + nh * 512 + 512],
                            ps_o[(pq % 2) * 2 + nh][:], cols[:, 4 + pq % 2 : 5 + pq % 2],
                            x0t[:, (pq % 2) * D + nh * 512 : (pq % 2) * D + nh * 512 + 512],
                            ALU.mult, ALU.add,
                        )
                    ins.then_inc(s_pev, 1)
                v.wait_ge(s_pe, smm_at[qc])
                ins = v.reduce_sum(cols[:, 2 + qc % 2 : 3 + qc % 2],
                                   cols_rs[:, (qc % 2) * 8 : (qc % 2) * 8 + R(qc)],
                                   axis=AX.X)
                ins.then_inc(s_rs, 1)
                v.wait_ge(s_rs, qc + 1)
                v.reciprocal(cols[:, 4 + qc % 2 : 5 + qc % 2],
                             cols[:, 2 + qc % 2 : 3 + qc % 2]).then_inc(s_rc, 1)
            pq = NT - 1
            for gi in range(G(pq)):
                gidx = grp_at[pq - 1] + gi + 1
                v.wait_ge(s_ptr, gidx)
                if gidx > 2:
                    v.wait_ge(s_pv, pq)
                wg = min(512, pq * 128 + 128 - gi * 512)
                ins = v.tensor_copy(
                    PT_sb[:, (pq % 2) * T + gi * 512 : (pq % 2) * T + gi * 512 + wg],
                    ps_t[(gidx - 1) % 2][:, 0:wg],
                )
                ins.then_inc(s_ptev, 1)
            v.wait_ge(s_pv, pq + 1)
            v.wait_ge(s_x0, 16 * (pq + 1))
            v.wait_ge(s_rc, pq + 1)
            v.wait_ge(s_pst, 16 * (pq - 1))
            for nh in range(2):
                ins = v.scalar_tensor_tensor(
                    p_sb[:, (pq % 2) * D + nh * 512 : (pq % 2) * D + nh * 512 + 512],
                    ps_o[(pq % 2) * 2 + nh][:], cols[:, 4 + pq % 2 : 5 + pq % 2],
                    x0t[:, (pq % 2) * D + nh * 512 : (pq % 2) * D + nh * 512 + 512],
                    ALU.mult, ALU.add,
                )
            ins.then_inc(s_pev, 1)
        block.vector(v)

        return {
            "gpsimd": [(s_kv, 80)],
            "sync": [(s_pst, 16 * NT)],
            "tensor": [(s_smm, sum(R(q) for q in range(NT))), (s_pv, NT),
                       (s_ptr, grp_at[NT - 1])],
            "scalar": [(s_pe, sum(R(q) for q in range(NT))), (s_qt, 16 * NT)],
            "vector": [(s_pev, NT), (s_ptev, grp_at[NT - 1]), (s_rc, NT),
                       (s_rs, NT), (s_dg, NT)],
        }


def phase_FI(kb, block, din, dr, stream, b):
    with ExitStack() as st:
        s_z = kb.sem(f"fi{b}z")

        def vz(v):
            v.memset(
                stream[:].rearrange("p (c w) -> p c w", c=NDC)[:, :, 0:1], 0.0
            ).then_inc(s_z, 1)
        block.vector(vz)

        def writer(s, tc, half, pt):
            dst = stream[:].rearrange("p (c w) -> p c w", c=NDC)[
                :, 4 * half : 4 * half + 4, 1 + tc * 128 : 1 + (tc + 1) * 128
            ]
            return s.copy(dst, pt[:].rearrange("p (c t) -> p c t", c=4))
        flush = emit_ln_tiles(kb, block, st, f"fi{b}", b, din, writer)
        flush.setdefault("vector", []).append((s_z, 1))
    return flush


FA_NCH = [NT]
FA_DBG = set()


def phase_Fa(kb, block, din, dr, stream, b, l):
    tag = f"fa{b}{l}"
    NCH = FA_NCH[0]
    with ExitStack() as st:
        wqkv_sb = kb.sb(st, tag + "wq", [128, NDC * 3 * INNER], BF16)
        wo_sb = kb.sb(st, tag + "wo", [128, 4 * D], BF16)
        qsc_bc = kb.sb(st, tag + "qs", [128, INNER], BF16)
        ksc_bc = kb.sb(st, tag + "ks", [128, INNER], BF16)
        bmask = kb.sb(st, tag + "bm", [128, 256], F32)
        idb = kb.sb(st, tag + "ib", [128, 128], BF16)
        tbl = kb.sb(st, tag + "tb", [128, 2 * 4 * 64], F32)
        xsh = kb.sb(st, tag + "xs", [128, D], BF16)
        qkf = kb.sb(st, tag + "qk", [128, 2 * INNER], F32)
        nrm = kb.sb(st, tag + "nr", [128, 16], F32)
        rst = kb.sb(st, tag + "rs", [128, 16], F32)
        tmp1 = kb.sb(st, tag + "t1", [128, INNER], F32)
        sqsc = kb.sb(st, tag + "sq", [128, DH], F32)
        qrot = kb.sb(st, tag + "qr", [128, INNER], BF16)
        krot = kb.sb(st, tag + "kr", [128, INNER], BF16)
        qT = kb.sb(st, tag + "qt", [64, HEADS * 128], BF16)
        kring = kb.sb(st, tag + "kg", [64, HEADS * 5 * 128], BF16)
        vring = kb.sb(st, tag + "vg", [128, 5 * 512], BF16)
        Pbn = kb.sb(st, tag + "pb", [128, 3 * 640], BF16)
        PT_sb = kb.sb(st, tag + "pt", [128, 2 * 640], BF16)
        oT_buf = kb.sb(st, tag + "ob", [128, 4 * 512], BF16)
        ncol = kb.sb(st, tag + "nc", [128, 24], F32)
        scr5 = kb.sb(st, tag + "s5", [128, 256], F32)
        stash = kb.sb(st, tag + "sh", [128, 4], F32)

        ps_qkv = kb.ps(st, tag + "pq", [128, 512], F32)
        ps_sa = [kb.ps(st, tag + f"psa{i}", [128, 512], F32) for i in range(2)]
        ps_sbp = kb.ps(st, tag + "psb", [128, 256], F32)
        ps_tr = [kb.ps(st, tag + f"ptr{i}", [128, 1024], BF16) for i in range(2)]
        ps_oT = kb.ps(st, tag + "pot", [128, 512], F32)
        ps_wo = kb.ps(st, tag + "pwo", [128, 512], F32)

        s_ini = kb.sem(tag + "ini")
        s_in2 = kb.sem(tag + "in2")
        s_in3 = kb.sem(tag + "in3")
        s_tb = kb.sem(tag + "tbl")
        s_xc = kb.sem(tag + "xc")
        s_qkvm = kb.sem(tag + "qm")
        s_qev = kb.sem(tag + "qe")
        s_sqq = kb.sem(tag + "q2")
        s_rst = kb.sem(tag + "rt")
        s_rot = kb.sem(tag + "ro")
        s_qktr = kb.sem(tag + "qk2")
        s_trev = kb.sem(tag + "tv")
        s_smm = kb.sem(tag + "sm")
        s_msk = kb.sem(tag + "mk")
        s_peh = kb.sem(tag + "pe")
        s_rci = kb.sem(tag + "rc")
        s_pnb = kb.sem(tag + "pn")
        s_ptr = kb.sem(tag + "pr")
        s_ptev = kb.sem(tag + "pv")
        s_pvm = kb.sem(tag + "pm")
        s_oev = kb.sem(tag + "oe")
        s_wom = kb.sem(tag + "wm")
        s_wov = kb.sem(tag + "wv")
        s_sta = kb.sem(tag + "sa")
        s_spc = kb.sem(tag + "sp")
        s_vv = kb.sem(tag + "vv")

        def nsl(tc):
            return min(tc + 1, 5)

        def spans(tc, h):
            n = nsl(tc)
            segs = [(h * 640 + (tc % 5) * 128, 128, 0)]
            lo = tc - n + 1
            si = 1
            while si < n:
                s0 = (lo + si - 1) % 5
                run = min(n - si, 5 - s0)
                segs.append((h * 640 + s0 * 128, run * 128, si))
                si += run
            return segs

        def g(g):
            g.dma_start(
                out=wqkv_sb[:].rearrange("p (k m) -> p k m", k=NDC),
                in_=din["wqkv"][l].rearrange("(k p) m -> p k m", p=128),
            ).then_inc(s_ini, 16)
            g.dma_start(
                out=wo_sb[:].rearrange("p (k m) -> p k m", k=4),
                in_=din["wo"][l].rearrange("(k p) m -> p k m", p=128),
            ).then_inc(s_ini, 16)
            g.dma_start(out=qsc_bc[:], in_=din["qscale_bc"][l]).then_inc(s_in3, 16)
            g.dma_start(out=ksc_bc[:], in_=din["kscale_bc"][l]).then_inc(s_in3, 16)
            g.dma_start(out=idb[:], in_=din["ident"]).then_inc(s_in3, 16)
            for tc in range(NCH):
                if tc >= 1:
                    g.wait_ge(s_qkvm, 3 * tc)
                bound = tc % 4 == 0 and tc >= 4
                if bound:
                    g.wait_ge(s_sta, tc // 4)
                for dc in range(NDC):
                    off = dc * STR_W + tc * 128 + (1 if dc < 4 else 0)
                    if dc >= 4 and bound:
                        g.tensor_copy(xsh[:, dc * 128 : dc * 128 + 1],
                                      stash[:, dc - 4 : dc - 3])
                        ins = g.tensor_copy(
                            xsh[:, dc * 128 + 1 : (dc + 1) * 128],
                            stream[:, off + 1 : off + 128])
                    else:
                        ins = g.tensor_copy(
                            xsh[:, dc * 128 : (dc + 1) * 128],
                            stream[:, off : off + 128])
                ins.then_inc(s_xc, 1)
        block.gpsimd(g)

        def sy(sy):
            sy.dma_start(out=bmask[:, 0:128], in_=din["b_diag"]).then_inc(s_in2, 16)
            sy.dma_start(out=bmask[:, 128:256], in_=din["b_old"]).then_inc(s_in2, 16)
            for gi in range(NCH):
                if gi >= 2:
                    sy.wait_ge(s_rot, 2 * (gi - 1))
                for ti, nmt in enumerate(("csq", "ssq", "csk", "ssk")):
                    sy.dma_start(
                        out=tbl[:, (gi % 2) * 256 + ti * 64 : (gi % 2) * 256 + (ti + 1) * 64],
                        in_=din[nmt][gi * 128 : (gi + 1) * 128, :],
                    ).then_inc(s_tb, 16)
        block.sync(sy)

        def t(tn):
            tn.wait_ge(s_ini, 32)
            tn.wait_ge(s_in3, 48)
            pend_wo = []
            pend_pp = []

            def emit_wo(k):
                for _ in range(k):
                    if pend_wo:
                        pend_wo.pop(0)()

            def ptrpv(ptc, h):
                pn = nsl(ptc)
                if FA_DBG & {"nosbp", "nosbpexp"}:
                    pn = min(pn, 4)
                gh = ptc * HEADS + h
                p3 = (gh % 3) * 640
                p2 = (gh % 2) * 640
                trb = ps_tr[gh % 2]
                tn.wait_ge(s_pnb, gh + 1)
                if gh >= 2:
                    tn.wait_ge(s_ptev, gh - 1)
                if h == 0:
                    tn.wait_ge(s_qev, 3 * (ptc + 1))
                for si in range(pn):
                    tr = tn.transpose(
                        trb[:, si * 128 : (si + 1) * 128],
                        Pbn[:, p3 + si * 128 : p3 + (si + 1) * 128], idb[:])
                tr.then_inc(s_ptr, 1)
                tn.wait_ge(s_ptev, gh + 1)
                if h == 0 and ptc >= 1:
                    tn.wait_ge(s_oev, ptc)
                for si in range(pn):
                    s_ = ptc if si == 0 else ptc - pn + si
                    tp = (0, 64) if h % 2 else None
                    mm = tn.matmul(
                        ps_oT[64 * (h % 2) : 64 * (h % 2) + 64,
                              (h // 2) * 128 : (h // 2) * 128 + 128],
                        vring[:, (s_ % 5) * 512 + h * 64 : (s_ % 5) * 512 + (h + 1) * 64],
                        PT_sb[:, p2 + si * 128 : p2 + (si + 1) * 128],
                        start=(si == 0), stop=(si == pn - 1),
                        tile_position=tp)
                mm.then_inc(s_pvm, 1)

            for tc in range(NCH):
                gh0 = tc * HEADS
                n = nsl(tc)
                tn.wait_ge(s_xc, tc + 1)
                if tc >= 1:
                    tn.wait_ge(s_qev, 3 * tc)
                for seg in range(3):
                    if seg >= 1:
                        tn.wait_ge(s_qev, 3 * tc + seg)
                    for dc in range(NDC):
                        mm = tn.matmul(
                            ps_qkv[:], xsh[:, dc * 128 : (dc + 1) * 128],
                            wqkv_sb[:, dc * 1536 + seg * 512 : dc * 1536 + (seg + 1) * 512],
                            start=(dc == 0), stop=(dc == NDC - 1))
                    mm.then_inc(s_qkvm, 1)
                    if pend_pp:
                        ptrpv(*pend_pp.pop(0))
                while pend_pp:
                    ptrpv(*pend_pp.pop(0))
                emit_wo(8)
                tn.wait_ge(s_rot, 2 * (tc + 1))
                if tc >= 1:
                    tn.wait_ge(s_trev, 4 * tc)
                    tn.wait_ge(s_ptev, 8 * tc)
                for ki, buf in enumerate((qrot, krot)):
                    for h in range(HEADS):
                        tp = (0, 64) if h % 2 else None
                        tr = tn.transpose(
                            ps_tr[ki][64 * (h % 2) : 64 * (h % 2) + 64,
                                      (h // 2) * 128 : (h // 2) * 128 + 128],
                            buf[:, h * 64 : (h + 1) * 64], idb[:],
                            tile_position=tp)
                tr.then_inc(s_qktr, 1)
                for h in range(HEADS):
                    gh = gh0 + h
                    if gh >= 2:
                        tn.wait_ge(s_peh, gh - 1)
                    if h == 0:
                        tn.wait_ge(s_trev, 4 * (tc + 1))
                    sb0 = (gh % 2) * 128
                    for col0, ncols, si0 in spans(tc, h):
                        if "nosbp" in FA_DBG and si0 + ncols // 128 > 4:
                            ncols -= 128
                            if ncols == 0:
                                continue
                        c = col0
                        rem = ncols
                        si = si0
                        while rem > 0:
                            if si < 4:
                                take = min(rem, (4 - si) * 128)
                                dst = ps_sa[gh % 2][:, si * 128 : si * 128 + take]
                            else:
                                take = rem
                                dst = ps_sbp[:, sb0 : sb0 + take]
                            mm = tn.matmul(
                                dst, qT[:, h * 128 : (h + 1) * 128],
                                kring[:, c : c + take],
                                start=True, stop=True)
                            c += take
                            si += take // 128
                            rem -= take
                    mm.then_inc(s_smm, 1)
                    pend_pp.append((tc, h))
                    while len(pend_pp) > 2:
                        ptrpv(*pend_pp.pop(0))
                if tc == NCH - 1:
                    while pend_pp:
                        ptrpv(*pend_pp.pop(0))
                if tc % 4 == 3:
                    gi = tc // 4

                    def mk(dc, gi=gi):
                        def go():
                            tn.wait_ge(s_oev, 4 * gi + 4)
                            if dc == 0:
                                if gi >= 1:
                                    tn.wait_ge(s_wov, 8 * gi)
                            else:
                                tn.wait_ge(s_wov, 8 * gi + dc)
                            for kc in range(4):
                                mm = tn.matmul(
                                    ps_wo[:],
                                    wo_sb[:, kc * D + dc * 128 : kc * D + (dc + 1) * 128],
                                    oT_buf[:, kc * 512 : (kc + 1) * 512],
                                    start=(kc == 0), stop=(kc == 3))
                            mm.then_inc(s_wom, 1)
                        return go
                    for dc in range(NDC):
                        pend_wo.append(mk(dc))
            emit_wo(len(pend_wo))
        block.tensor(t)

        spc_n = [0]

        def s(s):
            for tc in range(NCH):
                gh0 = tc * HEADS
                n = nsl(tc)
                s.wait_ge(s_qkvm, 3 * tc + 1)
                if tc >= 1:
                    s.wait_ge(s_rot, 2 * tc)
                ins = s.copy(qkf[:, 0:512], ps_qkv[:])
                ins.then_inc(s_qev, 1)
                s.wait_ge(s_qkvm, 3 * tc + 2)
                ins = s.copy(qkf[:, 512:1024], ps_qkv[:])
                ins.then_inc(s_qev, 1)
                s.wait_ge(s_qkvm, 3 * tc + 3)
                if tc >= 1:
                    s.wait_ge(s_pvm, 8 * tc)
                ins = s.copy(vring[:, (tc % 5) * 512 : (tc % 5) * 512 + 512], ps_qkv[:])
                ins.then_inc(s_qev, 1)
                s.wait_ge(s_qev, 3 * tc + 2)
                for hh in range(16):
                    ins = s.activation(sqsc[:], qkf[:, hh * 64 : (hh + 1) * 64],
                                       AF.Square, accum_out=nrm[:, hh : hh + 1])
                ins.then_inc(s_sqq, 1)
                s.wait_ge(s_sqq, tc + 1)
                s.activation(rst[:], nrm[:], AF.Ln, bias=kb.epses[:, 1:2]).then_inc(s_rst, 1)
                s.wait_ge(s_rst, 2 * tc + 1)
                s.activation(rst[:], rst[:], AF.Exp, scale=-0.5).then_inc(s_rst, 1)
                for h in range(HEADS):
                    gh = gh0 + h
                    p3 = (gh % 3) * 640
                    nb = (gh % 3) * 8
                    sb0 = (gh % 2) * 128
                    sbp_on = n == 5 and not (FA_DBG & {"nosbp", "nosbpexp"})
                    if sbp_on:
                        s.wait_ge(s_smm, gh + 1)
                        ins = s.copy(scr5[:, sb0 : sb0 + 128],
                                     ps_sbp[:, sb0 : sb0 + 128])
                        ins.then_inc(s_spc, 1)
                        spc_n[0] += 1
                    s.wait_ge(s_msk, gh + 1)
                    if gh >= 3:
                        s.wait_ge(s_ptr, gh - 2)
                    na = min(n, 4) * 128
                    ins = s.activation(
                        Pbn[:, p3 : p3 + na], ps_sa[gh % 2][:, 0:na], AF.Exp,
                        accum_out=ncol[:, nb : nb + 1])
                    if sbp_on:
                        s.wait_ge(s_spc, spc_n[0])
                        ins = s.activation(
                            Pbn[:, p3 + 512 : p3 + 640],
                            scr5[:, sb0 : sb0 + 128], AF.Exp,
                            accum_out=ncol[:, nb + 1 : nb + 2])
                    ins.then_inc(s_peh, 1)
        block.scalar(s)

        vv = [0]

        def v(v):
            pend_pt = []

            def flush_pt(k):
                for _ in range(k):
                    if pend_pt:
                        ptc, pgh = pend_pt.pop(0)
                        pn = nsl(ptc)
                        p2v = (pgh % 2) * 640
                        v.wait_ge(s_ptr, pgh + 1)
                        if pgh >= 2:
                            v.wait_ge(s_pvm, pgh - 1)
                        ins = v.tensor_copy(PT_sb[:, p2v : p2v + pn * 128],
                                            ps_tr[pgh % 2][:, 0 : pn * 128])
                        ins.then_inc(s_ptev, 1)

            def VI(ins):
                vv[0] += 1
                ins.then_inc(s_vv, 1)
            v.wait_ge(s_in2, 32)
            v.wait_ge(s_in3, 32)
            rot_end = {}

            pend_add = []

            def drain_add(k):
                for _ in range(k):
                    if pend_add:
                        pend_add.pop(0)()

            def tail(ptc):
                v.wait_ge(s_pvm, 8 * (ptc + 1))
                ins = v.tensor_copy(
                    oT_buf[:].rearrange("p (j g x) -> p j g x", j=4, g=4)[
                        :, :, ptc % 4, :],
                    ps_oT[:].rearrange("p (j x) -> p j x", j=4))
                ins.then_inc(s_oev, 1)
                if ptc % 4 == 3:
                    gi = ptc // 4

                    def mka(dc, gi=gi):
                        def go():
                            v.wait_ge(s_wom, 8 * gi + dc + 1)
                            sl = stream[:, dc * STR_W + 1 + gi * 512 :
                                        dc * STR_W + 1 + (gi + 1) * 512]
                            ins = v.tensor_tensor(sl, sl, ps_wo[:], ALU.add)
                            ins.then_inc(s_wov, 1)
                        return go
                    for dc in range(NDC):
                        pend_add.append(mka(dc))

            for tc in range(NCH):
                gh0 = tc * HEADS
                n = nsl(tc)
                flush_pt(len(pend_pt))
                if tc >= 1:
                    tail(tc - 1)
                if tc >= 1:
                    v.wait_ge(s_vv, rot_end[tc - 1])
                    v.wait_ge(s_rot, 2 * tc)
                v.wait_ge(s_rst, 2 * (tc + 1))
                rq = rst[:, 0:8].unsqueeze(2).broadcast_to([128, 8, 64]
                     ).rearrange("p h (two half) -> p h two half", two=2)
                rk = rst[:, 8:16].unsqueeze(2).broadcast_to([128, 8, 64]
                     ).rearrange("p h (two half) -> p h two half", two=2)

                def q4(x):
                    return x.rearrange("p (h two half) -> p h two half", h=8, two=2)

                def sw4(x):
                    return q4(x)[:, :, ::-1, :]

                def tb4(ts):
                    return ts.rearrange("p (two half) -> p two half", two=2
                           ).unsqueeze(1).broadcast_to([128, 8, 2, 32])
                VI(v.tensor_tensor(q4(qkf[:, 0:512]), q4(qkf[:, 0:512]), rq, ALU.mult))
                drain_add(1)
                VI(v.tensor_tensor(q4(qkf[:, 512:1024]), q4(qkf[:, 512:1024]), rk, ALU.mult))
                drain_add(1)
                v.wait_ge(s_vv, vv[0])
                VI(v.tensor_tensor(q4(qkf[:, 0:512]), q4(qkf[:, 0:512]), q4(qsc_bc[:]), ALU.mult))
                drain_add(1)
                VI(v.tensor_tensor(q4(qkf[:, 512:1024]), q4(qkf[:, 512:1024]), q4(ksc_bc[:]), ALU.mult))
                drain_add(1)
                v.wait_ge(s_tb, 64 * (tc + 1))
                tb0 = (tc % 2) * 256
                cq = tb4(tbl[:, tb0 + 0 : tb0 + 64])
                sq_ = tb4(tbl[:, tb0 + 64 : tb0 + 128])
                ck = tb4(tbl[:, tb0 + 128 : tb0 + 192])
                sk = tb4(tbl[:, tb0 + 192 : tb0 + 256])
                v.wait_ge(s_vv, vv[0])
                VI(v.tensor_tensor(q4(tmp1[:]), sw4(qkf[:, 0:512]), sq_, ALU.mult))
                drain_add(1)
                v.wait_ge(s_vv, vv[0])
                VI(v.tensor_tensor(q4(qkf[:, 0:512]), q4(qkf[:, 0:512]), cq, ALU.mult))
                drain_add(1)
                v.wait_ge(s_vv, vv[0])
                ins = v.tensor_add(qrot[:], tmp1[:], qkf[:, 0:512])
                ins.then_inc(s_rot, 1)
                drain_add(1)
                VI(v.tensor_tensor(q4(tmp1[:]), sw4(qkf[:, 512:1024]), sk, ALU.mult))
                v.wait_ge(s_vv, vv[0])
                VI(v.tensor_tensor(q4(qkf[:, 512:1024]), q4(qkf[:, 512:1024]), ck, ALU.mult))
                drain_add(1)
                v.wait_ge(s_vv, vv[0])
                ins = v.tensor_add(krot[:], tmp1[:], qkf[:, 512:1024])
                ins.then_inc(s_rot, 1)
                rot_end[tc] = vv[0]
                v.wait_ge(s_qktr, tc + 1)
                v.wait_ge(s_smm, 8 * tc)
                qTv = qT[:].rearrange("p (h x) -> p h x", h=HEADS)
                krv = kring[:].rearrange("p (h s x) -> p h s x", h=HEADS, s=5)[
                    :, :, tc % 5, :]
                ins = v.tensor_copy(qTv[:, 0::2, :],
                                    ps_tr[0][0:64, 0:512].rearrange("p (j x) -> p j x", j=4))
                ins.then_inc(s_trev, 1)
                ins = v.tensor_copy(qTv[:, 1::2, :],
                                    ps_tr[0][64:128, 0:512].rearrange("p (j x) -> p j x", j=4))
                ins.then_inc(s_trev, 1)
                ins = v.tensor_copy(krv[:, 0::2, :],
                                    ps_tr[1][0:64, 0:512].rearrange("p (j x) -> p j x", j=4))
                ins.then_inc(s_trev, 1)
                ins = v.tensor_copy(krv[:, 1::2, :],
                                    ps_tr[1][64:128, 0:512].rearrange("p (j x) -> p j x", j=4))
                ins.then_inc(s_trev, 1)

                def vpartB(pgh):
                    nb = (pgh % 3) * 8
                    p3 = (pgh % 3) * 640
                    pn = nsl(pgh // HEADS)
                    if FA_DBG & {"nosbp", "nosbpexp"}:
                        pn = min(pn, 4)
                    v.wait_ge(s_peh, pgh + 1)
                    if pn == 5:
                        VI(v.tensor_add(ncol[:, nb : nb + 1], ncol[:, nb : nb + 1],
                                        ncol[:, nb + 1 : nb + 2]))
                        v.wait_ge(s_vv, vv[0])
                    ins = v.reciprocal(ncol[:, nb + 2 : nb + 3],
                                       ncol[:, nb : nb + 1])
                    ins.then_inc(s_rci, 1)
                    v.wait_ge(s_rci, pgh + 1)
                    ins = v.tensor_scalar(
                        Pbn[:, p3 : p3 + pn * 128],
                        Pbn[:, p3 : p3 + pn * 128],
                        ncol[:, nb + 2 : nb + 3], None, ALU.mult)
                    ins.then_inc(s_pnb, 1)

                mw = 256 if (tc >= 4 and "nobold" not in FA_DBG) else 128
                for h in range(HEADS):
                    gh = gh0 + h
                    v.wait_ge(s_smm, gh + 1)
                    ins = v.tensor_tensor(ps_sa[gh % 2][:, 0:mw],
                                          ps_sa[gh % 2][:, 0:mw],
                                          bmask[:, 0:mw], ALU.add)
                    ins.then_inc(s_msk, 1)
                    if h >= 1:
                        vpartB(gh - 1)
                    pend_pt.append((tc, gh))
                    flush_pt(len(pend_pt) - 2)
                vpartB(gh0 + HEADS - 1)
                if tc % 4 == 3 and tc < NCH - 1:
                    gi = tc // 4
                    for dc in range(4, 8):
                        ins = v.tensor_copy(
                            stash[:, dc - 4 : dc - 3],
                            stream[:, dc * STR_W + (gi + 1) * 512 :
                                   dc * STR_W + (gi + 1) * 512 + 1])
                    ins.then_inc(s_sta, 1)
                if tc == NCH - 1:
                    flush_pt(len(pend_pt))
                    tail(tc)
                    drain_add(len(pend_add))
        block.vector(v)

        ngr = NCH // 4
        return {
            "gpsimd": [(s_xc, NCH), (s_ini, 32), (s_in3, 48)],
            "sync": [(s_tb, 64 * NCH), (s_in2, 32)],
            "tensor": [(s_qkvm, 3 * NCH), (s_qktr, NCH), (s_smm, 8 * NCH),
                       (s_wom, 8 * ngr), (s_pvm, 8 * NCH), (s_ptr, 8 * NCH)],
            "scalar": [(s_qev, 3 * NCH), (s_sqq, NCH), (s_rst, 2 * NCH),
                       (s_peh, 8 * NCH), (s_ptev, 8 * NCH), (s_spc, spc_n[0])],
            "vector": [(s_rot, 2 * NCH), (s_trev, 4 * NCH),
                       (s_msk, 8 * NCH), (s_sta, ngr - 1), (s_vv, vv[0]),
                       (s_wov, 8 * ngr), (s_oev, NCH), (s_pnb, 8 * NCH),
                       (s_rci, 8 * NCH)],
        }


def phase_Fb(kb, block, din, dr, stream, b, l):
    tag = f"fb{b}{l}"
    NTR = 8
    with ExitStack() as st:
        h2bf = kb.sb(st, tag + "h2", [128, NDC * 512], BF16)
        z_bf = kb.sb(st, tag + "zb", [128, NDC * 512], BF16)
        sq2 = kb.sb(st, tag + "sq", [128, NDC * 512], BF16)
        R1sb = kb.sb(st, tag + "r1", [128, 512], F32)
        R2sb = kb.sb(st, tag + "r2", [128, 512], F32)
        musb = kb.sb(st, tag + "mu", [128, 512], F32)
        ztmp = kb.sb(st, tag + "zt", [128, 512], F32)
        btmp = musb
        h4_buf = kb.sb(st, tag + "h4", [128, NPAIR * 512], BF16)
        w1t = kb.sb(st, tag + "w1", [128, 3 * NDC * 256], BF16)
        w2t = kb.sb(st, tag + "w2", [128, 2 * NPAIR * 128], BF16)
        gel = kb.sb(st, tag + "ge", [128, 2 * 512], BF16)
        b1c = kb.sb(st, tag + "b1", [128, 2 * NPAIR], F32)
        stash = kb.sb(st, tag + "sh", [128, 4], F32)
        lngc = kb.sb(st, tag + "lg", [128, 8], F32)
        lnbc = kb.sb(st, tag + "lb", [128, 8], F32)
        b2c = kb.sb(st, tag + "b2", [128, 8], F32)
        onb_nd = kb.sb(st, tag + "on", [128, 128], BF16)

        ps_mu = kb.ps(st, tag + "pmu", [128, 512], F32)
        ps_ms = kb.ps(st, tag + "pms", [128, 512], F32)
        ps_bc1 = kb.ps(st, tag + "pb1", [128, 512], F32)
        ps_bc2 = kb.ps(st, tag + "pb2", [128, 512], F32)
        ps_a = kb.ps(st, tag + "pa0", [128, 512], F32)
        ps_g = kb.ps(st, tag + "pg0", [128, 512], F32)
        ps_a2 = kb.ps(st, tag + "pa1", [128, 512], F32)
        ps_g2 = kb.ps(st, tag + "pg1", [128, 512], F32)

        s_ini = kb.sem(tag + "ini")
        s_in2 = kb.sem(tag + "in2")
        s_h2c = kb.sem(tag + "hc")
        s_sq2 = kb.sem(tag + "sq")
        s_stm = kb.sem(tag + "st")
        s_msv = kb.sem(tag + "mv")
        s_rld = kb.sem(tag + "rl")
        s_r2 = kb.sem(tag + "m2")
        s_z = kb.sem(tag + "zz")
        s_w1 = kb.sem(tag + "w1")
        s_h3m = kb.sem(tag + "h3")
        s_gel = kb.sem(tag + "gl")
        s_h4 = kb.sem(tag + "h4")
        s_w2 = kb.sem(tag + "w2")
        s_xm = kb.sem(tag + "xm")
        s_xadd = kb.sem(tag + "xa")
        s_stb = kb.sem(tag + "sb")

        def sy(sy):
            sy.dma_start(out=b1c[:], in_=din["b1_pk"][l]).then_inc(s_ini, 16)
            sy.dma_start(out=lngc[:], in_=din["lng_pk"][l]).then_inc(s_ini, 16)
            sy.dma_start(out=lnbc[:], in_=din["lnb_pk"][l]).then_inc(s_ini, 16)
            sy.dma_start(out=b2c[:], in_=din["b2_pk"][l]).then_inc(s_ini, 16)
        block.sync(sy)

        def g(g):
            g.dma_start(out=onb_nd[:], in_=din["ones_nd"]).then_inc(s_in2, 16)
            def cast_op(tr):
                if tr >= 1:
                    g.wait_ge(s_stm, 2 * tr)
                    g.wait_ge(s_sq2, tr)
                    g.wait_ge(s_stb, tr)
                h2v = h2bf[:].rearrange("p (d x) -> p d x", d=NDC)
                stv = stream[:].rearrange("p (d w) -> p d w", d=NDC)
                g.tensor_copy(h2v[:, 0:4, :],
                              stv[:, 0:4, tr * 512 + 1 : tr * 512 + 513])
                if tr >= 1:
                    g.tensor_copy(
                        h2v[:, 4:8, 0:1],
                        stash[:].rearrange("p (d x) -> p d x", d=4))
                    ins = g.tensor_copy(
                        h2v[:, 4:8, 1:512],
                        stv[:, 4:8, tr * 512 + 1 : tr * 512 + 512])
                else:
                    ins = g.tensor_copy(h2v[:, 4:8, :],
                                        stv[:, 4:8, 0:512])
                ins.then_inc(s_h2c, 1)

            def w1_loads(tr, lo=0, hi=NPAIR):
                for kc in range(lo, hi):
                    sz = PAIR_SZ[kc]
                    gk = tr * NPAIR + kc
                    if gk >= 3:
                        g.wait_ge(s_h3m, gk - 2)
                    g.dma_start(
                        out=w1t[:, ((gk % 3) * NDC) * 256 : ((gk % 3) * NDC) * 256 + NDC * 2 * sz
                                ].rearrange("p (c m) -> p c m", c=NDC),
                        in_=dr["w1p_bf"][l].rearrange("(c p) m -> p c m", p=128)[
                            :, :, CH_OFF[kc] : CH_OFF[kc] + 2 * sz],
                    ).then_inc(s_w1, 16)

            cast_op(0)
            w1_loads(0, 0, 14)
            for tr in range(NTR):
                if tr < NTR - 1:
                    cast_op(tr + 1)
                w1_loads(tr, 14, NPAIR)
                if tr < NTR - 1:
                    w1_loads(tr + 1, 0, 3)
                if tr < NTR - 1:
                    w1_loads(tr + 1, 3, 14)
        block.gpsimd(g)

        def sy2(sy):
            for tr in range(NTR):
                for dc in range(NDC):
                    if tr * NDC + dc >= 2:
                        sy.wait_ge(s_xm, tr * NDC + dc - 1)
                    sy.dma_start(
                        out=w2t[:, (dc % 2) * NPAIR * 128 : (dc % 2) * NPAIR * 128 + 21 * 128
                                ].rearrange("p (c m) -> p c m", c=21),
                        in_=dr["w2_bf"][l][0:2688, dc * 128 : (dc + 1) * 128].rearrange(
                            "(c p) m -> p c m", p=128),
                    ).then_inc(s_w2, 16)
                    sy.dma_start(
                        out=w2t[0:42, (dc % 2) * NPAIR * 128 + 21 * 128 : (dc % 2) * NPAIR * 128 + 22 * 128],
                        in_=dr["w2_bf"][l][2688:2730, dc * 128 : (dc + 1) * 128],
                    ).then_inc(s_w2, 16)
        block.sync(sy2)

        def t(tn):
            tn.wait_ge(s_ini, 64)
            tn.wait_ge(s_in2, 16)

            def stats_mm(T):
                tn.wait_ge(s_h2c, T + 1)
                if T >= 1:
                    tn.wait_ge(s_msv, 3 * T - 2)
                for dc in range(NDC):
                    mm = tn.matmul(ps_mu[:], onb_nd[:],
                                   h2bf[:, dc * 512 : (dc + 1) * 512],
                                   start=(dc == 0), stop=(dc == NDC - 1))
                mm.then_inc(s_stm, 1)
                tn.wait_ge(s_sq2, T + 1)
                if T >= 1:
                    tn.wait_ge(s_rld, 2 * T - 1)
                for dc in range(NDC):
                    mm = tn.matmul(ps_ms[:], onb_nd[:],
                                   sq2[:, dc * 512 : (dc + 1) * 512],
                                   start=(dc == 0), stop=(dc == NDC - 1))
                mm.then_inc(s_stm, 1)

            stats_mm(0)
            for tr in range(NTR):
                for kc in range(NPAIR):
                    sz = PAIR_SZ[kc]
                    gk = tr * NPAIR + kc
                    tn.wait_ge(s_w1, 16 * (gk + 1))
                    tn.wait_ge(s_z, 8 * (tr + 1))
                    if gk >= 2:
                        tn.wait_ge(s_h4, gk - 1)
                    pa = ps_a if kc % 2 == 0 else ps_a2
                    pg = ps_g if kc % 2 == 0 else ps_g2
                    for dc in range(NDC):
                        base = (gk % 3) * NDC * 256 + dc * 2 * sz
                        tn.matmul(pa[0:sz, :], w1t[:, base : base + sz],
                                  z_bf[:, dc * 512 : (dc + 1) * 512],
                                  start=(dc == 0), stop=(dc == NDC - 1))
                        mm = tn.matmul(pg[0:sz, :], w1t[:, base + sz : base + 2 * sz],
                                       z_bf[:, dc * 512 : (dc + 1) * 512],
                                       start=(dc == 0), stop=(dc == NDC - 1))
                    mm.then_inc(s_h3m, 1)
                if tr < NTR - 1:
                    stats_mm(tr + 1)
                for dc in range(NDC):
                    gd = tr * NDC + dc
                    tn.wait_ge(s_w2, 32 * (gd + 1))
                    px = ps_bc1 if dc % 2 == 0 else ps_bc2
                    if dc >= 2:
                        tn.wait_ge(s_xadd, 8 * tr + dc - 1)
                    for kc in range(NPAIR):
                        sz = PAIR_SZ[kc]
                        tn.wait_ge(s_h4, tr * NPAIR + kc + 1)
                        mm = tn.matmul(
                            px[:],
                            w2t[0:sz, (dc % 2) * NPAIR * 128 + kc * 128 : (dc % 2) * NPAIR * 128 + (kc + 1) * 128],
                            h4_buf[0:sz, kc * 512 : (kc + 1) * 512],
                            start=(kc == 0), stop=(kc == NPAIR - 1),
                        )
                    mm.then_inc(s_xm, 1)
        block.tensor(t)

        def s(s):
            def sq_op(T):
                s.wait_ge(s_h2c, T + 1)
                if T >= 1:
                    s.wait_ge(s_stm, 2 * T)
                s.activation(sq2[:], h2bf[:], AF.Square).then_inc(s_sq2, 1)

            def rstd_op(T):
                s.wait_ge(s_msv, 3 * (T + 1))
                if T >= 1:
                    s.wait_ge(s_z, 8 * T)
                s.activation(R1sb[:], ps_ms[:], AF.Ln, scale=-1.0,
                             bias=kb.epses[:, 0:1]).then_inc(s_rld, 1)
                s.wait_ge(s_rld, 2 * T + 1)
                s.activation(R1sb[:], R1sb[:], AF.Exp,
                             scale=-0.5).then_inc(s_rld, 1)

            sq_op(0)
            rstd_op(0)
            for tr in range(NTR):
                for kc in range(NPAIR):
                    sz = PAIR_SZ[kc]
                    gk = tr * NPAIR + kc
                    if kc == 10 and tr < NTR - 1:
                        sq_op(tr + 1)
                    s.wait_ge(s_h3m, gk + 1)
                    if gk >= 2:
                        s.wait_ge(s_h4, gk - 1)
                    pg = ps_g if kc % 2 == 0 else ps_g2
                    s.activation(gel[0:sz, (kc % 2) * 512 : (kc % 2) * 512 + 512],
                                 pg[0:sz, :], AF.Gelu,
                                 bias=b1c[0:sz, 2 * kc + 1 : 2 * kc + 2]).then_inc(s_gel, 1)
                if tr < NTR - 1:
                    rstd_op(tr + 1)
        block.scalar(s)

        def v(v):
            def stats_vec(T):
                v.wait_ge(s_stm, 2 * (T + 1))
                if T >= 1:
                    v.wait_ge(s_r2, T)
                    v.wait_ge(s_z, NDC * T)
                ins = v.tensor_copy(musb[:], ps_mu[:])
                ins.then_inc(s_msv, 1)
                v.wait_ge(s_msv, 3 * T + 1)
                ins = v.tensor_tensor(ztmp[:], musb[:], musb[:], ALU.mult)
                ins.then_inc(s_msv, 1)
                v.wait_ge(s_msv, 3 * T + 2)
                ins = v.tensor_tensor(ps_ms[:], ps_ms[:], ztmp[:], ALU.add)
                ins.then_inc(s_msv, 1)

            def r2_op(T):
                v.wait_ge(s_rld, 2 * (T + 1))
                ins = v.tensor_tensor(R2sb[:], musb[:], R1sb[:], ALU.mult)
                ins.then_inc(s_r2, 1)

            def z_op(T):
                v.wait_ge(s_rld, 2 * (T + 1))
                v.wait_ge(s_r2, T + 1)
                if T >= 1:
                    v.wait_ge(s_h3m, NPAIR * T)
                for dc in range(NDC):
                    gz = T * NDC + dc
                    off = dc * STR_W + T * 512 + (1 if dc < 4 else 0)
                    if dc >= 4 and T >= 1:
                        v.scalar_tensor_tensor(
                            ztmp[:, 0:1], stash[:, dc - 4 : dc - 3],
                            lngc[:, dc : dc + 1], R1sb[:, 0:1], ALU.mult, ALU.mult)
                        i1 = v.scalar_tensor_tensor(
                            ztmp[:, 1:512], stream[:, off + 1 : off + 512],
                            lngc[:, dc : dc + 1], R1sb[:, 1:512], ALU.mult, ALU.mult)
                    else:
                        i1 = v.scalar_tensor_tensor(
                            ztmp[:], stream[:, off : off + 512], lngc[:, dc : dc + 1],
                            R1sb[:], ALU.mult, ALU.mult)
                    i2 = v.tensor_scalar(btmp[:], R2sb[:], lngc[:, dc : dc + 1],
                                         lnbc[:, dc : dc + 1], ALU.mult, ALU.add)
                    i2.then_inc(s_zt, 1)
                    v.wait_ge(s_zt, gz + 1)
                    ins = v.tensor_tensor(z_bf[:, dc * 512 : (dc + 1) * 512],
                                          ztmp[:], btmp[:], ALU.add)
                    ins.then_inc(s_z, 1)

            s_zt = kb.sem(tag + "ztm")
            stats_vec(0)
            r2_op(0)
            z_op(0)
            for tr in range(NTR):
                if tr < NTR - 1:
                    v.wait_ge(s_h2c, tr + 1)
                    for dc in range(4, 8):
                        ins = v.tensor_copy(
                            stash[:, dc - 4 : dc - 3],
                            stream[:, dc * STR_W + (tr + 1) * 512 : dc * STR_W + (tr + 1) * 512 + 1])
                    ins.then_inc(s_stb, 1)
                for kc in range(NPAIR):
                    sz = PAIR_SZ[kc]
                    gk = tr * NPAIR + kc
                    v.wait_ge(s_gel, gk + 1)
                    if tr >= 1 and kc == 0:
                        v.wait_ge(s_xm, NDC * tr)
                    pa = ps_a if kc % 2 == 0 else ps_a2
                    ins = v.scalar_tensor_tensor(
                        h4_buf[0:sz, kc * 512 : (kc + 1) * 512],
                        pa[0:sz, :], b1c[0:sz, 2 * kc : 2 * kc + 1],
                        gel[0:sz, (kc % 2) * 512 : (kc % 2) * 512 + 512],
                        ALU.add, ALU.mult)
                    ins.then_inc(s_h4, 1)
                if tr < NTR - 1:
                    stats_vec(tr + 1)
                    r2_op(tr + 1)
                    z_op(tr + 1)
                for dc in range(NDC):
                    gd = tr * NDC + dc
                    v.wait_ge(s_xm, gd + 1)
                    px = ps_bc1 if dc % 2 == 0 else ps_bc2
                    sl = stream[:, dc * STR_W + 1 + tr * 512 : dc * STR_W + 1 + (tr + 1) * 512]
                    ins = v.scalar_tensor_tensor(sl, px[:], b2c[:, dc : dc + 1], sl,
                                                 ALU.add, ALU.add)
                    ins.then_inc(s_xadd, 1)
        block.vector(v)

        return {
            "sync": [(s_ini, 64), (s_w2, 32 * NDC * NTR)],
            "gpsimd": [(s_h2c, NTR), (s_w1, 16 * NPAIR * NTR), (s_in2, 16)],
            "tensor": [(s_xm, NDC * NTR), (s_stm, 2 * NTR)],
            "scalar": [(s_gel, NPAIR * NTR), (s_sq2, NTR), (s_rld, 2 * NTR)],
            "vector": [(s_xadd, NDC * NTR), (s_h4, NPAIR * NTR), (s_z, NDC * NTR),
                       (s_msv, 3 * NTR), (s_r2, NTR), (s_stb, NTR - 1)],
        }


def phase_FIN(kb, block, din, dr, stream, b):
    with ExitStack() as st:
        ptile = kb.sb(st, f"fin{b}p", [128, 2 * D], F32)
        otile = kb.sb(st, f"fin{b}o", [128, 2 * D], F32)
        idf = kb.sb(st, f"fin{b}i", [128, 128], F32)
        psF = [kb.ps(st, f"fin{b}ps{i}", [128, 512], F32) for i in range(4)]
        s_p = kb.sem(f"fin{b}pl")
        s_tr = kb.sem(f"fin{b}tr")
        s_ad = kb.sem(f"fin{b}ad")
        s_st = kb.sem(f"fin{b}st")
        s_idf = kb.sem(f"fin{b}id")

        def sy(sy):
            sy.dma_start(out=idf[:], in_=din["ident"]).then_inc(s_idf, 16)
            for tc in range(NT):
                if tc >= 2:
                    sy.wait_ge(s_ad, 2 * (tc - 1))
                sy.dma_start(
                    out=ptile[:, (tc % 2) * D : (tc % 2) * D + D],
                    in_=dr["pbuf"][b, tc * 128 : (tc + 1) * 128, :],
                ).then_inc(s_p, 16)
                sy.wait_ge(s_ad, 2 * (tc + 1))
                sy.dma_start(
                    out=dr["out"][b, tc * 128 : (tc + 1) * 128, :],
                    in_=otile[:, (tc % 2) * D : (tc % 2) * D + D],
                ).then_inc(s_st, 16)
            sy.wait_ge(s_st, 16 * NT)
        block.sync(sy)

        def t(tn):
            tn.wait_ge(s_idf, 16)
            for tc in range(NT):
                for half in range(2):
                    gi = 2 * tc + half
                    if gi >= 4:
                        tn.wait_ge(s_ad, gi - 3)
                    pt = psF[gi % 4]
                    for j in range(4):
                        dc = 4 * half + j
                        tr = tn.transpose(
                            pt[:, j * 128 : (j + 1) * 128],
                            stream[:, dc * STR_W + 1 + tc * 128 : dc * STR_W + 1 + (tc + 1) * 128],
                            idf[:],
                        )
                    tr.then_inc(s_tr, 1)
        block.tensor(t)

        def v(v):
            for tc in range(NT):
                v.wait_ge(s_p, 16 * (tc + 1))
                if tc >= 2:
                    v.wait_ge(s_st, 16 * (tc - 1))
                for half in range(2):
                    gi = 2 * tc + half
                    v.wait_ge(s_tr, gi + 1)
                    v.tensor_tensor(
                        otile[:, (tc % 2) * D + half * 512 : (tc % 2) * D + half * 512 + 512],
                        psF[gi % 4][:],
                        ptile[:, (tc % 2) * D + half * 512 : (tc % 2) * D + half * 512 + 512],
                        ALU.add,
                    ).then_inc(s_ad, 1)
        block.vector(v)

        return {
            "sync": [(s_st, 16 * NT), (s_p, 16 * NT), (s_idf, 16)],
            "tensor": [(s_tr, 2 * NT)],
            "vector": [(s_ad, 2 * NT)],
        }



def kernel(**inputs):
    hp = host_prep(inputs)
    x = np.ascontiguousarray(np.asarray(inputs["L0_attn_in_features"], np.float32))
    kb, dr = build(dev=False)
    in_maps = []
    for c in range(NCORES):
        m = {"xsh": np.ascontiguousarray(x[NB * c : NB * (c + 1)])}
        m.update(hp)
        in_maps.append(m)
    res = run_bass_kernel_spmd(kb.nc, in_maps, core_ids=list(range(NCORES)))
    return np.concatenate([r["out"] for r in res.results], axis=0)
